# version 05
# baseline (speedup 1.0000x reference)
import numpy as np
import ml_dtypes
from contextlib import ExitStack
import concourse.bass as bass
import concourse.mybir as mybir
from concourse.bass_utils import run_bass_kernel_spmd

F32 = mybir.dt.float32
BF16 = mybir.dt.bfloat16
I32 = mybir.dt.int32
AF = mybir.ActivationFunctionType
ALU = mybir.AluOpType
AX = mybir.AxisListType

T = 2048
D = 1024
ENGINES = ("tensor", "vector", "scalar", "gpsimd", "sync")
SEMCLASSES = ("t", "v", "s", "g", "gd", "syd", "sd")
GELU_L = {0: AF.Gelu_apprx_tanh, 1: AF.Gelu}


class Buf:
    __slots__ = ("name", "writer", "readers", "pending")
    def __init__(self, name, pending=()):
        self.name = name
        self.writer = None
        self.readers = []
        self.pending = list(pending)


class Op:
    __slots__ = ("engine", "emit", "idx", "semclass", "inc_amt", "need_inc",
                 "deps", "waits", "tag", "post")
    def __init__(self, engine, emit, semclass, inc_amt, tag):
        self.engine = engine
        self.emit = emit
        self.semclass = semclass
        self.inc_amt = inc_amt
        self.need_inc = False
        self.deps = set()
        self.waits = []
        self.tag = tag
        self.post = 0


class Plan:
    def __init__(self):
        self.ops = []
        self.eng_ops = {e: [] for e in ENGINES}

    def op(self, engine, emit, reads=(), writes=(), deps=(), dma=False, tag=""):
        if dma:
            semclass = {"gpsimd": "gd", "sync": "syd", "scalar": "sd"}[engine]
            amt = 16
        else:
            semclass = {"tensor": "t", "vector": "v", "scalar": "s", "gpsimd": "g"}[engine]
            amt = 1
        o = Op(engine, emit, semclass, amt, tag)
        o.idx = len(self.ops)
        dd = set(deps)
        for b in reads:
            if b.writer is not None:
                dd.add(b.writer)
            dd.update(b.pending)
            b.pending = []
        for b in writes:
            if b.writer is not None:
                dd.add(b.writer)
            dd.update(b.readers)
            dd.update(b.pending)
            b.pending = []
        for d in dd:
            if d is None or d is o:
                continue
            if d.engine == engine and engine == "tensor" and not dma and d.semclass == "t":
                continue
            o.deps.add(d)
            d.need_inc = True
        for b in reads:
            b.readers.append(o)
        for b in writes:
            b.writer = o
            b.readers = []
        self.ops.append(o)
        self.eng_ops[engine].append(o)
        return o

    def finalize(self, nc, ctx):
        sems = {c: ctx.enter_context(nc.semaphore(f"sem_{c}")) for c in SEMCLASSES}
        counts = {c: 0 for c in SEMCLASSES}
        for o in self.ops:
            if o.need_inc:
                counts[o.semclass] += o.inc_amt
            o.post = counts[o.semclass]
        hw = {e: {c: 0 for c in SEMCLASSES} for e in ENGINES}
        for o in self.ops:
            req = {}
            for d in o.deps:
                assert d.need_inc and d.post > 0, (d.tag, o.tag)
                c = d.semclass
                if d.post > req.get(c, 0):
                    req[c] = d.post
            for c, v in sorted(req.items()):
                if v > hw[o.engine][c]:
                    hw[o.engine][c] = v
                    o.waits.append((c, v))
        self.sems = sems
        return sems

    def run_engine(self, eng_obj, ename):
        sems = self.sems
        for o in self.eng_ops[ename]:
            for (c, v) in o.waits:
                eng_obj.wait_ge(sems[c], v)
            if o.emit is None:
                assert not o.need_inc, f"marker op with dependents on {ename}: {o.tag}"
                continue
            try:
                inst = o.emit(eng_obj)
            except Exception:
                print(f"EMIT FAILED: engine={ename} tag={o.tag}")
                raise
            if o.need_inc:
                inst.then_inc(sems[o.semclass], o.inc_amt)


class Region:
    def __init__(self, arena, col0, ncols, name, grain, pending):
        self.arena = arena
        self.col0 = col0
        self.ncols = ncols
        self.name = name
        self.grain = grain
        self.alive = True
        n = (ncols + grain - 1) // grain
        self.bufs = [Buf(f"{name}[{i}]", pending) for i in range(n)]

    def free(self):
        self.alive = False

    def ap(self, c0=0, ncols=None, r0=0, nrows=128):
        ncols = self.ncols - c0 if ncols is None else ncols
        return self.arena.h[r0:r0 + nrows, self.col0 + c0:self.col0 + c0 + ncols]

    def bufs_for(self, c0=0, ncols=None):
        ncols = self.ncols - c0 if ncols is None else ncols
        i0 = c0 // self.grain
        i1 = (c0 + ncols - 1) // self.grain
        return self.bufs[i0:i1 + 1]


class Arena:
    def __init__(self, handle, ncols, name):
        self.h = handle
        self.ncols = ncols
        self.name = name
        self.owners = []
        self.peak = 0

    def alloc(self, ncols, name, grain=None):
        taken = sorted((c0, c1) for (c0, c1, r) in self.owners if r.alive)
        pos = 0
        for (c0, c1) in taken:
            if c0 - pos >= ncols:
                break
            pos = max(pos, c1)
        assert pos + ncols <= self.ncols, f"{self.name} arena full: {ncols} for {name} at {pos}"
        pend = []
        for (c0, c1, r) in self.owners:
            if not r.alive and not (c1 <= pos or c0 >= pos + ncols):
                for b in r.bufs:
                    if b.writer is not None:
                        pend.append(b.writer)
                    pend.extend(b.readers)
        reg = Region(self, pos, ncols, name, grain or ncols, pend)
        self.owners.append((pos, pos + ncols, reg))
        self.peak = max(self.peak, max(c1 for (c0, c1, r) in self.owners if r.alive))
        return reg


class DBufs:
    def __init__(self, name, n):
        self.bufs = [Buf(f"{name}.{i}") for i in range(n)]
    def __getitem__(self, i):
        return self.bufs[i]


WEIGHT_SPECS = {
    "L0_attn_Wc_down": (1024, 512), "L0_attn_Wq_up": (512, 1024), "L0_attn_Wkv_up": (512, 128),
    "L0_attn_Wc_proj": (1024, 1024),
    "L0_ffn_gate_w": (1024, 8), "L0_ffn_gate_b": (8,),
    "L0_ffn_sh_w1": (1024, 4096), "L0_ffn_sh_b1": (4096,),
    "L0_ffn_sh_w2": (4096, 1024), "L0_ffn_sh_b2": (1024,),
    "L0_ffn_ew1": (8, 1024, 4096), "L0_ffn_eb1": (8, 4096),
    "L0_ffn_ew2": (8, 4096, 1024), "L0_ffn_eb2": (8, 1024),
    "L1_attn_q_w": (16, 64, 1024), "L1_attn_q_b": (16, 1, 64),
    "L1_attn_k_w": (16, 64, 1024), "L1_attn_k_b": (16, 1, 64),
    "L1_attn_v_w": (16, 64, 1024), "L1_attn_v_b": (16, 1, 64),
    "L1_attn_l_w": (1024, 1024), "L1_attn_l_b": (1, 1024),
    "L1_ffn_gate_w": (1024, 8), "L1_ffn_gate_b": (8,),
    "L1_ffn_w1": (8, 1024, 4096), "L1_ffn_b1": (8, 4096),
    "L1_ffn_w2": (8, 4096, 1024), "L1_ffn_b2": (8, 1024),
    "L1_ffn_ws1": (1024, 4096), "L1_ffn_bs1": (4096,),
    "L1_ffn_ws2": (4096, 1024), "L1_ffn_bs2": (1024,),
}


class MKB:
    def __init__(self, nc, ctx, blocks, dumps):
        self.nc = nc
        self.ctx = ctx
        self.p = Plan()
        self.blocks = blocks
        self.dumps = set(dumps)

        self.x_in = nc.dram_tensor("x_in", [T, D], F32, kind="ExternalInput").ap()
        self.out = nc.dram_tensor("x_out", [T, D], F32, kind="ExternalOutput").ap()
        self.wap = {}

        self.XT = nc.dram_tensor("XT", [D, T], F32, kind="Internal").ap()
        self.XT_b = DBufs("XT", 8)
        self.c1 = [nc.dram_tensor(f"c1_{L}", [9 * 1024, 4096], BF16, kind="Internal").ap() for L in range(2)]
        self.c2 = [nc.dram_tensor(f"c2_{L}", [9 * 4096, 1024], BF16, kind="Internal").ap() for L in range(2)]
        self.c1_b = [DBufs(f"c1_{L}", 9) for L in range(2)]
        self.c2_b = [DBufs(f"c2_{L}", 9) for L in range(2)]

        sb = lambda name, shape, dt: ctx.enter_context(nc.sbuf_tensor(name, shape, dt))
        self.identf = sb("identf", [128, 128], F32)
        self.onesb = sb("onesb", [1, 512], BF16)
        self.onesf = sb("onesf", [65, 512], F32)
        self.onescol_b = sb("onescol_b", [128, 1], BF16)
        self.onescol_f = sb("onescol_f", [128, 1], F32)
        self.fidxf = sb("fidxf", [128, 512], F32)
        self.pidxf = sb("pidxf", [128, 1], F32)
        self.eps = sb("eps", [128, 1], F32)
        self.rows = sb("rows", [65, 2048], F32)
        self.srow = sb("srow", [1, 2048], F32)
        self.gx = sb("gx", [65, 2048], F32)
        self.mbc = sb("mbc", [128, 2048], BF16)
        self.rbc = sb("rbc", [128, 2048], BF16)
        self.b1cols = sb("b1cols", [128, 576], F32)
        self.qkvb = sb("qkvb", [128, 24], F32)
        self.lbcols = sb("lbcols", [128, 8], F32)
        self.gl_tok = sb("gl_tok", [128, 128], F32)
        self.gscr = sb("gscr", [128, 128], F32)
        self.gscr2 = sb("gscr2", [128, 128], F32)
        self.gscr3 = sb("gscr3", [128, 128], F32)
        self.gcoef = sb("gcoef", [128, 128], F32)
        self.gst = sb("gst", [128, 64], F32)
        self.gw_sb = sb("gw_sb", [128, 64], F32)
        self.gb_sb = sb("gb_sb", [1, 8], F32)
        self.gbc = sb("gbc", [40, 1], F32)
        self.s_col = sb("s_col", [8, 1], F32)
        self.Mcol = sb("Mcol", [128, 16], F32)
        self.kstat = sb("kstat", [128, 8], F32)
        self.bias_ld = sb("bias_ld", [128, 128], F32)

        names = ["identf", "onesb", "onesf", "onescol_b", "onescol_f", "masks1", "masks2", "vb_bc",
                 "fidxf", "pidxf", "eps", "rows", "srow", "gx", "mbc", "rbc", "b1cols",
                 "qkvb", "lbcols", "gl_tok", "gscr", "gscr2", "gscr3", "gcoef", "gst",
                 "gw_sb", "gb_sb", "gbc", "s_col", "Mcol", "kstat", "bias_ld"]
        self.B = {k: Buf(k) for k in names}

        self.PA = ctx.enter_context(nc.psum_tensor("PA", [128, 2048], F32))
        self.PB = ctx.enter_context(nc.psum_tensor("PB", [128, 2048], F32))
        self.PAb = [Buf(f"PA{b}") for b in range(4)]
        self.PBb = [Buf(f"PB{b}") for b in range(4)]

        self.AB = Arena(ctx.enter_context(nc.sbuf_tensor("AB", [128, 73000], BF16)), 73000, "AB")
        self.AF = Arena(ctx.enter_context(nc.sbuf_tensor("AFA", [128, 5500], F32)), 5500, "AF")
        self.xnT = None

    def t(self, emit, reads=(), writes=(), deps=(), tag=""):
        return self.p.op("tensor", emit, reads, writes, deps, tag=tag)
    def v(self, emit, reads=(), writes=(), deps=(), tag=""):
        return self.p.op("vector", emit, reads, writes, deps, tag=tag)
    def s(self, emit, reads=(), writes=(), deps=(), tag=""):
        return self.p.op("scalar", emit, reads, writes, deps, tag=tag)
    def g(self, emit, reads=(), writes=(), deps=(), tag=""):
        return self.p.op("gpsimd", emit, reads, writes, deps, tag=tag)
    def dma(self, queue, emit, reads=(), writes=(), deps=(), tag=""):
        return self.p.op(queue, emit, reads, writes, deps, dma=True, tag=tag)
    def pa(self, bank, r0=0, nrows=128, c0=0, ncols=512):
        return self.PA[r0:r0 + nrows, bank * 512 + c0:bank * 512 + c0 + ncols]
    def pb(self, bank, r0=0, nrows=128, c0=0, ncols=512):
        return self.PB[r0:r0 + nrows, bank * 512 + c0:bank * 512 + c0 + ncols]

    def prolog_consts(self):
        B = self.B
        self.g(lambda e: e.iota(self.fidxf[:], pattern=[[1, 512]], base=0, channel_multiplier=0,
                                allow_small_or_imprecise_dtypes=True),
               writes=[B["fidxf"]], tag="iota_f")
        self.g(lambda e: e.iota(self.pidxf[:], pattern=[[0, 1]], base=0, channel_multiplier=1,
                                allow_small_or_imprecise_dtypes=True),
               writes=[B["pidxf"]], tag="iota_p")
        self.v(lambda e: e.tensor_single_scalar(out=self.identf[:], in_=self.fidxf[:, 0:128],
                                                scalar=self.pidxf[:, 0:1], op=ALU.is_equal),
               reads=[B["fidxf"], B["pidxf"]], writes=[B["identf"]], tag="ident")
        for nm, val in [("onesb", 1.0), ("onesf", 1.0), ("onescol_b", 1.0), ("onescol_f", 1.0),
                        ("eps", 1e-5)]:
            t_ = getattr(self, nm)
            self.v(lambda e, t_=t_, val=val: e.memset(t_[:], val), writes=[B[nm]], tag=f"ms_{nm}")
        if "l0a" in self.blocks:
            self.masksR = self.AB.alloc(2048, "masks", grain=2048)
            self.masks2 = lambda o: self.masksR.ap(o * 512, 512)
            mscr = self.AF.alloc(512, "mscr")
            bm = mscr.bufs[0]
            for o in range(4):
                self.v(lambda e, o=o: e.tensor_scalar(out=mscr.ap(0, 512), in0=self.fidxf[:],
                                                      scalar1=float(o * 128), scalar2=None, op0=ALU.subtract),
                       reads=[B["fidxf"], bm], writes=[bm], tag=f"mk{o}a")
                self.v(lambda e, o=o: e.tensor_scalar(
                        out=self.masks2(o), in0=mscr.ap(0, 512),
                        scalar1=self.pidxf[:, 0:1], scalar2=-1e30, op0=ALU.is_lt, op1=ALU.mult),
                       reads=[bm, B["pidxf"]], writes=[B["masks2"]], tag=f"mk{o}b")
            mscr.free()

    def prolog_input_transpose(self):
        B = self.B
        xti = self.AF.alloc(2 * 1024, "xti", grain=1024)
        xtt = self.AF.alloc(2 * 1024, "xtt", grain=1024)
        for ti in range(16):
            i = ti % 2
            dst = xti.ap(i * 1024, 1024)
            self.dma("scalar", lambda e, d=dst, ti=ti: e.dma_start(out=d, in_=self.x_in[ti * 128:(ti + 1) * 128, :]),
                     reads=xti.bufs_for(i * 1024, 1024), writes=xti.bufs_for(i * 1024, 1024), tag=f"xin{ti}")
            for dc in range(8):
                ps = self.pb(dc % 2, c0=0, ncols=128)
                self.t(lambda e, ps=ps, s=xti.ap(i * 1024 + dc * 128, 128): e.transpose(ps, s, self.identf[:]),
                       reads=xti.bufs_for(i * 1024, 1024) + [B["identf"]], writes=[self.PBb[dc % 2]],
                       tag=f"xtr{ti}.{dc}")
                self.v(lambda e, ps=ps, d=xtt.ap(i * 1024 + dc * 128, 128): e.tensor_copy(out=d, in_=ps),
                       reads=[self.PBb[dc % 2]], writes=xtt.bufs_for(i * 1024, 1024), tag=f"xev{ti}.{dc}")
            dst2 = self.XT[:, ti * 128:(ti + 1) * 128].rearrange("(c p) f -> p c f", p=128)
            src2 = xtt.ap(i * 1024, 1024).rearrange("p (c f) -> p c f", c=8)
            self.dma("scalar", lambda e, d=dst2, s=src2: e.dma_start(out=d, in_=s),
                     reads=xtt.bufs_for(i * 1024, 1024), writes=[self.XT_b[c] for c in range(8)],
                     tag=f"xst{ti}")
        xti.free(); xtt.free()

    def prolog_casts_ffn(self, L):
        names1 = {0: ("L0_ffn_ew1", "L0_ffn_sh_w1"), 1: ("L1_ffn_w1", "L1_ffn_ws1")}[L]
        names2 = {0: ("L0_ffn_ew2", "L0_ffn_sh_w2"), 1: ("L1_ffn_w2", "L1_ffn_ws2")}[L]
        for e_ in [8] + list(range(8)):
            if e_ == 8:
                s1, s2 = self.wap[names1[1]], self.wap[names2[1]]
            else:
                s1 = self.wap[names1[0]].rearrange("e d h -> (e d) h")[e_ * 1024:(e_ + 1) * 1024, :]
                s2 = self.wap[names2[0]].rearrange("e h d -> (e h) d")[e_ * 4096:(e_ + 1) * 4096, :]
            d1 = self.c1[L][e_ * 1024:(e_ + 1) * 1024, :]
            d2 = self.c2[L][e_ * 4096:(e_ + 1) * 4096, :]
            self.dma("gpsimd", lambda e, d=d1, s=s1: e.dma_start(out=d, in_=s),
                     writes=[self.c1_b[L][e_]], tag=f"cast1_{L}.{e_}")
            self.dma("gpsimd", lambda e, d=d2, s=s2: e.dma_start(out=d, in_=s),
                     writes=[self.c2_b[L][e_]], tag=f"cast2_{L}.{e_}")

    def prolog_biases(self):
        B = self.B
        ld = self.bias_ld
        jobs = []
        for L in range(2):
            n1 = {0: "L0_ffn_eb1", 1: "L1_ffn_b1"}[L]
            ns1 = {0: "L0_ffn_sh_b1", 1: "L1_ffn_bs1"}[L]
            jobs.append((self.wap[n1].rearrange("e h -> (e h)"), 32768, self.b1cols, L * 288, "b1cols"))
            jobs.append((self.wap[ns1], 4096, self.b1cols, L * 288 + 256, "b1cols"))
        jobs.append((self.wap["L1_attn_l_b"].rearrange("o p -> (o p)"), 1024, self.lbcols, 0, "lbcols"))
        for hn, i in [("L1_attn_q_b", 0), ("L1_attn_k_b", 1), ("L1_attn_v_b", 2)]:
            jobs.append((self.wap[hn].rearrange("h o p -> (h o p)"), 1024, self.qkvb, i * 8, "qkvb"))
        for (src, n, dstt, col0, dname) in jobs:
            rows = n // 128
            for blk in range((rows + 127) // 128):
                r0 = blk * 128
                nr = min(rows, r0 + 128) - r0
                sap = src[r0 * 128:(r0 + nr) * 128].rearrange("(r c) -> r c", c=128)
                dst = ld[0:nr, 0:128]
                self.dma("scalar", lambda e, d=dst, s=sap: e.dma_start(out=d, in_=s),
                         reads=[B["bias_ld"]], writes=[B["bias_ld"]], tag="biasld")
                ps = self.pa(0, nrows=128, ncols=nr)
                self.t(lambda e, d=dst, ps=ps, nr=nr: e.transpose(ps, d, self.identf[0:nr, 0:nr]),
                       reads=[B["bias_ld"], B["identf"]], writes=[self.PAb[0]], tag="biastr")
                dap = dstt[0:128, col0 + blk * 128:col0 + blk * 128 + nr]
                ps2 = self.pa(0, nrows=128, ncols=nr)
                self.v(lambda e, dap=dap, ps2=ps2: e.tensor_copy(out=dap, in_=ps2),
                       reads=[self.PAb[0]], writes=[B[dname]], tag="biasev")

    def load_gates(self, L):
        gw = self.wap[{0: "L0_ffn_gate_w", 1: "L1_ffn_gate_w"}[L]]
        gb = self.wap[{0: "L0_ffn_gate_b", 1: "L1_ffn_gate_b"}[L]]
        B = self.B
        s3 = gw.rearrange("(c p) e -> p c e", p=128)
        d3 = self.gw_sb[:].rearrange("p (c e) -> p c e", e=8)
        self.dma("scalar", lambda e, d=d3, s=s3: e.dma_start(out=d, in_=s),
                 reads=[B["gw_sb"]], writes=[B["gw_sb"]], tag=f"gwld{L}")
        self.dma("scalar", lambda e, s=gb.rearrange("(o e) -> o e", o=1): e.dma_start(out=self.gb_sb[0:1, 0:8], in_=s),
                 reads=[B["gb_sb"]], writes=[B["gb_sb"]], tag=f"gbld{L}")

    def _ln_rows_and_bcast(self, width, tag, ffn_tail=False):
        B = self.B
        nb = width // 512
        R = self.rows
        S = self.srow
        G0 = self.gx[0:1, 0:width]
        if ffn_tail:
            sq_ap, sq_bufs = self.pa(1, nrows=1, ncols=width), [self.PAb[1]]
            bcm_ap, bcm_buf = self.pa(2), self.PAb[2]
            bcr_ap, bcr_buf = self.pa(3), self.PAb[3]
        else:
            sq_ap, sq_bufs = self.PB[0:1, 0:width], self.PBb[:nb]
            bcm_ap, bcm_buf = None, None
        self.v(lambda e: e.tensor_copy(out=R[0:1, 0:width], in_=self.PA[0:1, 0:width]),
               reads=self.PAb[:nb], writes=[B["rows"]], tag=f"{tag}.evm")
        self.v(lambda e, sq_ap=sq_ap: e.tensor_copy(out=S[0:1, 0:width], in_=sq_ap),
               reads=sq_bufs, writes=[B["srow"]], tag=f"{tag}.evs")
        self.v(lambda e: e.tensor_scalar(out=R[0:1, 0:width], in0=R[0:1, 0:width],
                                         scalar1=1.0 / D, scalar2=None, op0=ALU.mult),
               reads=[B["rows"]], writes=[B["rows"]], tag=f"{tag}.m")
        self.v(lambda e: e.tensor_scalar(out=S[0:1, 0:width], in0=S[0:1, 0:width],
                                         scalar1=1.0 / D, scalar2=None, op0=ALU.mult),
               reads=[B["srow"]], writes=[B["srow"]], tag=f"{tag}.s2")
        self.s(lambda e: e.activation(out=G0, in_=R[0:1, 0:width], func=AF.Square),
               reads=[B["rows"], B["gx"]], writes=[B["gx"]], tag=f"{tag}.mm")
        self.v(lambda e: e.tensor_sub(out=S[0:1, 0:width], in0=S[0:1, 0:width], in1=G0),
               reads=[B["srow"], B["gx"]], writes=[B["srow"]], tag=f"{tag}.var")
        self.s(lambda e: e.activation(out=G0, in_=S[0:1, 0:width],
                                      func=AF.Ln, bias=self.eps[0:1, 0:1]),
               reads=[B["srow"], B["eps"], B["gx"]], writes=[B["gx"]], tag=f"{tag}.ln")
        self.s(lambda e: e.activation(out=R[64:65, 0:width], in_=G0,
                                      func=AF.Exp, scale=-0.5),
               reads=[B["gx"], B["rows"]], writes=[B["rows"]], tag=f"{tag}.exp")
        if ffn_tail:
            self.t(lambda e: e.matmul(bcm_ap, self.onesf[0:1, 0:128],
                                      R[0:1, 0:512], start=True, stop=True),
                   reads=[B["onesf"], B["rows"]], writes=[bcm_buf], tag=f"{tag}.bcm")
            self.t(lambda e: e.matmul(bcr_ap, self.onesf[64:65, 0:128],
                                      R[64:65, 0:512], start=True, stop=True),
                   reads=[B["onesf"], B["rows"]], writes=[bcr_buf], tag=f"{tag}.bcr")
            self.v(lambda e: e.tensor_copy(out=self.mbc[:, 0:width], in_=bcm_ap),
                   reads=[bcm_buf], writes=[B["mbc"]], tag=f"{tag}.evbm")
            self.v(lambda e: e.tensor_copy(out=self.rbc[:, 0:width], in_=bcr_ap),
                   reads=[bcr_buf], writes=[B["rbc"]], tag=f"{tag}.evbr")
            return
        for b in range(nb):
            self.t(lambda e, b=b: e.matmul(self.pa(b), self.onesf[0:1, 0:128],
                                           R[0:1, b * 512:(b + 1) * 512], start=True, stop=True),
                   reads=[B["onesf"], B["rows"]], writes=[self.PAb[b]], tag=f"{tag}.bcm{b}")
            self.t(lambda e, b=b: e.matmul(self.pb(b), self.onesf[64:65, 0:128],
                                           R[64:65, b * 512:(b + 1) * 512], start=True, stop=True),
                   reads=[B["onesf"], B["rows"]], writes=[self.PBb[b]], tag=f"{tag}.bcr{b}")
        self.v(lambda e: e.tensor_copy(out=self.mbc[:, 0:width], in_=self.PA[0:128, 0:width]),
               reads=self.PAb[:nb], writes=[B["mbc"]], tag=f"{tag}.evbm")
        self.v(lambda e: e.tensor_copy(out=self.rbc[:, 0:width], in_=self.PB[0:128, 0:width]),
               reads=self.PBb[:nb], writes=[B["rbc"]], tag=f"{tag}.evbr")

    def pre_ln(self, L, gate, tag):
        B = self.B
        xnT = self.AB.alloc(16384, f"xnT_{tag}", grain=2048)
        xf = self.AF.alloc(2 * 2048, f"xf_{tag}", grain=2048)
        xc = self.AB.alloc(2 * 2048, f"xc_{tag}", grain=2048)
        xq = self.AB.alloc(2 * 2048, f"xq_{tag}", grain=2048)
        if gate:
            self.load_gates(L)
        for dc in range(8):
            i = dc % 2
            xfa = xf.ap(i * 2048, 2048)
            self.dma("scalar", lambda e, d=xfa, dc=dc: e.dma_start(out=d, in_=self.XT[dc * 128:(dc + 1) * 128, :]),
                     reads=[self.XT_b[dc]] + xf.bufs_for(i * 2048, 2048),
                     writes=xf.bufs_for(i * 2048, 2048), tag=f"{tag}.ldA{dc}")
            xca = xc.ap(i * 2048, 2048)
            self.v(lambda e, s=xfa, d=xca: e.tensor_copy(out=d, in_=s),
                   reads=xf.bufs_for(i * 2048, 2048), writes=xc.bufs_for(i * 2048, 2048), tag=f"{tag}.cst{dc}")
            xqa = xq.ap(i * 2048, 2048)
            self.s(lambda e, s=xca, d=xqa: e.activation(out=d, in_=s, func=AF.Square),
                   reads=xc.bufs_for(i * 2048, 2048), writes=xq.bufs_for(i * 2048, 2048), tag=f"{tag}.sq{dc}")
            for b in range(4):
                self.t(lambda e, b=b, a=xc.ap(i * 2048 + b * 512, 512), dc=dc:
                       e.matmul(self.pa(b, nrows=1), self.onescol_b[:], a, start=(dc == 0), stop=(dc == 7)),
                       reads=xc.bufs_for(i * 2048, 2048) + [B["onescol_b"]], writes=[self.PAb[b]],
                       tag=f"{tag}.mv{dc}.{b}")
                self.t(lambda e, b=b, a=xq.ap(i * 2048 + b * 512, 512), dc=dc:
                       e.matmul(self.pb(b, nrows=1), self.onescol_b[:], a, start=(dc == 0), stop=(dc == 7)),
                       reads=xq.bufs_for(i * 2048, 2048) + [B["onescol_b"]], writes=[self.PBb[b]],
                       tag=f"{tag}.sv{dc}.{b}")
        self._ln_rows_and_bcast(2048, tag)
        for dc in range(8):
            i = dc % 2
            xfa = xf.ap(i * 2048, 2048)
            self.dma("scalar", lambda e, d=xfa, dc=dc: e.dma_start(out=d, in_=self.XT[dc * 128:(dc + 1) * 128, :]),
                     reads=[self.XT_b[dc]] + xf.bufs_for(i * 2048, 2048),
                     writes=xf.bufs_for(i * 2048, 2048), tag=f"{tag}.ldB{dc}")
            if gate:
                for b in range(4):
                    self.t(lambda e, b=b, a=xf.ap(i * 2048 + b * 512, 512), dc=dc:
                           e.matmul(self.pa(b, nrows=8), self.gw_sb[:, dc * 8:(dc + 1) * 8], a,
                                    start=(dc == 0), stop=(dc == 7)),
                           reads=xf.bufs_for(i * 2048, 2048) + [B["gw_sb"]], writes=[self.PAb[b]],
                           tag=f"{tag}.gr{dc}.{b}")
            self.v(lambda e, d=xfa: e.tensor_sub(out=d, in0=d, in1=self.mbc[:]),
                   reads=xf.bufs_for(i * 2048, 2048) + [B["mbc"]], writes=xf.bufs_for(i * 2048, 2048),
                   tag=f"{tag}.a1{dc}")
            xna = xnT.ap(dc * 2048, 2048)
            self.v(lambda e, s=xfa, d=xna: e.tensor_mul(out=d, in0=s, in1=self.rbc[:]),
                   reads=xf.bufs_for(i * 2048, 2048) + [B["rbc"]], writes=xnT.bufs_for(dc * 2048, 2048),
                   tag=f"{tag}.a2{dc}")
        if gate:
            self.t(lambda e: e.transpose(self.PB[0:8, 0:1], self.gb_sb[0:1, 0:8], self.identf[0:1, 0:1]),
                   reads=[B["gb_sb"], B["identf"]], writes=[self.PBb[0]], tag=f"{tag}.gbt")
            self.v(lambda e: e.tensor_copy(out=self.gbc[32:40, 0:1], in_=self.PB[0:8, 0:1]),
                   reads=[self.PBb[0]], writes=[B["gbc"]], tag=f"{tag}.gbc")
            for dc in range(8):
                self.t(lambda e, dc=dc: e.matmul(self.pb(0, nrows=8, ncols=1),
                                                 self.gw_sb[:, dc * 8:(dc + 1) * 8], self.onescol_f[:],
                                                 start=(dc == 0), stop=(dc == 7)),
                       reads=[B["gw_sb"], B["onescol_f"]], writes=[self.PBb[0]], tag=f"{tag}.sc{dc}")
            self._gating_finalize(L, tag)
        xf.free(); xc.free(); xq.free()
        self.xnT = xnT
        return xnT

    def _gating_finalize(self, L, tag):
        B = self.B
        GX = self.gx
        self.v(lambda e: e.tensor_copy(out=GX[0:8, :], in_=self.PA[0:8, 0:2048]),
               reads=self.PAb, writes=[B["gx"]], tag=f"{tag}.gev")
        self.v(lambda e: e.tensor_copy(out=self.s_col[:], in_=self.PB[0:8, 0:1]),
               reads=[self.PBb[0]], writes=[B["s_col"]], tag=f"{tag}.scv")
        for b in range(4):
            self.t(lambda e, b=b: e.matmul(self.pb(b, nrows=8), self.onesf[0:1, 0:8],
                                           self.rows[0:1, b * 512:(b + 1) * 512], start=True, stop=True),
                   reads=[B["onesf"], B["rows"]], writes=[self.PBb[b]], tag=f"{tag}.m8{b}")
        self.v(lambda e: e.scalar_tensor_tensor(out=GX[32:40, :], in0=self.PB[0:8, 0:2048],
                                                scalar=self.s_col[0:8, 0:1], in1=GX[0:8, :],
                                                op0=ALU.mult, op1=ALU.subtract),
               reads=self.PBb + [B["s_col"], B["gx"]], writes=[B["gx"]], tag=f"{tag}.gms")
        for b in range(4):
            self.t(lambda e, b=b: e.matmul(self.pb(b, nrows=8), self.onesf[64:65, 0:8],
                                           self.rows[64:65, b * 512:(b + 1) * 512], start=True, stop=True),
                   reads=[B["onesf"], B["rows"]], writes=[self.PBb[b]], tag=f"{tag}.r8{b}")
        self.v(lambda e: e.tensor_mul(out=GX[32:40, :], in0=GX[32:40, :], in1=self.PB[0:8, 0:2048]),
               reads=[B["gx"]] + self.PBb, writes=[B["gx"]], tag=f"{tag}.gfin")
        self.v(lambda e: e.tensor_scalar(out=GX[32:40, :], in0=GX[32:40, :],
                                         scalar1=self.gbc[32:40, 0:1], scalar2=None, op0=ALU.subtract),
               reads=[B["gx"], B["gbc"]], writes=[B["gx"]], tag=f"{tag}.gbias")
        for ti in range(16):
            self.t(lambda e, ti=ti: e.transpose(self.PA[0:128, ti * 8:ti * 8 + 8],
                                                GX[32:40, ti * 128:(ti + 1) * 128], self.identf[32:40, 32:40]),
                   reads=[B["gx"], B["identf"]], writes=[self.PAb[0]], tag=f"{tag}.gt{ti}")
        self.v(lambda e: e.tensor_copy(out=self.gl_tok[:], in_=self.PA[0:128, 0:128]),
               reads=[self.PAb[0]], writes=[B["gl_tok"]], tag=f"{tag}.gtv")
        self.v(lambda e: e.tensor_scalar(out=self.gl_tok[:], in0=self.gl_tok[:],
                                         scalar1=-1.0, scalar2=None, op0=ALU.mult),
               reads=[B["gl_tok"]], writes=[B["gl_tok"]], tag=f"{tag}.gneg")
        G3 = self.gl_tok[:].rearrange("p (t e) -> p t e", e=8)
        gst = self.gst
        if L == 0:
            self.v(lambda e: e.reduce_max(out=gst[:, 0:16], in_=G3, axis=AX.X),
                   reads=[B["gl_tok"]], writes=[B["gst"]], tag=f"{tag}.mx")
            for ti in range(16):
                self.v(lambda e, ti=ti: e.tensor_scalar(out=self.gscr[:, ti * 8:ti * 8 + 8],
                                                        in0=self.gl_tok[:, ti * 8:ti * 8 + 8],
                                                        scalar1=gst[:, ti:ti + 1], scalar2=None, op0=ALU.subtract),
                       reads=[B["gl_tok"], B["gst"]], writes=[B["gscr"]], tag=f"{tag}.sb{ti}")
            self.s(lambda e: e.activation(out=self.gscr[:], in_=self.gscr[:], func=AF.Exp),
                   reads=[B["gscr"]], writes=[B["gscr"]], tag=f"{tag}.ex")
            GS = self.gscr[:].rearrange("p (t e) -> p t e", e=8)
            self.v(lambda e: e.reduce_sum(out=gst[:, 16:32], in_=GS, axis=AX.X),
                   reads=[B["gscr"]], writes=[B["gst"]], tag=f"{tag}.sm")
            self.v(lambda e: e.reciprocal(out=gst[:, 16:32], in_=gst[:, 16:32]),
                   reads=[B["gst"]], writes=[B["gst"]], tag=f"{tag}.rc")
            for ti in range(16):
                self.v(lambda e, ti=ti: e.tensor_scalar(out=self.gscr[:, ti * 8:ti * 8 + 8],
                                                        in0=self.gscr[:, ti * 8:ti * 8 + 8],
                                                        scalar1=gst[:, 16 + ti:17 + ti], scalar2=None, op0=ALU.mult),
                       reads=[B["gscr"], B["gst"]], writes=[B["gscr"]], tag=f"{tag}.pr{ti}")
            self.v(lambda e: e.reduce_max(out=gst[:, 0:16], in_=GS, axis=AX.X),
                   reads=[B["gscr"]], writes=[B["gst"]], tag=f"{tag}.v1")
            for ti in range(16):
                self.v(lambda e, ti=ti: e.tensor_scalar(out=self.gscr2[:, ti * 8:ti * 8 + 8],
                                                        in0=self.gscr[:, ti * 8:ti * 8 + 8],
                                                        scalar1=gst[:, ti:ti + 1], scalar2=None, op0=ALU.is_equal),
                       reads=[B["gscr"], B["gst"]], writes=[B["gscr2"]], tag=f"{tag}.e1{ti}")
            self.v(lambda e: e.scalar_tensor_tensor(out=self.gscr3[:], in0=self.gscr2[:], scalar=-1e30,
                                                    in1=self.gscr[:], op0=ALU.mult, op1=ALU.add),
                   reads=[B["gscr2"], B["gscr"]], writes=[B["gscr3"]], tag=f"{tag}.p2")
            GS3 = self.gscr3[:].rearrange("p (t e) -> p t e", e=8)
            self.v(lambda e: e.reduce_max(out=gst[:, 16:32], in_=GS3, axis=AX.X),
                   reads=[B["gscr3"]], writes=[B["gst"]], tag=f"{tag}.v2")
            for ti in range(16):
                self.v(lambda e, ti=ti: e.tensor_scalar(out=self.gscr3[:, ti * 8:ti * 8 + 8],
                                                        in0=self.gscr3[:, ti * 8:ti * 8 + 8],
                                                        scalar1=gst[:, 16 + ti:17 + ti], scalar2=None, op0=ALU.is_equal),
                       reads=[B["gscr3"], B["gst"]], writes=[B["gscr3"]], tag=f"{tag}.e2{ti}")
            self.v(lambda e: e.tensor_add(out=gst[:, 32:48], in0=gst[:, 0:16], in1=gst[:, 16:32]),
                   reads=[B["gst"]], writes=[B["gst"]], tag=f"{tag}.dn")
            self.v(lambda e: e.tensor_scalar(out=gst[:, 32:48], in0=gst[:, 32:48], scalar1=1e-9,
                                             scalar2=None, op0=ALU.add),
                   reads=[B["gst"]], writes=[B["gst"]], tag=f"{tag}.dne")
            self.v(lambda e: e.reciprocal(out=gst[:, 32:48], in_=gst[:, 32:48]),
                   reads=[B["gst"]], writes=[B["gst"]], tag=f"{tag}.dnr")
            self.v(lambda e: e.tensor_add(out=self.gscr2[:], in0=self.gscr2[:], in1=self.gscr3[:]),
                   reads=[B["gscr2"], B["gscr3"]], writes=[B["gscr2"]], tag=f"{tag}.mu")
            self.v(lambda e: e.tensor_mul(out=self.gscr[:], in0=self.gscr[:], in1=self.gscr2[:]),
                   reads=[B["gscr"], B["gscr2"]], writes=[B["gscr"]], tag=f"{tag}.pw")
            for ti in range(16):
                self.v(lambda e, ti=ti: e.tensor_scalar(out=self.gcoef[:, ti * 8:ti * 8 + 8],
                                                        in0=self.gscr[:, ti * 8:ti * 8 + 8],
                                                        scalar1=gst[:, 32 + ti:33 + ti], scalar2=None, op0=ALU.mult),
                       reads=[B["gscr"], B["gst"]], writes=[B["gcoef"]], tag=f"{tag}.wE{ti}")
        else:
            self.v(lambda e: e.reduce_max(out=gst[:, 0:16], in_=G3, axis=AX.X),
                   reads=[B["gl_tok"]], writes=[B["gst"]], tag=f"{tag}.v1")
            for ti in range(16):
                self.v(lambda e, ti=ti: e.tensor_scalar(out=self.gscr[:, ti * 8:ti * 8 + 8],
                                                        in0=self.gl_tok[:, ti * 8:ti * 8 + 8],
                                                        scalar1=gst[:, ti:ti + 1], scalar2=None, op0=ALU.is_equal),
                       reads=[B["gl_tok"], B["gst"]], writes=[B["gscr"]], tag=f"{tag}.e1{ti}")
            self.v(lambda e: e.scalar_tensor_tensor(out=self.gscr2[:], in0=self.gscr[:], scalar=-1e30,
                                                    in1=self.gl_tok[:], op0=ALU.mult, op1=ALU.add),
                   reads=[B["gscr"], B["gl_tok"]], writes=[B["gscr2"]], tag=f"{tag}.p2")
            GS2 = self.gscr2[:].rearrange("p (t e) -> p t e", e=8)
            self.v(lambda e: e.reduce_max(out=gst[:, 16:32], in_=GS2, axis=AX.X),
                   reads=[B["gscr2"]], writes=[B["gst"]], tag=f"{tag}.v2")
            for ti in range(16):
                self.v(lambda e, ti=ti: e.tensor_scalar(out=self.gscr3[:, ti * 8:ti * 8 + 8],
                                                        in0=self.gscr2[:, ti * 8:ti * 8 + 8],
                                                        scalar1=gst[:, 16 + ti:17 + ti], scalar2=None, op0=ALU.is_equal),
                       reads=[B["gscr2"], B["gst"]], writes=[B["gscr3"]], tag=f"{tag}.e2{ti}")
            self.v(lambda e: e.tensor_sub(out=gst[:, 32:48], in0=gst[:, 0:16], in1=gst[:, 16:32]),
                   reads=[B["gst"]], writes=[B["gst"]], tag=f"{tag}.d12")
            self.s(lambda e: e.activation(out=gst[:, 48:64], in_=gst[:, 32:48], func=AF.Sigmoid),
                   reads=[B["gst"]], writes=[B["gst"]], tag=f"{tag}.sg")
            self.v(lambda e: e.tensor_scalar(out=gst[:, 32:48], in0=gst[:, 48:64], scalar1=-1.0,
                                             scalar2=1.0, op0=ALU.mult, op1=ALU.add),
                   reads=[B["gst"]], writes=[B["gst"]], tag=f"{tag}.g2")
            for ti in range(16):
                self.v(lambda e, ti=ti: e.tensor_scalar(out=self.gcoef[:, ti * 8:ti * 8 + 8],
                                                        in0=self.gscr[:, ti * 8:ti * 8 + 8],
                                                        scalar1=gst[:, 48 + ti:49 + ti], scalar2=None, op0=ALU.mult),
                       reads=[B["gscr"], B["gst"]], writes=[B["gcoef"]], tag=f"{tag}.c1{ti}")
                self.v(lambda e, ti=ti: e.scalar_tensor_tensor(out=self.gcoef[:, ti * 8:ti * 8 + 8],
                                                               in0=self.gscr3[:, ti * 8:ti * 8 + 8],
                                                               scalar=gst[:, 32 + ti:33 + ti],
                                                               in1=self.gcoef[:, ti * 8:ti * 8 + 8],
                                                               op0=ALU.mult, op1=ALU.add),
                       reads=[B["gscr3"], B["gst"], B["gcoef"]], writes=[B["gcoef"]], tag=f"{tag}.c2{ti}")

    def ffn(self, L, tag):
        B = self.B
        xnT = self.xnT
        h = self.AB.alloc(16384, f"h_{tag}", grain=512)
        w1q = self.AB.alloc(2 * 4096, f"w1q_{tag}", grain=4096)
        w2q = self.AB.alloc(2 * 4096, f"w2q_{tag}", grain=4096)
        fT = self.AB.alloc(4096, f"fT_{tag}", grain=512)
        fq = self.AB.alloc(2 * 512, f"fq_{tag}", grain=512)
        b2reg = self.AB.alloc(9216, f"b2_{tag}", grain=9216)
        acc = self.AF.alloc(4096, f"acc_{tag}", grain=512)
        tl1 = self.AF.alloc(2 * 512, f"tl1_{tag}", grain=512)
        xo2 = self.AB.alloc(2 * 1024, f"xo2_{tag}", grain=1024)
        b2e = self.wap[{0: "L0_ffn_eb2", 1: "L1_ffn_b2"}[L]].rearrange("e d -> (e d)").rearrange("(o n) -> o n", o=1)
        b2s = self.wap[{0: "L0_ffn_sh_b2", 1: "L1_ffn_bs2"}[L]].rearrange("(o n) -> o n", o=1)
        self.dma("gpsimd", lambda e: e.dma_start(out=b2reg.ap(0, 8192, nrows=1), in_=b2e),
                 reads=b2reg.bufs, writes=b2reg.bufs, tag=f"{tag}.b2e")
        self.dma("gpsimd", lambda e: e.dma_start(out=b2reg.ap(8192, 1024, nrows=1), in_=b2s),
                 reads=b2reg.bufs, writes=b2reg.bufs, tag=f"{tag}.b2s")

        experts = [8] + list(range(8))
        loadseq = []
        for sc in range(4):
            for e_ in experts:
                for q8 in range(8):
                    loadseq.append(("w1", e_, q8))
                for q8 in range(8):
                    loadseq.append(("w2", e_, q8))
        li = [0]

        def plan_load():
            if li[0] >= len(loadseq):
                return
            kind, e_, q8 = loadseq[li[0]]
            li[0] += 1
            par = q8 % 2
            if kind == "w1":
                dst = w1q.ap(par * 4096, 4096).rearrange("p (c n) -> p c n", c=8)
                src = self.c1[L][e_ * 1024:(e_ + 1) * 1024, q8 * 512:(q8 + 1) * 512] \
                    .rearrange("(c p) n -> p c n", p=128)
                bufs, cb = w1q.bufs_for(par * 4096, 4096), self.c1_b[L][e_]
            else:
                dst = w2q.ap(par * 4096, 4096).rearrange("p (c n) -> p c n", c=4)
                src = self.c2[L][e_ * 4096 + q8 * 512:e_ * 4096 + (q8 + 1) * 512, :] \
                    .rearrange("(c p) n -> p c n", p=128)
                bufs, cb = w2q.bufs_for(par * 4096, 4096), self.c2_b[L][e_]
            self.dma("sync", lambda e, d=dst, s=src: e.dma_start(out=d, in_=s),
                     reads=[cb] + bufs, writes=bufs, tag=f"{tag}.{kind}ld{e_}.{q8}")

        plan_load(); plan_load()
        for sc in range(4):
            for e_ in experts:
                for q8 in range(8):
                    par = q8 % 2
                    for j in range(4):
                        hc = q8 * 4 + j
                        ps, pbuf = self.pb(hc % 2), self.PBb[hc % 2]
                        for dc in range(8):
                            lh = w1q.ap(par * 4096 + dc * 512 + j * 128, 128)
                            rh = xnT.ap(dc * 2048 + sc * 512, 512)
                            self.t(lambda e, ps=ps, lh=lh, rh=rh, dc=dc:
                                   e.matmul(ps, lh, rh, start=(dc == 0), stop=(dc == 7)),
                                   reads=w1q.bufs_for(par * 4096, 4096) + xnT.bufs_for(dc * 2048, 2048),
                                   writes=[pbuf], tag=f"{tag}.up{sc}.{e_}.{hc}.{dc}")
                        bcol = L * 288 + (256 if e_ == 8 else e_ * 32) + hc
                        ha = h.ap(hc * 512, 512)
                        self.s(lambda e, ps=ps, ha=ha, bcol=bcol:
                               e.activation(out=ha, in_=ps, func=GELU_L[L],
                                            bias=self.b1cols[:, bcol:bcol + 1]),
                               reads=[pbuf, B["b1cols"]], writes=h.bufs_for(hc * 512, 512),
                               tag=f"{tag}.ge{sc}.{e_}.{hc}")
                    plan_load()
                banks = [(self.pa(k), self.PAb[k]) for k in range(4)] + \
                        [(self.pb(k), self.PBb[k]) for k in range(4)]
                for k in range(8):
                    tsub, dh = k // 2, k % 2
                    ps, pbuf = banks[k]
                    c0 = (8192 if e_ == 8 else e_ * 1024) + dh * 512
                    self.t(lambda e, ps=ps, brow=b2reg.ap(c0, 512, nrows=1):
                           e.matmul(ps, self.onesb[0:1, 0:128], brow, start=True, stop=False),
                           reads=[B["onesb"]] + b2reg.bufs, writes=[pbuf], tag=f"{tag}.au{sc}.{e_}.{k}")
                for q8 in range(8):
                    par = q8 % 2
                    for j in range(4):
                        hc = q8 * 4 + j
                        for k in range(8):
                            tsub, dh = k // 2, k % 2
                            ps, pbuf = banks[k]
                            lh = h.ap(hc * 512 + tsub * 128, 128)
                            rh = w2q.ap(par * 4096 + j * 1024 + dh * 512, 512)
                            self.t(lambda e, ps=ps, lh=lh, rh=rh, last=(q8 == 7 and j == 3):
                                   e.matmul(ps, lh, rh, start=False, stop=last),
                                   reads=h.bufs_for(hc * 512, 512) + w2q.bufs_for(par * 4096, 4096),
                                   writes=[pbuf], tag=f"{tag}.dn{sc}.{e_}.{hc}.{k}")
                    plan_load()
                for k in range(8):
                    tsub, dh = k // 2, k % 2
                    ps, pbuf = banks[k]
                    aap = acc.ap(tsub * 1024 + dh * 512, 512)
                    abufs = acc.bufs_for(tsub * 1024 + dh * 512, 512)
                    if e_ == 8:
                        self.v(lambda e, ps=ps, aap=aap: e.tensor_copy(out=aap, in_=ps),
                               reads=[pbuf], writes=abufs, tag=f"{tag}.e8{sc}.{k}")
                    else:
                        col = (sc * 4 + tsub) * 8 + e_
                        self.v(lambda e, ps=ps, aap=aap, col=col:
                               e.scalar_tensor_tensor(out=aap, in0=ps, scalar=self.gcoef[:, col:col + 1],
                                                      in1=aap, op0=ALU.mult, op1=ALU.add),
                               reads=[pbuf, B["gcoef"]] + abufs, writes=abufs,
                               tag=f"{tag}.ev{sc}.{e_}.{k}")
            ctr = 0
            for tsub in range(4):
                for dc in range(8):
                    slot = ctr % 8; ctr += 1
                    bank = 2 + slot // 4
                    ps = self.pa(bank, c0=(slot % 4) * 128, ncols=128)
                    pbuf = self.PAb[bank]
                    src = acc.ap(tsub * 1024 + dc * 128, 128)
                    self.t(lambda e, ps=ps, src=src: e.transpose(ps, src, self.identf[:]),
                           reads=acc.bufs_for(tsub * 1024 + dc * 128, 128) + [B["identf"]],
                           writes=[pbuf], tag=f"{tag}.tr{sc}.{tsub}.{dc}")
                    fap = fT.ap(dc * 512 + tsub * 128, 128)
                    fbufs = fT.bufs_for(dc * 512, 512)
                    if L == 0:
                        xs = xnT.ap(dc * 2048 + sc * 512 + tsub * 128, 128)
                        self.v(lambda e, ps=ps, fap=fap, xs=xs: e.tensor_add(out=fap, in0=ps, in1=xs),
                               reads=[pbuf] + xnT.bufs_for(dc * 2048, 2048), writes=fbufs,
                               tag=f"{tag}.fx{sc}.{tsub}.{dc}")
                    else:
                        self.v(lambda e, ps=ps, fap=fap: e.tensor_copy(out=fap, in_=ps),
                               reads=[pbuf], writes=fbufs, tag=f"{tag}.fx{sc}.{tsub}.{dc}")
            for dc in range(8):
                i = dc % 2
                fqa = fq.ap(i * 512, 512)
                self.s(lambda e, s=fT.ap(dc * 512, 512), d=fqa: e.activation(out=d, in_=s, func=AF.Square),
                       reads=fT.bufs_for(dc * 512, 512), writes=fq.bufs_for(i * 512, 512),
                       tag=f"{tag}.fsq{sc}.{dc}")
                self.t(lambda e, a=fT.ap(dc * 512, 512), dc=dc:
                       e.matmul(self.pa(0, nrows=1), self.onescol_b[:], a, start=(dc == 0), stop=(dc == 7)),
                       reads=fT.bufs_for(dc * 512, 512) + [B["onescol_b"]], writes=[self.PAb[0]],
                       tag=f"{tag}.tm{sc}.{dc}")
                self.t(lambda e, a=fqa, dc=dc:
                       e.matmul(self.pa(1, nrows=1), self.onescol_b[:], a, start=(dc == 0), stop=(dc == 7)),
                       reads=fq.bufs_for(i * 512, 512) + [B["onescol_b"]], writes=[self.PAb[1]],
                       tag=f"{tag}.ts{sc}.{dc}")
            self._ln_rows_and_bcast(512, f"{tag}.t{sc}", ffn_tail=True)
            for dc in range(8):
                i = dc % 2
                t1 = tl1.ap(i * 512, 512)
                xoa = xo2.ap(i * 1024, 1024).bitcast(F32)
                self.dma("scalar", lambda e, d=xoa, dc=dc, sc=sc:
                         e.dma_start(out=d, in_=self.XT[dc * 128:(dc + 1) * 128, sc * 512:(sc + 1) * 512]),
                         reads=[self.XT_b[dc]] + xo2.bufs_for(i * 1024, 1024),
                         writes=xo2.bufs_for(i * 1024, 1024), tag=f"{tag}.xl{sc}.{dc}")
                self.v(lambda e, s=fT.ap(dc * 512, 512), d=t1: e.tensor_sub(out=d, in0=s, in1=self.mbc[:, 0:512]),
                       reads=fT.bufs_for(dc * 512, 512) + [B["mbc"]] + tl1.bufs_for(i * 512, 512),
                       writes=tl1.bufs_for(i * 512, 512), tag=f"{tag}.l1{sc}.{dc}")
                self.v(lambda e, d=t1: e.tensor_mul(out=d, in0=d, in1=self.rbc[:, 0:512]),
                       reads=tl1.bufs_for(i * 512, 512) + [B["rbc"]], writes=tl1.bufs_for(i * 512, 512),
                       tag=f"{tag}.l2{sc}.{dc}")
                self.v(lambda e, d=t1, xoa=xoa: e.tensor_add(out=d, in0=d, in1=xoa),
                       reads=tl1.bufs_for(i * 512, 512) + xo2.bufs_for(i * 1024, 1024),
                       writes=tl1.bufs_for(i * 512, 512), tag=f"{tag}.l3{sc}.{dc}")
                dst = self.XT[dc * 128:(dc + 1) * 128, sc * 512:(sc + 1) * 512]
                self.dma("scalar", lambda e, d=dst, s=t1: e.dma_start(out=d, in_=s),
                         reads=tl1.bufs_for(i * 512, 512) + [self.XT_b[dc]], writes=[self.XT_b[dc]],
                         tag=f"{tag}.st{sc}.{dc}")
        for r in (h, w1q, w2q, fT, fq, b2reg, acc, tl1, xo2):
            r.free()
        xnT.free()
        self.xnT = None

    def load_l0_attn_weights(self):
        self.wcd = self.AB.alloc(4096, "wcd", grain=4096)
        self.wqu = self.AB.alloc(4096, "wqu", grain=4096)
        self.wkv = self.AB.alloc(512, "wkv", grain=512)
        self.wcp = self.AB.alloc(8192, "wcp", grain=8192)
        for (reg, nm, cdim) in [(self.wcd, "L0_attn_Wc_down", 512), (self.wqu, "L0_attn_Wq_up", 1024),
                                (self.wkv, "L0_attn_Wkv_up", 128), (self.wcp, "L0_attn_Wc_proj", 1024)]:
            nchunk = self.wap[nm].tensor.shape[0] // 128
            src = self.wap[nm].rearrange("(c p) n -> p c n", p=128)
            dst = reg.ap(0, nchunk * cdim).rearrange("p (c n) -> p c n", c=nchunk)
            self.dma("gpsimd", lambda e, d=dst, s=src: e.dma_start(out=d, in_=s),
                     writes=reg.bufs, tag=f"ld_{nm}")

    def l0_attn(self, tag="l0a"):
        B = self.B
        xnT = self.xnT
        lat = self.AB.alloc(8192, f"lat_{tag}", grain=2048)
        for lc in range(4):
            for tb in range(4):
                ps, pbuf = self.pa(tb), self.PAb[tb]
                for dc in range(8):
                    lh = self.wcd.ap(dc * 512 + lc * 128, 128)
                    rh = xnT.ap(dc * 2048 + tb * 512, 512)
                    self.t(lambda e, ps=ps, lh=lh, rh=rh, dc=dc:
                           e.matmul(ps, lh, rh, start=(dc == 0), stop=(dc == 7)),
                           reads=self.wcd.bufs + xnT.bufs_for(dc * 2048, 2048), writes=[pbuf],
                           tag=f"{tag}.lat{lc}.{tb}.{dc}")
                dst = lat.ap(lc * 2048 + tb * 512, 512)
                self.s(lambda e, ps=ps, d=dst: e.copy(out=d, in_=ps),
                       reads=[pbuf], writes=lat.bufs_for(lc * 2048, 2048), tag=f"{tag}.le{lc}.{tb}")
        xnT.free()
        self.xnT = None
        kTaug = self.AB.alloc(2048, f"kta_{tag}", grain=2048)
        vaug = self.AB.alloc(1040, f"vau_{tag}", grain=1040)
        qTf = self.AB.alloc(16384, f"qTf_{tag}", grain=2048)
        yscr = self.AF.alloc(2048, f"ys_{tag}", grain=2048)
        for tb in range(4):
            ps, pbuf = self.pa(tb, nrows=64), self.PAb[tb]
            for lc in range(4):
                lh = self.wkv.ap(lc * 128, 64)
                rh = lat.ap(lc * 2048 + tb * 512, 512)
                self.t(lambda e, ps=ps, lh=lh, rh=rh, lc=lc:
                       e.matmul(ps, lh, rh, start=(lc == 0), stop=(lc == 3)),
                       reads=self.wkv.bufs + lat.bufs_for(lc * 2048, 2048), writes=[pbuf],
                       tag=f"{tag}.kt{tb}.{lc}")
            self.s(lambda e, ps=ps, d=kTaug.ap(tb * 512, 512, r0=0, nrows=64): e.copy(out=d, in_=ps),
                   reads=[pbuf], writes=kTaug.bufs, tag=f"{tag}.kte{tb}")
        self.v(lambda e: e.memset(kTaug.ap(0, 2048, r0=64, nrows=1), -1.0),
               reads=kTaug.bufs, writes=kTaug.bufs, tag=f"{tag}.km1")
        for si in range(16):
            ps, pbuf = self.pa(si % 4, ncols=64), self.PAb[si % 4]
            for lc in range(4):
                lh = lat.ap(lc * 2048 + si * 128, 128)
                rh = self.wkv.ap(lc * 128 + 64, 64)
                self.t(lambda e, ps=ps, lh=lh, rh=rh, lc=lc:
                       e.matmul(ps, lh, rh, start=(lc == 0), stop=(lc == 3)),
                       reads=self.wkv.bufs + lat.bufs_for(lc * 2048, 2048), writes=[pbuf],
                       tag=f"{tag}.v{si}.{lc}")
            self.s(lambda e, ps=ps, d=vaug.ap(si * 65, 64): e.copy(out=d, in_=ps),
                   reads=[pbuf], writes=vaug.bufs, tag=f"{tag}.ve{si}")
            self.v(lambda e, si=si: e.memset(vaug.ap(si * 65 + 64, 1), 1.0),
                   reads=vaug.bufs, writes=vaug.bufs, tag=f"{tag}.vm{si}")
        for hp in range(8):
            for tb in range(4):
                k8 = (hp * 4 + tb) % 8
                ps = self.pa(k8) if k8 < 4 else self.pb(k8 - 4)
                pbuf = self.PAb[k8] if k8 < 4 else self.PBb[k8 - 4]
                for lc in range(4):
                    lh = self.wqu.ap(lc * 1024 + hp * 128, 128)
                    rh = lat.ap(lc * 2048 + tb * 512, 512)
                    self.t(lambda e, ps=ps, lh=lh, rh=rh, lc=lc:
                           e.matmul(ps, lh, rh, start=(lc == 0), stop=(lc == 3)),
                           reads=self.wqu.bufs + lat.bufs_for(lc * 2048, 2048), writes=[pbuf],
                           tag=f"{tag}.q{hp}.{tb}")
                dst = qTf.ap(hp * 2048 + tb * 512, 512)
                self.s(lambda e, ps=ps, d=dst: e.activation(out=d, in_=ps, func=AF.Copy, scale=0.125),
                       reads=[pbuf], writes=qTf.bufs_for(hp * 2048, 2048), tag=f"{tag}.qe{hp}.{tb}")
        for r in (lat, self.wcd, self.wqu, self.wkv):
            r.free()
        qTaug = self.AB.alloc(2048, f"qta_{tag}", grain=2048)
        Ap = [self.AB.alloc(2048, f"A{i}_{tag}", grain=2048) for i in range(2)]
        yallT = self.AB.alloc(16384, f"ya_{tag}", grain=2048)
        sqs = self.AB.alloc(2048, f"sqs_{tag}", grain=2048)
        self.s(lambda e: e.activation(out=sqs.ap(0, 2048, r0=0, nrows=64),
                                      in_=kTaug.ap(0, 2048, r0=0, nrows=64), func=AF.Square),
               reads=kTaug.bufs + sqs.bufs, writes=sqs.bufs, tag=f"{tag}.ksq")
        for tb in range(4):
            self.t(lambda e, tb=tb: e.matmul(self.pb(tb, nrows=1), self.onescol_b[0:64, 0:1],
                                             sqs.ap(tb * 512, 512, r0=0, nrows=64), start=True, stop=True),
                   reads=sqs.bufs + [B["onescol_b"]], writes=[self.PBb[tb]], tag=f"{tag}.knm{tb}")
        self.v(lambda e: e.tensor_copy(out=self.gx[0:1, 0:2048], in_=self.PB[0:1, 0:2048]),
               reads=self.PBb + [B["gx"]], writes=[B["gx"]], tag=f"{tag}.knr")
        self.v(lambda e: e.reduce_max(out=self.gx[0:1, 2048 - 1:2048] if False else self.kstat[0:1, 0:1],
                                      in_=self.gx[0:1, 0:2048], axis=AX.X),
               reads=[B["gx"], B["kstat"]], writes=[B["kstat"]], tag=f"{tag}.kmx")
        self.s(lambda e: e.activation(out=self.kstat[0:1, 1:2], in_=self.kstat[0:1, 0:1], func=AF.Ln),
               reads=[B["kstat"]], writes=[B["kstat"]], tag=f"{tag}.kln")
        self.s(lambda e: e.activation(out=self.kstat[0:1, 2:3], in_=self.kstat[0:1, 1:2],
                                      func=AF.Exp, scale=0.5),
               reads=[B["kstat"]], writes=[B["kstat"]], tag=f"{tag}.kex")
        for hh in range(16):
            r0h = (hh % 2) * 64
            ch = (hh // 2) * 2048
            self.v(lambda e, ch=ch, r0h=r0h: e.tensor_copy(out=qTaug.ap(0, 2048, r0=0, nrows=64),
                                                           in_=qTf.ap(ch, 2048, r0=r0h, nrows=64)),
                   reads=qTf.bufs_for(ch, 2048) + qTaug.bufs, writes=qTaug.bufs,
                   tag=f"{tag}.qc{hh}")
            self.s(lambda e: e.activation(out=sqs.ap(0, 2048, r0=0, nrows=64),
                                          in_=qTaug.ap(0, 2048, r0=0, nrows=64), func=AF.Square),
                   reads=qTaug.bufs + sqs.bufs, writes=sqs.bufs, tag=f"{tag}.qsq{hh}")
            for tb in range(4):
                self.t(lambda e, tb=tb: e.matmul(self.pb(tb, nrows=1), self.onescol_b[0:64, 0:1],
                                                 sqs.ap(tb * 512, 512, r0=0, nrows=64), start=True, stop=True),
                       reads=sqs.bufs + [B["onescol_b"]], writes=[self.PBb[tb]], tag=f"{tag}.qnm{hh}.{tb}")
            self.s(lambda e: e.activation(out=self.gx[0:1, 0:2048], in_=self.PB[0:1, 0:2048], func=AF.Ln),
                   reads=self.PBb + [B["gx"]], writes=[B["gx"]], tag=f"{tag}.qln{hh}")
            self.s(lambda e: e.activation(out=self.gx[0:1, 0:2048], in_=self.gx[0:1, 0:2048],
                                          func=AF.Exp, scale=0.5),
                   reads=[B["gx"]], writes=[B["gx"]], tag=f"{tag}.qex{hh}")
            self.v(lambda e: e.tensor_scalar(out=qTaug.ap(0, 2048, r0=64, nrows=1),
                                             in0=self.gx[0:1, 0:2048], scalar1=self.kstat[0:1, 2:3],
                                             scalar2=None, op0=ALU.mult),
                   reads=[B["gx"], B["kstat"]] + qTaug.bufs, writes=qTaug.bufs, tag=f"{tag}.qbr{hh}")
            for si in range(16):
                par = si % 2
                tb0 = si // 4
                for tb in range(tb0, 4):
                    ps, pbuf = self.pa(tb), self.PAb[tb]
                    lh = kTaug.ap(si * 128, 128, r0=0, nrows=65)
                    rh = qTaug.ap(tb * 512, 512, r0=0, nrows=65)
                    self.t(lambda e, ps=ps, lh=lh, rh=rh: e.matmul(ps, lh, rh, start=True, stop=True),
                           reads=kTaug.bufs + qTaug.bufs, writes=[pbuf], tag=f"{tag}.p2m{hh}.{si}.{tb}")
                dps = self.pa(tb0)
                self.v(lambda e, dps=dps, o=(si % 4): e.tensor_add(
                        out=dps, in0=dps, in1=self.masks2(o)),
                       reads=[self.PAb[tb0], B["masks2"]], writes=[self.PAb[tb0]],
                       tag=f"{tag}.p2d{hh}.{si}")
                w = 2048 - tb0 * 512
                dst = Ap[par].ap(tb0 * 512, w)
                self.s(lambda e, tb0=tb0, dst=dst: e.activation(
                        out=dst, in_=self.PA[0:128, tb0 * 512:2048], func=AF.Exp),
                       reads=[self.PAb[b] for b in range(tb0, 4)] + Ap[par].bufs,
                       writes=Ap[par].bufs, tag=f"{tag}.p2e{hh}.{si}")
                for tb in range(tb0, 4):
                    ps, pbuf = self.pb(tb, nrows=65), self.PBb[tb]
                    lh = vaug.ap(si * 65, 65)
                    rh = Ap[par].ap(tb * 512, 512)
                    self.t(lambda e, ps=ps, lh=lh, rh=rh, si=si, tb=tb:
                           e.matmul(ps, lh, rh, start=(si == 0), stop=(si == 4 * tb + 3)),
                           reads=vaug.bufs + Ap[par].bufs, writes=[pbuf],
                           tag=f"{tag}.ym{hh}.{si}.{tb}")
                if (getattr(self, "dbg_l0a", None) is not None and hh == self.dbg_l0a[0]
                        and si == 0):
                    dA = self.dbg_l0a[1]
                    mk = self.dma("gpsimd", lambda e, dA=dA: e.dma_start(out=dA, in_=Ap[0].ap(0, 2048)),
                                  reads=Ap[0].bufs, tag=f"{tag}.dbgA{hh}")
                    self.g(None, deps=[mk], tag=f"{tag}.dbgwa{hh}")
            if getattr(self, "dbg_l0a", None) is not None and hh == self.dbg_l0a[0]:
                dPB = self.dbg_l0a[2]
                dbgR = self.AF.alloc(2048, "dbgpb", grain=2048)
                self.v(lambda e: e.tensor_copy(out=dbgR.ap(0, 2048, r0=0, nrows=65),
                                               in_=self.PB[0:65, 0:2048]),
                       reads=self.PBb + dbgR.bufs, writes=dbgR.bufs, tag=f"{tag}.dbgc{hh}")
                mk2 = self.dma("gpsimd", lambda e, dPB=dPB, dbgR=dbgR: e.dma_start(
                        out=dPB, in_=dbgR.ap(0, 2048, r0=0, nrows=65)),
                               reads=dbgR.bufs, tag=f"{tag}.dbgP{hh}")
                self.g(None, deps=[mk2], tag=f"{tag}.dbgw{hh}")
                dbgR.free()
            self.v(lambda e: e.tensor_copy(out=self.gx[64:65, :], in_=self.PB[64:65, 0:2048]),
                   reads=self.PBb, writes=[B["gx"]], tag=f"{tag}.lv{hh}")
            self.v(lambda e: e.reciprocal(out=self.gx[64:65, :], in_=self.gx[64:65, :]),
                   reads=[B["gx"]], writes=[B["gx"]], tag=f"{tag}.lr{hh}")
            for tb in range(4):
                self.t(lambda e, tb=tb: e.matmul(self.pa(tb, nrows=64), self.onesf[64:65, 0:64],
                                                 self.gx[64:65, tb * 512:(tb + 1) * 512],
                                                 start=True, stop=True),
                       reads=[B["onesf"], B["gx"]], writes=[self.PAb[tb]], tag=f"{tag}.rb{hh}.{tb}")
            self.v(lambda e: e.tensor_copy(out=yscr.ap(0, 2048, r0=0, nrows=64),
                                           in_=self.PB[0:64, 0:2048]),
                   reads=self.PBb + yscr.bufs, writes=yscr.bufs, tag=f"{tag}.ye{hh}")
            ydst = yallT.ap((hh // 2) * 2048, 2048, r0=(hh % 2) * 64, nrows=64)
            self.v(lambda e, ydst=ydst: e.tensor_mul(out=ydst, in0=yscr.ap(0, 2048, r0=0, nrows=64),
                                                     in1=self.PA[0:64, 0:2048]),
                   reads=yscr.bufs + self.PAb + yallT.bufs_for((hh // 2) * 2048, 2048),
                   writes=yallT.bufs_for((hh // 2) * 2048, 2048), tag=f"{tag}.ys{hh}")
        for r in (kTaug, qTf, qTaug, vaug, Ap[0], Ap[1], self.masksR, yscr, sqs):
            r.free()
        attn_out = self.AB.alloc(16384, f"ao_{tag}", grain=2048)
        for oc in range(8):
            for tb in range(4):
                ps, pbuf = self.pa(tb), self.PAb[tb]
                for ic in range(8):
                    lh = self.wcp.ap(ic * 1024 + oc * 128, 128)
                    rh = yallT.ap(ic * 2048 + tb * 512, 512)
                    self.t(lambda e, ps=ps, lh=lh, rh=rh, ic=ic:
                           e.matmul(ps, lh, rh, start=(ic == 0), stop=(ic == 7)),
                           reads=self.wcp.bufs + yallT.bufs_for(ic * 2048, 2048), writes=[pbuf],
                           tag=f"{tag}.pj{oc}.{tb}.{ic}")
                dst = attn_out.ap(oc * 2048 + tb * 512, 512)
                self.s(lambda e, ps=ps, d=dst: e.copy(out=d, in_=ps),
                       reads=[pbuf], writes=attn_out.bufs_for(oc * 2048, 2048), tag=f"{tag}.pe{oc}.{tb}")
        yallT.free()
        self.tail_f(attn_out, tag)
        attn_out.free()
        self.wcp.free()

    def tail_f(self, src, tag):
        B = self.B
        sq = self.AB.alloc(2 * 2048, f"sq_{tag}", grain=2048)
        t1r = self.AF.alloc(2 * 2048, f"t1_{tag}", grain=2048)
        xo = self.AB.alloc(2 * 4096, f"xo_{tag}", grain=4096)
        for dc in range(8):
            i = dc % 2
            sqa = sq.ap(i * 2048, 2048)
            self.s(lambda e, s=src.ap(dc * 2048, 2048), d=sqa: e.activation(out=d, in_=s, func=AF.Square),
                   reads=src.bufs_for(dc * 2048, 2048) + sq.bufs_for(i * 2048, 2048),
                   writes=sq.bufs_for(i * 2048, 2048), tag=f"{tag}.q{dc}")
            for b in range(4):
                self.t(lambda e, b=b, a=src.ap(dc * 2048 + b * 512, 512), dc=dc:
                       e.matmul(self.pa(b, nrows=1), self.onescol_b[:], a, start=(dc == 0), stop=(dc == 7)),
                       reads=src.bufs_for(dc * 2048, 2048) + [B["onescol_b"]], writes=[self.PAb[b]],
                       tag=f"{tag}.m{dc}.{b}")
                self.t(lambda e, b=b, a=sq.ap(i * 2048 + b * 512, 512), dc=dc:
                       e.matmul(self.pb(b, nrows=1), self.onescol_b[:], a, start=(dc == 0), stop=(dc == 7)),
                       reads=sq.bufs_for(i * 2048, 2048) + [B["onescol_b"]], writes=[self.PBb[b]],
                       tag=f"{tag}.s{dc}.{b}")
        self._ln_rows_and_bcast(2048, f"{tag}.t")
        for dc in range(8):
            i = dc % 2
            t1 = t1r.ap(i * 2048, 2048)
            xoa = xo.ap(i * 4096, 4096).bitcast(F32)
            self.dma("scalar", lambda e, d=xoa, dc=dc: e.dma_start(out=d, in_=self.XT[dc * 128:(dc + 1) * 128, :]),
                     reads=[self.XT_b[dc]] + xo.bufs_for(i * 4096, 4096),
                     writes=xo.bufs_for(i * 4096, 4096), tag=f"{tag}.xl{dc}")
            self.v(lambda e, s=src.ap(dc * 2048, 2048), d=t1: e.tensor_sub(out=d, in0=s, in1=self.mbc[:]),
                   reads=src.bufs_for(dc * 2048, 2048) + [B["mbc"]] + t1r.bufs_for(i * 2048, 2048),
                   writes=t1r.bufs_for(i * 2048, 2048), tag=f"{tag}.a{dc}")
            self.v(lambda e, d=t1: e.tensor_mul(out=d, in0=d, in1=self.rbc[:]),
                   reads=t1r.bufs_for(i * 2048, 2048) + [B["rbc"]],
                   writes=t1r.bufs_for(i * 2048, 2048), tag=f"{tag}.b{dc}")
            self.v(lambda e, d=t1, xoa=xoa: e.tensor_add(out=d, in0=d, in1=xoa),
                   reads=t1r.bufs_for(i * 2048, 2048) + xo.bufs_for(i * 4096, 4096),
                   writes=t1r.bufs_for(i * 2048, 2048), tag=f"{tag}.c{dc}")
            dst = self.XT[dc * 128:(dc + 1) * 128, :]
            self.dma("scalar", lambda e, d=dst, s=t1: e.dma_start(out=d, in_=s),
                     reads=t1r.bufs_for(i * 2048, 2048) + [self.XT_b[dc]], writes=[self.XT_b[dc]],
                     tag=f"{tag}.st{dc}")
        sq.free(); t1r.free(); xo.free()

    def l1_attn(self, tag="l1a"):
        B = self.B
        xnT = self.xnT
        wbuf = self.AB.alloc(4096, f"w_{tag}", grain=4096)
        v_all = self.AB.alloc(16384, f"vl_{tag}", grain=1024)
        vbbc = self.AB.alloc(1024, f"vbbc_{tag}", grain=1024)

        stg = self.AB.alloc(2048, f"stg_{tag}", grain=2048)
        wnames = ["L1_attn_q_w", "L1_attn_k_w", "L1_attn_v_w", "L1_attn_l_w"]

        def load_w(i, reg, half):
            wsrc = self.wap[wnames[i]]
            if len(wsrc.tensor.shape) == 3:
                wsrc = wsrc.rearrange("h p d -> (h p) d")
            for rc in range(half * 4, half * 4 + 4):
                j = 0
                sa = stg.ap(j * 2048, 2048).bitcast(F32)
                self.dma("sync", lambda e, d=sa, s=wsrc[rc * 128:(rc + 1) * 128, :]:
                         e.dma_start(out=d, in_=s),
                         reads=stg.bufs_for(j * 2048, 2048), writes=stg.bufs_for(j * 2048, 2048),
                         tag=f"{tag}.wl{i}.{rc}")
                for cc in range(8):
                    ps = self.pb(3, c0=(cc % 4) * 128, ncols=128) if cc % 2 else                          self.pa(3, c0=(cc % 4) * 128, ncols=128)
                    pbuf = self.PBb[3] if cc % 2 else self.PAb[3]
                    tsrc = stg.ap(j * 2048 + cc * 256, 256).bitcast(F32)
                    self.t(lambda e, ps=ps, tsrc=tsrc: e.transpose(ps, tsrc, self.identf[:]),
                           reads=stg.bufs_for(j * 2048, 2048) + [B["identf"]], writes=[pbuf],
                           tag=f"{tag}.wt{i}.{rc}.{cc}")
                    dst = reg.ap(cc * 512 + (rc % 4) * 128, 128)
                    self.v(lambda e, ps=ps, d=dst: e.tensor_copy(out=d, in_=ps),
                           reads=[pbuf] + reg.bufs, writes=reg.bufs, tag=f"{tag}.we{i}.{rc}.{cc}")
        load_w(2, wbuf, 0)
        vbs = self.wap["L1_attn_v_b"].rearrange("h o p -> (h o p)").rearrange("(o n) -> o n", o=1)
        self.dma("scalar", lambda e: e.dma_start(out=self.rows[64:65, 0:1024], in_=vbs),
                 reads=[B["rows"]], writes=[B["rows"]], tag=f"{tag}.vb")
        for hb in range(2):
            self.t(lambda e, hb=hb: e.matmul(self.pb(2 + hb), self.onesf[64:65, 0:128],
                                             self.rows[64:65, hb * 512:(hb + 1) * 512], start=True, stop=True),
                   reads=[B["onesf"], B["rows"]], writes=[self.PBb[2 + hb]], tag=f"{tag}.vc{hb}")
        self.v(lambda e: e.tensor_copy(out=vbbc.ap(0, 1024), in_=self.PB[0:128, 1024:2048]),
               reads=[self.PBb[2], self.PBb[3]], writes=vbbc.bufs, tag=f"{tag}.vbe")
        for hb in range(2):
            if hb == 1:
                load_w(2, wbuf, 1)
            for si in range(16):
                bk = 2 * (si % 2) + hb
                ps, pbuf = self.pa(bk), self.PAb[bk]
                for dc in range(8):
                    lh = xnT.ap(dc * 2048 + si * 128, 128)
                    rh = wbuf.ap(dc * 512, 512)
                    self.t(lambda e, ps=ps, lh=lh, rh=rh, dc=dc:
                           e.matmul(ps, lh, rh, start=(dc == 0), stop=(dc == 7)),
                           reads=xnT.bufs_for(dc * 2048, 2048) + wbuf.bufs, writes=[pbuf],
                           tag=f"{tag}.v{si}.{hb}.{dc}")
                dst = v_all.ap(si * 1024 + hb * 512, 512)
                self.v(lambda e, ps=ps, d=dst, hb=hb:
                       e.tensor_add(out=d, in0=ps, in1=vbbc.ap(hb * 512, 512)),
                       reads=[pbuf] + vbbc.bufs, writes=v_all.bufs_for(si * 1024 + hb * 512, 512),
                       tag=f"{tag}.ve{si}.{hb}")
        vbbc.free()
        qkregs = []
        for (wi, bcol0, scale) in ((0, 0, None), (1, 8, 0.125)):
            reg = self.AB.alloc(16384, f"{'q' if wi == 0 else 'k'}T_{tag}", grain=2048)
            qkregs.append(reg)
            for hp in range(8):
                if hp % 4 == 0:
                    load_w(wi, wbuf, hp // 4)
                for tb in range(4):
                    k8 = (hp * 4 + tb) % 8
                    ps = self.pa(k8) if k8 < 4 else self.pb(k8 - 4)
                    pbuf = self.PAb[k8] if k8 < 4 else self.PBb[k8 - 4]
                    for dc in range(8):
                        lh = wbuf.ap(dc * 512 + (hp % 4) * 128, 128)
                        rh = xnT.ap(dc * 2048 + tb * 512, 512)
                        self.t(lambda e, ps=ps, lh=lh, rh=rh, dc=dc:
                               e.matmul(ps, lh, rh, start=(dc == 0), stop=(dc == 7)),
                               reads=wbuf.bufs + xnT.bufs_for(dc * 2048, 2048), writes=[pbuf],
                               tag=f"{tag}.qk{wi}.{hp}.{tb}")
                    dst = reg.ap(hp * 2048 + tb * 512, 512)
                    bias = self.qkvb[:, bcol0 + hp:bcol0 + hp + 1]
                    if scale is None:
                        self.v(lambda e, ps=ps, d=dst, bias=bias:
                               e.tensor_scalar(out=d, in0=ps, scalar1=bias, scalar2=None, op0=ALU.add),
                               reads=[pbuf, B["qkvb"]] + reg.bufs_for(hp * 2048, 2048),
                               writes=reg.bufs_for(hp * 2048, 2048), tag=f"{tag}.qe{wi}.{hp}.{tb}")
                    else:
                        self.v(lambda e, ps=ps, d=dst, bias=bias, sc_=scale:
                               e.tensor_scalar(out=d, in0=ps, scalar1=bias, scalar2=sc_,
                                               op0=ALU.add, op1=ALU.mult),
                               reads=[pbuf, B["qkvb"]] + reg.bufs_for(hp * 2048, 2048),
                               writes=reg.bufs_for(hp * 2048, 2048), tag=f"{tag}.qe{wi}.{hp}.{tb}")
        qT, kT = qkregs
        xnT.free()
        self.xnT = None
        wbuf.free()
        concatT = self.AB.alloc(16384, f"cc_{tag}", grain=2048)
        Ap = self.AB.alloc(2048, f"A_{tag}", grain=512)
        vsr = self.AB.alloc(2 * 64, f"vs_{tag}", grain=64)
        kstatB = [Buf("kstat0"), Buf("kstat1")]
        for hh in range(16):
            r0h = (hh % 2) * 64
            ch = (hh // 2) * 2048
            for ki in range(16):
                par = ki % 2
                kb = kstatB[par]
                c4 = par * 4
                lh = kT.ap(ch + ki * 128, 128, r0=r0h, nrows=64)
                for qh in range(2):
                    for qb_ in range(2):
                        bank = 2 * qh + qb_
                        ps, pbuf = self.pa(bank), self.PAb[bank]
                        rh = qT.ap(ch + qh * 1024 + qb_ * 512, 512, r0=r0h, nrows=64)
                        self.t(lambda e, ps=ps, lh=lh, rh=rh: e.matmul(ps, lh, rh, start=True, stop=True),
                               reads=kT.bufs_for(ch, 2048) + qT.bufs_for(ch, 2048), writes=[pbuf],
                               tag=f"{tag}.s{hh}.{ki}.{bank}")
                    self.v(lambda e, c4=c4, qh=qh: e.reduce_max(out=self.kstat[:, c4 + qh:c4 + qh + 1],
                            in_=self.PA[0:128, qh * 1024:(qh + 1) * 1024], axis=AX.X),
                           reads=[self.PAb[2 * qh], self.PAb[2 * qh + 1], kb], writes=[kb],
                           tag=f"{tag}.mx{hh}.{ki}.{qh}")
                self.v(lambda e, c4=c4: e.tensor_max(out=self.kstat[:, c4 + 2:c4 + 3],
                        in0=self.kstat[:, c4:c4 + 1], in1=self.kstat[:, c4 + 1:c4 + 2]),
                       reads=[kb], writes=[kb], tag=f"{tag}.mc{hh}.{ki}")
                self.v(lambda e, c4=c4: e.tensor_scalar(out=self.kstat[:, c4 + 2:c4 + 3],
                        in0=self.kstat[:, c4 + 2:c4 + 3], scalar1=-1.0, scalar2=None, op0=ALU.mult),
                       reads=[kb], writes=[kb], tag=f"{tag}.ng{hh}.{ki}")
                for qh in range(2):
                    dst = Ap.ap(qh * 1024, 1024)
                    self.s(lambda e, dst=dst, c4=c4, qh=qh: e.activation(
                            out=dst, in_=self.PA[0:128, qh * 1024:(qh + 1) * 1024],
                            func=AF.Exp, bias=self.kstat[:, c4 + 2:c4 + 3],
                            accum_out=self.kstat[:, c4 + qh:c4 + qh + 1]),
                           reads=[self.PAb[2 * qh], self.PAb[2 * qh + 1], kb] + Ap.bufs_for(qh * 1024, 1024),
                           writes=Ap.bufs_for(qh * 1024, 1024) + [kb], tag=f"{tag}.ex{hh}.{ki}.{qh}")
                self.v(lambda e, c4=c4: e.tensor_add(out=self.kstat[:, c4 + 3:c4 + 4],
                        in0=self.kstat[:, c4:c4 + 1], in1=self.kstat[:, c4 + 1:c4 + 2]),
                       reads=[kb], writes=[kb], tag=f"{tag}.ls{hh}.{ki}")
                self.v(lambda e, c4=c4: e.reciprocal(out=self.kstat[:, c4 + 3:c4 + 4],
                                                     in_=self.kstat[:, c4 + 3:c4 + 4]),
                       reads=[kb], writes=[kb], tag=f"{tag}.rc{hh}.{ki}")
                vs = vsr.ap(par * 64, 64)
                self.v(lambda e, d=vs, s=v_all.ap(ki * 1024 + hh * 64, 64), c4=c4:
                       e.tensor_scalar(out=d, in0=s, scalar1=self.kstat[:, c4 + 3:c4 + 4],
                                       scalar2=None, op0=ALU.mult),
                       reads=v_all.bufs_for(ki * 1024 + hh * 64, 64) + [kb] + vsr.bufs_for(par * 64, 64),
                       writes=vsr.bufs_for(par * 64, 64), tag=f"{tag}.vs{hh}.{ki}")
                for qb_ in range(4):
                    ps, pbuf = self.pb(qb_, nrows=64), self.PBb[qb_]
                    self.t(lambda e, ps=ps, vs=vs, rh=Ap.ap(qb_ * 512, 512), ki=ki:
                           e.matmul(ps, vs, rh, start=(ki == 0), stop=(ki == 15)),
                           reads=vsr.bufs_for(par * 64, 64) + Ap.bufs_for(qb_ * 512, 512), writes=[pbuf],
                           tag=f"{tag}.c{hh}.{ki}.{qb_}")
            cdst = concatT.ap(ch, 2048, r0=r0h, nrows=64)
            self.v(lambda e, d=cdst: e.tensor_copy(out=d, in_=self.PB[0:64, 0:2048]),
                   reads=self.PBb + concatT.bufs_for(ch, 2048),
                   writes=concatT.bufs_for(ch, 2048), tag=f"{tag}.cv{hh}")
        for r in (qT, kT, Ap, vsr, v_all):
            r.free()
        attn_out = self.AB.alloc(16384, f"ao_{tag}", grain=2048)
        wbuf2 = self.AB.alloc(4096, f"w2_{tag}", grain=4096)
        for oc in range(8):
            if oc % 4 == 0:
                load_w(3, wbuf2, oc // 4)
            for tb in range(4):
                ps, pbuf = self.pa(tb), self.PAb[tb]
                for ic in range(8):
                    lh = wbuf2.ap(ic * 512 + (oc % 4) * 128, 128)
                    rh = concatT.ap(ic * 2048 + tb * 512, 512)
                    self.t(lambda e, ps=ps, lh=lh, rh=rh, ic=ic:
                           e.matmul(ps, lh, rh, start=(ic == 0), stop=(ic == 7)),
                           reads=wbuf2.bufs + concatT.bufs_for(ic * 2048, 2048), writes=[pbuf],
                           tag=f"{tag}.pj{oc}.{tb}.{ic}")
                dst = attn_out.ap(oc * 2048 + tb * 512, 512)
                self.v(lambda e, ps=ps, d=dst, oc=oc:
                       e.tensor_scalar(out=d, in0=ps, scalar1=self.lbcols[:, oc:oc + 1],
                                       scalar2=None, op0=ALU.add),
                       reads=[pbuf, B["lbcols"]], writes=attn_out.bufs_for(oc * 2048, 2048),
                       tag=f"{tag}.pe{oc}.{tb}")
        self.tail_f(attn_out, tag)
        for r in (wbuf2, concatT, attn_out, stg):
            r.free()

    def epilog(self):
        B = self.B
        xtt = self.AF.alloc(2 * 1024, "ep_in", grain=1024)
        ott = self.AF.alloc(2 * 1024, "ep_out", grain=1024)
        stores = []
        for ti in range(16):
            i = ti % 2
            dst = xtt.ap(i * 1024, 1024).rearrange("p (c f) -> p c f", c=8)
            src = self.XT[:, ti * 128:(ti + 1) * 128].rearrange("(c p) f -> p c f", p=128)
            self.dma("scalar", lambda e, d=dst, s=src: e.dma_start(out=d, in_=s),
                     reads=[self.XT_b[c] for c in range(8)] + xtt.bufs_for(i * 1024, 1024),
                     writes=xtt.bufs_for(i * 1024, 1024), tag=f"ep.ld{ti}")
            for dc in range(8):
                ps = self.pb(dc % 2, c0=(dc // 2 % 2) * 256, ncols=128)
                pbuf = self.PBb[dc % 2]
                self.t(lambda e, ps=ps, s=xtt.ap(i * 1024 + dc * 128, 128): e.transpose(ps, s, self.identf[:]),
                       reads=xtt.bufs_for(i * 1024, 1024) + [B["identf"]], writes=[pbuf],
                       tag=f"ep.tr{ti}.{dc}")
                self.v(lambda e, ps=ps, d=ott.ap(i * 1024 + dc * 128, 128): e.tensor_copy(out=d, in_=ps),
                       reads=[pbuf] + ott.bufs_for(i * 1024, 1024), writes=ott.bufs_for(i * 1024, 1024),
                       tag=f"ep.ev{ti}.{dc}")
            st = self.dma("scalar", lambda e, s=ott.ap(i * 1024, 1024), ti=ti:
                          e.dma_start(out=self.out[ti * 128:(ti + 1) * 128, :], in_=s),
                          reads=ott.bufs_for(i * 1024, 1024), tag=f"ep.st{ti}")
            stores.append(st)
        fin = self.p.op("scalar", None, deps=stores, tag="ep.fin")
        xtt.free(); ott.free()

    def dump_xt(self, name):
        d = self.nc.dram_tensor(f"dump_{name}", [D, T], F32, kind="ExternalOutput").ap()
        o = self.dma("gpsimd", lambda e, d=d: e.dma_start(out=d, in_=self.XT[:, :]),
                     reads=[self.XT_b[c] for c in range(8)], tag=f"dump.{name}")
        self.p.op("gpsimd", None, deps=[o], tag=f"dumpfin.{name}")

    def dump_region(self, name, region, ncols, rows=128):
        dt = region.arena.h.dtype if hasattr(region.arena.h, "dtype") else BF16
        d = self.nc.dram_tensor(f"dump_{name}", [rows, ncols], dt, kind="ExternalOutput").ap()
        self.dma("gpsimd", lambda e, d=d: e.dma_start(out=d, in_=region.ap(0, ncols, r0=0, nrows=rows)),
                 reads=region.bufs, tag=f"dump.{name}")


def build_nc(blocks=("l0a", "l0f", "l1a", "l1f"), dumps=()):
    nc = bass.Bass("TRN2", target_bir_lowering=False)
    with ExitStack() as ctx:
        b = MKB(nc, ctx, blocks, dumps)
        for nm, shape in WEIGHT_SPECS.items():
            b.wap[nm] = nc.dram_tensor(nm, list(shape), F32, kind="ExternalInput").ap()
        b.prolog_consts()
        if "l0a" in blocks:
            b.load_l0_attn_weights()
        b.prolog_input_transpose()
        b.prolog_biases()
        if "l0f" in blocks:
            b.prolog_casts_ffn(0)
        if "l0a" in blocks:
            b.pre_ln(0, False, "p0")
            b.l0_attn()
            if "xt_l0a" in dumps:
                b.dump_xt("xt_l0a")
        if "l0f" in blocks:
            b.pre_ln(0, True, "g0")
            b.ffn(0, "f0")
            if "xt_l0f" in dumps:
                b.dump_xt("xt_l0f")
        if "l1f" in blocks:
            b.prolog_casts_ffn(1)
        if "l1a" in blocks:
            b.pre_ln(1, False, "p2")
            b.l1_attn()
            if "xt_l1a" in dumps:
                b.dump_xt("xt_l1a")
        if "l1f" in blocks:
            b.pre_ln(1, True, "g1")
            b.ffn(1, "f1")
        b.epilog()
        b.p.finalize(nc, ctx)
        block = ctx.enter_context(nc.Block())
        block.tensor(lambda e: b.p.run_engine(e, "tensor"))
        block.vector(lambda e: b.p.run_engine(e, "vector"))
        block.scalar(lambda e: b.p.run_engine(e, "scalar"))
        block.gpsimd(lambda e: b.p.run_engine(e, "gpsimd"))
        block.sync(lambda e: b.p.run_engine(e, "sync"))
    return nc, b


def kernel(**inputs):
    x = np.ascontiguousarray(inputs["L0_attn_x"], dtype=np.float32)
    nc, _ = build_nc()
    in_maps = []
    for c in range(8):
        m = {"x_in": x[c]}
        for nm in WEIGHT_SPECS:
            m[nm] = np.ascontiguousarray(inputs[nm], np.float32)
        in_maps.append(m)
    res = run_bass_kernel_spmd(nc, in_maps, core_ids=list(range(8)))
    return np.stack([r["x_out"] for r in res.results]).astype(np.float32)


CAP = 768
NTS = CAP // 128


def _sparse_methods():
    def sparse_dram(self):
        nc = self.nc
        self.xnD = nc.dram_tensor("xnD", [T, D], F32, kind="Internal").ap()
        self.xnD_b = Buf("xnD")
        self.accD = nc.dram_tensor("accD", [T, D], F32, kind="Internal").ap()
        self.accD_b = Buf("accD")
        self.yD = nc.dram_tensor("yD", [16384, D], BF16, kind="Internal").ap()
        self.yD_b = [Buf(f"yD{e}") for e in range(8)]
        self.idxD = nc.dram_tensor("idxD", [16384, 1], I32, kind="Internal").ap()
        self.idxD_b = Buf("idxD")

    def prolog_sparse_consts(self):
        B = self.B
        self.trir = self.AB.alloc(256, "trir", grain=256)
        trif = self.trir.ap(0, 256).bitcast(F32)
        self.v(lambda e, trif=trif: e.tensor_single_scalar(out=trif, in_=self.fidxf[:, 0:128],
                                                           scalar=self.pidxf[:, 0:1], op=ALU.is_gt),
               reads=[B["fidxf"], B["pidxf"]] + self.trir.bufs, writes=self.trir.bufs, tag="tri")
        self.idbr = self.AB.alloc(128, "idbr", grain=128)
        self.v(lambda e: e.tensor_copy(out=self.idbr.ap(0, 128), in_=self.identf[:]),
               reads=[B["identf"]] + self.idbr.bufs, writes=self.idbr.bufs, tag="idb")
        self.tokr = self.AB.alloc(32, "tokr", grain=32)
        toki = self.tokr.ap(0, 32).bitcast(I32)
        self.g(lambda e, toki=toki: e.iota(toki, pattern=[[128, 16]], base=0, channel_multiplier=1),
               reads=self.tokr.bufs, writes=self.tokr.bufs, tag="toki")
        self.gm1r = self.AB.alloc(256, "gm1r", grain=256)
        self.gm2r = self.AB.alloc(256, "gm2r", grain=256)

    def prep_sparse(self, L, tag):
        B = self.B
        xnT = self.xnT
        rt = self.AB.alloc(1700, f"rt_{tag}", grain=1700)
        self.rt = rt
        gmB = rt.ap(0, 256).bitcast(F32)
        slt = rt.ap(256, 256).bitcast(F32)
        smf = rt.ap(512, 256).bitcast(F32)
        g1f = rt.ap(768, 32).bitcast(F32)
        g2f = rt.ap(800, 32).bitcast(F32)
        g1t = rt.ap(832, 32).bitcast(F32)
        g2t = rt.ap(864, 32).bitcast(F32)
        cnt = rt.ap(896, 16).bitcast(F32)
        basec = rt.ap(912, 16).bitcast(F32)
        racc = rt.ap(928, 2).bitcast(F32)
        brow = rt.ap(930, 256).bitcast(F32)
        smI = rt.ap(1000, 256).bitcast(I32)
        gxI = rt.ap(1256, 64).bitcast(I32)
        idxS = rt.ap(1320, 96).bitcast(I32)
        zz = rt.ap(1416, 256).bitcast(I32)
        rb = rt.bufs
        m1 = self.gm1r.ap(0, 256).bitcast(F32)
        m2 = self.gm2r.ap(0, 256).bitcast(F32)
        mb1, mb2 = self.gm1r.bufs, self.gm2r.bufs

        xot = self.AB.alloc(2 * 2048, f"xot_{tag}", grain=2048)
        idb = self.idbr.ap(0, 128)
        for ti in range(16):
            i = ti % 2
            ota = xot.ap(i * 2048, 2048).bitcast(F32)
            for dc in range(8):
                slot = dc % 4
                bk = self.pa(2 + slot // 2, c0=(slot % 2) * 256, ncols=64)
                pbuf = self.PAb[2 + slot // 2]
                pbv = bk.bitcast(BF16)
                src = xnT.ap(dc * 2048 + ti * 128, 128)
                self.t(lambda e, d=pbv, s=src, idb=idb: e.transpose(d, s, idb),
                       reads=xnT.bufs_for(dc * 2048, 2048) + self.idbr.bufs, writes=[pbuf],
                       tag=f"{tag}.xt{ti}.{dc}")
                self.v(lambda e, s=pbv, d=ota[0:128, dc * 128:dc * 128 + 128]: e.tensor_copy(out=d, in_=s),
                       reads=[pbuf] + xot.bufs_for(i * 2048, 2048),
                       writes=xot.bufs_for(i * 2048, 2048), tag=f"{tag}.xe{ti}.{dc}")
            self.dma("sync", lambda e, s=ota, ti=ti: e.dma_start(
                    out=self.xnD[ti * 128:(ti + 1) * 128, :], in_=s),
                    reads=xot.bufs_for(i * 2048, 2048) + [self.xnD_b], writes=[self.xnD_b],
                    tag=f"{tag}.xs{ti}")
        xot.free()

        for ap_ in (g1f, g2f, g1t, g2t):
            self.v(lambda e, a=ap_: e.memset(a, 0.0), reads=rb, writes=rb, tag=f"{tag}.rz")
        self.v(lambda e: e.tensor_add(out=gmB, in0=m1, in1=m2),
               reads=mb1 + mb2 + rb, writes=rb, tag=f"{tag}.gmB")
        trif = self.trir.ap(0, 256).bitcast(F32)
        for e_ in range(8):
            self.v(lambda e: e.memset(racc, 0.0), reads=rb, writes=rb, tag=f"{tag}.rc0.{e_}")
            for ti in range(16):
                c = ti * 8 + e_
                self.v(lambda e, c=c: e.tensor_copy(out=slt[0:128, c:c + 1], in_=racc),
                       reads=rb, writes=rb, tag=f"{tag}.pc{e_}.{ti}")
                self.v(lambda e, c=c: e.tensor_add(out=racc, in0=racc, in1=gmB[0:128, c:c + 1]),
                       reads=rb, writes=rb, tag=f"{tag}.pa{e_}.{ti}")
            self.v(lambda e, e_=e_: e.tensor_copy(out=cnt[0:128, e_:e_ + 1], in_=racc),
                   reads=rb, writes=rb, tag=f"{tag}.cn{e_}")
            self.t(lambda e, e_=e_: e.matmul(self.PA[0:1, 1536:1664], cnt[0:128, e_:e_ + 1],
                                             trif, start=True, stop=True),
                   reads=rb + self.trir.bufs, writes=[self.PAb[3]], tag=f"{tag}.tm{e_}")
            self.v(lambda e: e.tensor_copy(out=brow[0:1, 0:128], in_=self.PA[0:1, 1536:1664]),
                   reads=[self.PAb[3]] + rb, writes=rb, tag=f"{tag}.te{e_}")
            self.t(lambda e: e.transpose(self.PA[0:128, 1536:1537], brow[0:1, 0:128],
                                         self.identf[0:1, 0:1]),
                   reads=rb + [B["identf"]], writes=[self.PAb[3]], tag=f"{tag}.tt{e_}")
            self.v(lambda e, e_=e_: e.tensor_copy(out=basec[0:128, e_:e_ + 1],
                                                  in_=self.PA[0:128, 1536:1537]),
                   reads=[self.PAb[3]] + rb, writes=rb, tag=f"{tag}.be{e_}")
            for ti in range(16):
                c = ti * 8 + e_
                self.v(lambda e, c=c, e_=e_: e.tensor_scalar(out=slt[0:128, c:c + 1],
                        in0=slt[0:128, c:c + 1], scalar1=basec[0:128, e_:e_ + 1],
                        scalar2=None, op0=ALU.add),
                       reads=rb, writes=rb, tag=f"{tag}.sb{e_}.{ti}")
                self.v(lambda e, c=c, ti=ti, e_=e_: e.scalar_tensor_tensor(
                        out=g1f[0:128, ti:ti + 1],
                        in0=slt[0:128, c:c + 1] if e_ == 0 else smf[0:128, c:c + 1],
                        scalar=m1[0:128, c:c + 1], in1=g1f[0:128, ti:ti + 1],
                        op0=ALU.mult, op1=ALU.add) if False else
                        e.scalar_tensor_tensor(
                        out=g1f[0:128, ti:ti + 1], in0=slt[0:128, c:c + 1],
                        scalar=m1[0:128, c:c + 1], in1=g1f[0:128, ti:ti + 1],
                        op0=ALU.mult, op1=ALU.add),
                       reads=rb + mb1, writes=rb, tag=f"{tag}.gi1.{e_}.{ti}")
                self.v(lambda e, c=c, ti=ti: e.scalar_tensor_tensor(
                        out=g2f[0:128, ti:ti + 1], in0=slt[0:128, c:c + 1],
                        scalar=m2[0:128, c:c + 1], in1=g2f[0:128, ti:ti + 1],
                        op0=ALU.mult, op1=ALU.add),
                       reads=rb + mb2, writes=rb, tag=f"{tag}.gi2.{e_}.{ti}")
                self.v(lambda e, c=c, ti=ti: e.scalar_tensor_tensor(
                        out=g1t[0:128, ti:ti + 1], in0=self.gcoef[:, c:c + 1],
                        scalar=m1[0:128, c:c + 1], in1=g1t[0:128, ti:ti + 1],
                        op0=ALU.mult, op1=ALU.add),
                       reads=rb + mb1 + [B["gcoef"]], writes=rb, tag=f"{tag}.gt1.{e_}.{ti}")
                self.v(lambda e, c=c, ti=ti: e.scalar_tensor_tensor(
                        out=g2t[0:128, ti:ti + 1], in0=self.gcoef[:, c:c + 1],
                        scalar=m2[0:128, c:c + 1], in1=g2t[0:128, ti:ti + 1],
                        op0=ALU.mult, op1=ALU.add),
                       reads=rb + mb2 + [B["gcoef"]], writes=rb, tag=f"{tag}.gt2.{e_}.{ti}")
                if e_ > 0:
                    self.v(lambda e, c=c, ti=ti, e_=e_: e.scalar_tensor_tensor(
                            out=g1f[0:128, ti:ti + 1], in0=m1[0:128, c:c + 1],
                            scalar=float(e_ * CAP), in1=g1f[0:128, ti:ti + 1],
                            op0=ALU.mult, op1=ALU.add),
                           reads=rb + mb1, writes=rb, tag=f"{tag}.gc1.{e_}.{ti}")
                    self.v(lambda e, c=c, ti=ti, e_=e_: e.scalar_tensor_tensor(
                            out=g2f[0:128, ti:ti + 1], in0=m2[0:128, c:c + 1],
                            scalar=float(e_ * CAP), in1=g2f[0:128, ti:ti + 1],
                            op0=ALU.mult, op1=ALU.add),
                           reads=rb + mb2, writes=rb, tag=f"{tag}.gc2.{e_}.{ti}")
        self.v(lambda e: e.tensor_copy(out=gxI[0:128, 0:16], in_=g1f), reads=rb, writes=rb, tag=f"{tag}.gI1")
        self.v(lambda e: e.tensor_copy(out=gxI[0:128, 16:32], in_=g2f), reads=rb, writes=rb, tag=f"{tag}.gI2")
        self.v(lambda e: e.memset(zz, 0), reads=rb, writes=rb, tag=f"{tag}.zz")
        self.dma("gpsimd", lambda e: e.dma_start(
                out=self.idxD.rearrange("(p c) o -> p (c o)", p=128),
                in_=zz),
                reads=rb + [self.idxD_b], writes=[self.idxD_b], tag=f"{tag}.iz")
        toki = self.tokr.ap(0, 32).bitcast(I32)
        for ti in range(16):
            for half in range(2):
                icol = half * 16 + ti
                self.dma("gpsimd", lambda e, icol=icol, ti=ti: e.indirect_dma_start(
                        out=self.idxD,
                        out_offset=bass.IndirectOffsetOnAxis(gxI[0:128, icol:icol + 1], 0),
                        in_=toki[0:128, ti:ti + 1], in_offset=None),
                        reads=rb + self.tokr.bufs + [self.idxD_b], writes=[self.idxD_b],
                        tag=f"{tag}.sc{ti}.{half}")
        for e_ in range(8):
            for j in range(NTS):
                col = e_ * NTS + j
                self.dma("sync", lambda e, col=col, e_=e_, j=j: e.dma_start(
                        out=idxS[0:128, col:col + 1],
                        in_=self.idxD[e_ * CAP + j * 128:e_ * CAP + (j + 1) * 128, :]),
                        reads=[self.idxD_b] + rb, writes=rb, tag=f"{tag}.il{e_}.{j}")
        self.rt_aps = dict(g1t=g1t, g2t=g2t, gxI=gxI, idxS=idxS)

    def ffn_sparse(self, L, tag):
        B = self.B
        xnT = self.xnT
        idxS = self.rt_aps["idxS"]
        rb = self.rt.bufs
        h = self.AB.alloc(16384, f"h_{tag}", grain=512)
        w1q = self.AB.alloc(2 * 4096, f"w1q_{tag}", grain=4096)
        w2q = self.AB.alloc(2 * 4096, f"w2q_{tag}", grain=4096)
        b2reg = self.AB.alloc(9216, f"b2_{tag}", grain=9216)
        acc = self.AF.alloc(4096, f"acc_{tag}", grain=512)
        b2e = self.wap[{0: "L0_ffn_eb2", 1: "L1_ffn_b2"}[L]].rearrange("e d -> (e d)").rearrange("(o n) -> o n", o=1)
        b2s = self.wap[{0: "L0_ffn_sh_b2", 1: "L1_ffn_bs2"}[L]].rearrange("(o n) -> o n", o=1)
        self.dma("gpsimd", lambda e: e.dma_start(out=b2reg.ap(0, 8192, nrows=1), in_=b2e),
                 reads=b2reg.bufs, writes=b2reg.bufs, tag=f"{tag}.b2e")
        self.dma("gpsimd", lambda e: e.dma_start(out=b2reg.ap(8192, 1024, nrows=1), in_=b2s),
                 reads=b2reg.bufs, writes=b2reg.bufs, tag=f"{tag}.b2s")

        loadseq = []
        for sc in range(4):
            for q8 in range(8):
                loadseq.append(("w1", 8, q8, 0))
            for q8 in range(8):
                loadseq.append(("w2f", 8, q8, 0))
        for e_ in range(8):
            for q8 in range(8):
                loadseq.append(("w1", e_, q8, 0))
            for dh in range(2):
                for q8 in range(8):
                    loadseq.append(("w2h", e_, q8, dh))
        li = [0]

        def plan_load():
            if li[0] >= len(loadseq):
                return
            kind, e_, q8, dh = loadseq[li[0]]
            li[0] += 1
            par = q8 % 2
            if kind == "w1":
                dst = w1q.ap(par * 4096, 4096).rearrange("p (c n) -> p c n", c=8)
                src = self.c1[L][e_ * 1024:(e_ + 1) * 1024, q8 * 512:(q8 + 1) * 512] \
                    .rearrange("(c p) n -> p c n", p=128)
                bufs, cb = w1q.bufs_for(par * 4096, 4096), self.c1_b[L][e_]
            elif kind == "w2f":
                dst = w2q.ap(par * 4096, 4096).rearrange("p (c n) -> p c n", c=4)
                src = self.c2[L][e_ * 4096 + q8 * 512:e_ * 4096 + (q8 + 1) * 512, :] \
                    .rearrange("(c p) n -> p c n", p=128)
                bufs, cb = w2q.bufs_for(par * 4096, 4096), self.c2_b[L][e_]
            else:
                dst = w2q.ap(par * 2048, 2048).rearrange("p (c n) -> p c n", c=4)
                src = self.c2[L][e_ * 4096 + q8 * 512:e_ * 4096 + (q8 + 1) * 512,
                                 dh * 512:(dh + 1) * 512].rearrange("(c p) n -> p c n", p=128)
                bufs, cb = w2q.bufs_for(par * 2048, 2048), self.c2_b[L][e_]
            self.dma("sync", lambda e, d=dst, s=src: e.dma_start(out=d, in_=s),
                     reads=[cb] + bufs, writes=bufs, tag=f"{tag}.{kind}ld{e_}.{q8}.{dh}")

        plan_load(); plan_load()
        e_ = 8
        for sc in range(4):
            for q8 in range(8):
                par = q8 % 2
                for j in range(4):
                    hc = q8 * 4 + j
                    ps, pbuf = self.pb(hc % 2), self.PBb[hc % 2]
                    for dc in range(8):
                        lh = w1q.ap(par * 4096 + dc * 512 + j * 128, 128)
                        rh = xnT.ap(dc * 2048 + sc * 512, 512)
                        self.t(lambda e, ps=ps, lh=lh, rh=rh, dc=dc:
                               e.matmul(ps, lh, rh, start=(dc == 0), stop=(dc == 7)),
                               reads=w1q.bufs_for(par * 4096, 4096) + xnT.bufs_for(dc * 2048, 2048),
                               writes=[pbuf], tag=f"{tag}.up{sc}.{hc}")
                    bcol = L * 288 + 256 + hc
                    self.s(lambda e, ps=ps, ha=h.ap(hc * 512, 512), bcol=bcol:
                           e.activation(out=ha, in_=ps, func=GELU_L[L],
                                        bias=self.b1cols[:, bcol:bcol + 1]),
                           reads=[pbuf, B["b1cols"]], writes=h.bufs_for(hc * 512, 512),
                           tag=f"{tag}.ge{sc}.{hc}")
                plan_load()
            banks = [(self.pa(k), self.PAb[k]) for k in range(4)] + \
                    [(self.pb(k), self.PBb[k]) for k in range(4)]
            for k in range(8):
                tsub, dh = k // 2, k % 2
                ps, pbuf = banks[k]
                c0 = 8192 + dh * 512
                self.t(lambda e, ps=ps, brow=b2reg.ap(c0, 512, nrows=1):
                       e.matmul(ps, self.onesb[0:1, 0:128], brow, start=True, stop=False),
                       reads=[B["onesb"]] + b2reg.bufs, writes=[pbuf], tag=f"{tag}.au{sc}.{k}")
            for q8 in range(8):
                par = q8 % 2
                for j in range(4):
                    hc = q8 * 4 + j
                    for k in range(8):
                        tsub, dh = k // 2, k % 2
                        ps, pbuf = banks[k]
                        lh = h.ap(hc * 512 + tsub * 128, 128)
                        rh = w2q.ap(par * 4096 + j * 1024 + dh * 512, 512)
                        self.t(lambda e, ps=ps, lh=lh, rh=rh, last=(q8 == 7 and j == 3):
                               e.matmul(ps, lh, rh, start=False, stop=last),
                               reads=h.bufs_for(hc * 512, 512) + w2q.bufs_for(par * 4096, 4096),
                               writes=[pbuf], tag=f"{tag}.dn{sc}.{hc}.{k}")
                plan_load()
            for k in range(8):
                tsub, dh = k // 2, k % 2
                ps, pbuf = banks[k]
                aap = acc.ap(tsub * 1024 + dh * 512, 512)
                self.v(lambda e, ps=ps, aap=aap: e.tensor_copy(out=aap, in_=ps),
                       reads=[pbuf] + acc.bufs_for(tsub * 1024 + dh * 512, 512),
                       writes=acc.bufs_for(tsub * 1024 + dh * 512, 512), tag=f"{tag}.shv{sc}.{k}")
            for tsub in range(4):
                self.dma("scalar", lambda e, s=acc.ap(tsub * 1024, 1024), sc=sc, tsub=tsub:
                         e.dma_start(out=self.accD[sc * 512 + tsub * 128:sc * 512 + (tsub + 1) * 128, :], in_=s),
                         reads=acc.bufs_for(tsub * 1024, 1024) + [self.accD_b], writes=[self.accD_b],
                         tag=f"{tag}.shs{sc}.{tsub}")
        xnT.free()
        self.xnT = None
        h.free()
        hS = self.AB.alloc(32 * CAP, f"hS_{tag}", grain=CAP)
        xgT = [self.AB.alloc(8 * CAP, f"xgT{i}_{tag}", grain=CAP) for i in range(2)]
        xgk = self.AB.alloc(2048, f"xgk_{tag}", grain=2048)
        ysb = self.AB.alloc(2 * 512, f"ysb_{tag}", grain=512)

        def plan_gather(e_):
            reg = xgT[e_ % 2]
            for j in range(NTS):
                i = 0
                ga = xgk.ap(0, 2048).bitcast(F32)
                icol = e_ * NTS + j
                self.dma("gpsimd", lambda e, d=ga, icol=icol: e.indirect_dma_start(
                        out=d, out_offset=None, in_=self.xnD,
                        in_offset=bass.IndirectOffsetOnAxis(idxS[0:128, icol:icol + 1], 0)),
                        reads=[self.xnD_b] + rb + xgk.bufs_for(i * 2048, 2048),
                        writes=xgk.bufs_for(i * 2048, 2048), tag=f"{tag}.g{e_}.{j}")
                for dc in range(8):
                    slot = dc % 4
                    ps = self.pa(2 + slot // 2, c0=(slot % 2) * 256, ncols=128)
                    pbuf = self.PAb[2 + slot // 2]
                    self.t(lambda e, ps=ps, s=ga[0:128, dc * 128:(dc + 1) * 128]:
                           e.transpose(ps, s, self.identf[:]),
                           reads=xgk.bufs_for(i * 2048, 2048) + [B["identf"]], writes=[pbuf],
                           tag=f"{tag}.gt{e_}.{j}.{dc}")
                    dcol = dc * CAP + j * 128
                    self.v(lambda e, ps=ps, d=reg.ap(dcol, 128): e.tensor_copy(out=d, in_=ps),
                           reads=[pbuf] + reg.bufs_for(dcol, 128), writes=reg.bufs_for(dcol, 128),
                           tag=f"{tag}.gv{e_}.{j}.{dc}")

        plan_gather(0)
        for e_ in range(8):
            reg = xgT[e_ % 2]
            for q8 in range(8):
                par = q8 % 2
                for j in range(4):
                    hc = q8 * 4 + j
                    b0 = (hc % 2) * 2
                    psA, pbA = self.pa(b0), self.PAb[b0]
                    psB, pbB = self.pa(b0 + 1, ncols=256), self.PAb[b0 + 1]
                    for dc in range(8):
                        lh = w1q.ap(par * 4096 + dc * 512 + j * 128, 128)
                        self.t(lambda e, ps=psA, lh=lh, rh=reg.ap(dc * CAP, 512), dc=dc:
                               e.matmul(ps, lh, rh, start=(dc == 0), stop=(dc == 7)),
                               reads=w1q.bufs_for(par * 4096, 4096) + reg.bufs_for(dc * CAP, CAP),
                               writes=[pbA], tag=f"{tag}.e{e_}up{hc}a")
                        self.t(lambda e, ps=psB, lh=lh, rh=reg.ap(dc * CAP + 512, 256), dc=dc:
                               e.matmul(ps, lh, rh, start=(dc == 0), stop=(dc == 7)),
                               reads=w1q.bufs_for(par * 4096, 4096) + reg.bufs_for(dc * CAP, CAP),
                               writes=[pbB], tag=f"{tag}.e{e_}up{hc}b")
                    bcol = L * 288 + e_ * 32 + hc
                    self.s(lambda e, ps=psA, d=hS.ap(hc * CAP, 512), bcol=bcol:
                           e.activation(out=d, in_=ps, func=GELU_L[L], bias=self.b1cols[:, bcol:bcol + 1]),
                           reads=[pbA, B["b1cols"]] + hS.bufs_for(hc * CAP, CAP),
                           writes=hS.bufs_for(hc * CAP, CAP), tag=f"{tag}.e{e_}ge{hc}a")
                    self.s(lambda e, ps=psB, d=hS.ap(hc * CAP + 512, 256), bcol=bcol:
                           e.activation(out=d, in_=ps, func=GELU_L[L], bias=self.b1cols[:, bcol:bcol + 1]),
                           reads=[pbB, B["b1cols"]] + hS.bufs_for(hc * CAP, CAP),
                           writes=hS.bufs_for(hc * CAP, CAP), tag=f"{tag}.e{e_}ge{hc}b")
                plan_load()
            if e_ < 7:
                plan_gather(e_ + 1)
            for dh in range(2):
                dbanks = [(self.pb(k), self.PBb[k]) for k in range(4)] + \
                         [(self.pa(k), self.PAb[k]) for k in range(2)]
                for k in range(NTS):
                    ps, pbuf = dbanks[k]
                    c0 = e_ * 1024 + dh * 512
                    self.t(lambda e, ps=ps, brow=b2reg.ap(c0, 512, nrows=1):
                           e.matmul(ps, self.onesb[0:1, 0:128], brow, start=True, stop=False),
                           reads=[B["onesb"]] + b2reg.bufs, writes=[pbuf], tag=f"{tag}.e{e_}au{dh}.{k}")
                for q8 in range(8):
                    par = q8 % 2
                    for j in range(4):
                        hc = q8 * 4 + j
                        for k in range(NTS):
                            ps, pbuf = dbanks[k]
                            lh = hS.ap(hc * CAP + k * 128, 128)
                            rh = w2q.ap(par * 2048 + j * 512, 512)
                            self.t(lambda e, ps=ps, lh=lh, rh=rh, last=(q8 == 7 and j == 3):
                                   e.matmul(ps, lh, rh, start=False, stop=last),
                                   reads=hS.bufs_for(hc * CAP, CAP) + w2q.bufs_for(par * 2048, 2048),
                                   writes=[pbuf], tag=f"{tag}.e{e_}dn{dh}.{hc}.{k}")
                    plan_load()
                for k in range(NTS):
                    ps, pbuf = dbanks[k]
                    i = k % 2
                    ya = ysb.ap(i * 512, 512)
                    self.v(lambda e, ps=ps, d=ya: e.tensor_copy(out=d, in_=ps),
                           reads=[pbuf] + ysb.bufs_for(i * 512, 512),
                           writes=ysb.bufs_for(i * 512, 512), tag=f"{tag}.e{e_}ye{dh}.{k}")
                    r0 = e_ * CAP + k * 128
                    self.dma("scalar", lambda e, s=ya, r0=r0, dh=dh: e.dma_start(
                            out=self.yD[r0:r0 + 128, dh * 512:(dh + 1) * 512], in_=s),
                            reads=ysb.bufs_for(i * 512, 512) + [self.yD_b[e_]],
                            writes=[self.yD_b[e_]], tag=f"{tag}.e{e_}ys{dh}.{k}")
        for r in (hS, xgT[0], xgT[1], xgk, ysb, w1q, w2q, b2reg, acc):
            r.free()
        self._recombine(L, tag)
        self.rt.free()

    def _recombine(self, L, tag):
        B = self.B
        g1t, g2t, gxI = self.rt_aps["g1t"], self.rt_aps["g2t"], self.rt_aps["gxI"]
        rb = self.rt.bufs
        fac = self.AF.alloc(2 * 1024, f"fac_{tag}", grain=1024)
        sqr = self.AB.alloc(2 * 1024, f"sqr_{tag}", grain=1024)
        y1r = self.AB.alloc(2 * 1024, f"y1_{tag}", grain=1024)
        y2r = self.AB.alloc(2 * 1024, f"y2_{tag}", grain=1024)
        xnr = self.AB.alloc(2 * 2048, f"xnr_{tag}", grain=2048)
        xor_ = self.AB.alloc(2 * 2048, f"xor_{tag}", grain=2048)
        stc = self.AF.alloc(16, f"stc_{tag}", grain=16)
        lnr = self.AF.alloc(2 * 1024, f"lnr_{tag}", grain=1024)
        for ti in range(16):
            i = ti % 2
            fa = fac.ap(i * 1024, 1024)
            self.dma("scalar", lambda e, d=fa, ti=ti: e.dma_start(
                    out=d, in_=self.accD[ti * 128:(ti + 1) * 128, :]),
                    reads=[self.accD_b] + fac.bufs_for(i * 1024, 1024),
                    writes=fac.bufs_for(i * 1024, 1024), tag=f"{tag}.ral{ti}")
            y1a = y1r.ap(i * 1024, 1024)
            self.dma("gpsimd", lambda e, d=y1a, ti=ti: e.indirect_dma_start(
                    out=d, out_offset=None, in_=self.yD,
                    in_offset=bass.IndirectOffsetOnAxis(gxI[0:128, ti:ti + 1], 0)),
                    reads=[self.yD_b[x] for x in range(8)] + rb + y1r.bufs_for(i * 1024, 1024),
                    writes=y1r.bufs_for(i * 1024, 1024), tag=f"{tag}.rg1{ti}")
            y2a = y2r.ap(i * 1024, 1024)
            self.dma("gpsimd", lambda e, d=y2a, ti=ti: e.indirect_dma_start(
                    out=d, out_offset=None, in_=self.yD,
                    in_offset=bass.IndirectOffsetOnAxis(gxI[0:128, 16 + ti:17 + ti], 0)),
                    reads=[self.yD_b[x] for x in range(8)] + rb + y2r.bufs_for(i * 1024, 1024),
                    writes=y2r.bufs_for(i * 1024, 1024), tag=f"{tag}.rg2{ti}")
            self.v(lambda e, fa=fa, y1a=y1a, ti=ti: e.scalar_tensor_tensor(
                    out=fa, in0=y1a, scalar=g1t[0:128, ti:ti + 1], in1=fa,
                    op0=ALU.mult, op1=ALU.add),
                   reads=y1r.bufs_for(i * 1024, 1024) + rb + fac.bufs_for(i * 1024, 1024),
                   writes=fac.bufs_for(i * 1024, 1024), tag=f"{tag}.rc1{ti}")
            self.v(lambda e, fa=fa, y2a=y2a, ti=ti: e.scalar_tensor_tensor(
                    out=fa, in0=y2a, scalar=g2t[0:128, ti:ti + 1], in1=fa,
                    op0=ALU.mult, op1=ALU.add),
                   reads=y2r.bufs_for(i * 1024, 1024) + rb + fac.bufs_for(i * 1024, 1024),
                   writes=fac.bufs_for(i * 1024, 1024), tag=f"{tag}.rc2{ti}")
            if L == 0:
                xna = xnr.ap(i * 2048, 2048).bitcast(F32)
                self.dma("scalar", lambda e, d=xna, ti=ti: e.dma_start(
                        out=d, in_=self.xnD[ti * 128:(ti + 1) * 128, :]),
                        reads=[self.xnD_b] + xnr.bufs_for(i * 2048, 2048),
                        writes=xnr.bufs_for(i * 2048, 2048), tag=f"{tag}.rxn{ti}")
                self.v(lambda e, fa=fa, xna=xna: e.tensor_add(out=fa, in0=fa, in1=xna),
                       reads=xnr.bufs_for(i * 2048, 2048) + fac.bufs_for(i * 1024, 1024),
                       writes=fac.bufs_for(i * 1024, 1024), tag=f"{tag}.rc3{ti}")
            sc0 = stc.ap(0, 16)
            self.v(lambda e, fa=fa: e.reduce_sum(out=sc0[0:128, 0:1], in_=fa, axis=AX.X),
                   reads=fac.bufs_for(i * 1024, 1024) + stc.bufs, writes=stc.bufs, tag=f"{tag}.rm{ti}")
            sq = sqr.ap(i * 1024, 1024)
            self.s(lambda e, fa=fa, d=sq: e.activation(out=d, in_=fa, func=AF.Square),
                   reads=fac.bufs_for(i * 1024, 1024) + sqr.bufs_for(i * 1024, 1024),
                   writes=sqr.bufs_for(i * 1024, 1024), tag=f"{tag}.rsq{ti}")
            self.v(lambda e, d=sq: e.reduce_sum(out=sc0[0:128, 1:2], in_=d, axis=AX.X),
                   reads=sqr.bufs_for(i * 1024, 1024) + stc.bufs, writes=stc.bufs, tag=f"{tag}.rs{ti}")
            self.v(lambda e: e.tensor_scalar(out=sc0[0:128, 0:1], in0=sc0[0:128, 0:1],
                                             scalar1=1.0 / D, scalar2=None, op0=ALU.mult),
                   reads=stc.bufs, writes=stc.bufs, tag=f"{tag}.r1{ti}")
            self.v(lambda e: e.tensor_scalar(out=sc0[0:128, 1:2], in0=sc0[0:128, 1:2],
                                             scalar1=1.0 / D, scalar2=None, op0=ALU.mult),
                   reads=stc.bufs, writes=stc.bufs, tag=f"{tag}.r2{ti}")
            self.v(lambda e: e.tensor_mul(out=sc0[0:128, 2:3], in0=sc0[0:128, 0:1], in1=sc0[0:128, 0:1]),
                   reads=stc.bufs, writes=stc.bufs, tag=f"{tag}.r3{ti}")
            self.v(lambda e: e.tensor_sub(out=sc0[0:128, 1:2], in0=sc0[0:128, 1:2], in1=sc0[0:128, 2:3]),
                   reads=stc.bufs, writes=stc.bufs, tag=f"{tag}.r4{ti}")
            self.s(lambda e: e.activation(out=sc0[0:128, 2:3], in_=sc0[0:128, 1:2],
                                          func=AF.Ln, bias=self.eps[0:128, 0:1]),
                   reads=stc.bufs + [B["eps"]], writes=stc.bufs, tag=f"{tag}.r5{ti}")
            self.s(lambda e: e.activation(out=sc0[0:128, 3:4], in_=sc0[0:128, 2:3],
                                          func=AF.Exp, scale=-0.5),
                   reads=stc.bufs, writes=stc.bufs, tag=f"{tag}.r6{ti}")
            ln = lnr.ap(i * 1024, 1024)
            self.v(lambda e, fa=fa, d=ln: e.tensor_scalar(out=d, in0=fa,
                    scalar1=sc0[0:128, 0:1], scalar2=sc0[0:128, 3:4],
                    op0=ALU.subtract, op1=ALU.mult),
                   reads=fac.bufs_for(i * 1024, 1024) + stc.bufs + lnr.bufs_for(i * 1024, 1024),
                   writes=lnr.bufs_for(i * 1024, 1024), tag=f"{tag}.rap{ti}")
            xoa = xor_.ap(i * 2048, 2048).bitcast(F32)
            src3 = self.XT[:, ti * 128:(ti + 1) * 128].rearrange("(c p) f -> p c f", p=128)
            self.dma("scalar", lambda e, d=xoa.rearrange("p (c f) -> p c f", c=8), s=src3:
                     e.dma_start(out=d, in_=s),
                     reads=[self.XT_b[c] for c in range(8)] + xor_.bufs_for(i * 2048, 2048),
                     writes=xor_.bufs_for(i * 2048, 2048), tag=f"{tag}.rxo{ti}")
            for dc in range(8):
                slot = dc % 4
                ps = self.pb(2 + slot // 2, c0=(slot % 2) * 256, ncols=128)
                pbuf = self.PBb[2 + slot // 2]
                self.t(lambda e, ps=ps, s=ln[0:128, dc * 128:(dc + 1) * 128]:
                       e.transpose(ps, s, self.identf[:]),
                       reads=lnr.bufs_for(i * 1024, 1024) + [B["identf"]], writes=[pbuf],
                       tag=f"{tag}.rtr{ti}.{dc}")
                self.v(lambda e, ps=ps, d=xoa[0:128, dc * 128:(dc + 1) * 128]:
                       e.tensor_add(out=d, in0=d, in1=ps),
                       reads=[pbuf] + xor_.bufs_for(i * 2048, 2048),
                       writes=xor_.bufs_for(i * 2048, 2048), tag=f"{tag}.rad{ti}.{dc}")
            self.dma("scalar", lambda e, s=xoa.rearrange("p (c f) -> p c f", c=8), d3=src3:
                     e.dma_start(out=d3, in_=s),
                     reads=xor_.bufs_for(i * 2048, 2048) + [self.XT_b[c] for c in range(8)],
                     writes=[self.XT_b[c] for c in range(8)], tag=f"{tag}.rst{ti}")
        for r in (fac, sqr, y1r, y2r, xnr, xor_, stc, lnr):
            r.free()

    for f in (sparse_dram, prolog_sparse_consts, prep_sparse, ffn_sparse, _recombine):
        setattr(MKB, f.__name__, f)


_sparse_methods()


# version 06
# speedup vs baseline: 1.1772x; 1.1772x over previous
import numpy as np
import ml_dtypes
from contextlib import ExitStack
import concourse.bass as bass
import concourse.mybir as mybir
from concourse.bass_utils import run_bass_kernel_spmd

F32 = mybir.dt.float32
BF16 = mybir.dt.bfloat16
I32 = mybir.dt.int32
AF = mybir.ActivationFunctionType
ALU = mybir.AluOpType
AX = mybir.AxisListType

T = 2048
D = 1024
ENGINES = ("tensor", "vector", "scalar", "gpsimd", "sync")
SEMCLASSES = ("t", "v", "s", "g", "gd", "syd", "sd")
GELU_L = {0: AF.Gelu_apprx_tanh, 1: AF.Gelu}


class Buf:
    __slots__ = ("name", "writer", "readers", "pending")
    def __init__(self, name, pending=()):
        self.name = name
        self.writer = None
        self.readers = []
        self.pending = list(pending)


class Op:
    __slots__ = ("engine", "emit", "idx", "semclass", "inc_amt", "need_inc",
                 "deps", "waits", "tag", "post")
    def __init__(self, engine, emit, semclass, inc_amt, tag):
        self.engine = engine
        self.emit = emit
        self.semclass = semclass
        self.inc_amt = inc_amt
        self.need_inc = False
        self.deps = set()
        self.waits = []
        self.tag = tag
        self.post = 0


class Plan:
    def __init__(self):
        self.ops = []
        self.eng_ops = {e: [] for e in ENGINES}

    def op(self, engine, emit, reads=(), writes=(), deps=(), dma=False, tag=""):
        if dma:
            semclass = {"gpsimd": "gd", "sync": "syd", "scalar": "sd"}[engine]
            amt = 16
        else:
            semclass = {"tensor": "t", "vector": "v", "scalar": "s", "gpsimd": "g"}[engine]
            amt = 1
        o = Op(engine, emit, semclass, amt, tag)
        o.idx = len(self.ops)
        dd = set(deps)
        for b in reads:
            if b.writer is not None:
                dd.add(b.writer)
            dd.update(b.pending)
            b.pending = []
        for b in writes:
            if b.writer is not None:
                dd.add(b.writer)
            dd.update(b.readers)
            dd.update(b.pending)
            b.pending = []
        for d in dd:
            if d is None or d is o:
                continue
            if d.engine == engine and engine == "tensor" and not dma and d.semclass == "t":
                continue
            o.deps.add(d)
            d.need_inc = True
        for b in reads:
            b.readers.append(o)
        for b in writes:
            b.writer = o
            b.readers = []
        self.ops.append(o)
        self.eng_ops[engine].append(o)
        return o

    def finalize(self, nc, ctx):
        sems = {c: ctx.enter_context(nc.semaphore(f"sem_{c}")) for c in SEMCLASSES}
        counts = {c: 0 for c in SEMCLASSES}
        for o in self.ops:
            if o.need_inc:
                counts[o.semclass] += o.inc_amt
            o.post = counts[o.semclass]
        hw = {e: {c: 0 for c in SEMCLASSES} for e in ENGINES}
        for o in self.ops:
            req = {}
            for d in o.deps:
                assert d.need_inc and d.post > 0, (d.tag, o.tag)
                c = d.semclass
                if d.post > req.get(c, 0):
                    req[c] = d.post
            for c, v in sorted(req.items()):
                if v > hw[o.engine][c]:
                    hw[o.engine][c] = v
                    o.waits.append((c, v))
        self.sems = sems
        return sems

    def run_engine(self, eng_obj, ename):
        sems = self.sems
        for o in self.eng_ops[ename]:
            for (c, v) in o.waits:
                eng_obj.wait_ge(sems[c], v)
            if o.emit is None:
                assert not o.need_inc, f"marker op with dependents on {ename}: {o.tag}"
                continue
            try:
                inst = o.emit(eng_obj)
            except Exception:
                print(f"EMIT FAILED: engine={ename} tag={o.tag}")
                raise
            if o.need_inc:
                inst.then_inc(sems[o.semclass], o.inc_amt)


class Region:
    def __init__(self, arena, col0, ncols, name, grain, pending):
        self.arena = arena
        self.col0 = col0
        self.ncols = ncols
        self.name = name
        self.grain = grain
        self.alive = True
        n = (ncols + grain - 1) // grain
        self.bufs = [Buf(f"{name}[{i}]", pending) for i in range(n)]

    def free(self):
        self.alive = False

    def ap(self, c0=0, ncols=None, r0=0, nrows=128):
        ncols = self.ncols - c0 if ncols is None else ncols
        return self.arena.h[r0:r0 + nrows, self.col0 + c0:self.col0 + c0 + ncols]

    def bufs_for(self, c0=0, ncols=None):
        ncols = self.ncols - c0 if ncols is None else ncols
        i0 = c0 // self.grain
        i1 = (c0 + ncols - 1) // self.grain
        return self.bufs[i0:i1 + 1]


class Arena:
    def __init__(self, handle, ncols, name):
        self.h = handle
        self.ncols = ncols
        self.name = name
        self.owners = []
        self.peak = 0

    def alloc(self, ncols, name, grain=None):
        taken = sorted((c0, c1) for (c0, c1, r) in self.owners if r.alive)
        pos = 0
        for (c0, c1) in taken:
            if c0 - pos >= ncols:
                break
            pos = max(pos, c1)
        assert pos + ncols <= self.ncols, f"{self.name} arena full: {ncols} for {name} at {pos}"
        pend = []
        for (c0, c1, r) in self.owners:
            if not r.alive and not (c1 <= pos or c0 >= pos + ncols):
                for b in r.bufs:
                    if b.writer is not None:
                        pend.append(b.writer)
                    pend.extend(b.readers)
        reg = Region(self, pos, ncols, name, grain or ncols, pend)
        self.owners.append((pos, pos + ncols, reg))
        self.peak = max(self.peak, max(c1 for (c0, c1, r) in self.owners if r.alive))
        return reg


class DBufs:
    def __init__(self, name, n):
        self.bufs = [Buf(f"{name}.{i}") for i in range(n)]
    def __getitem__(self, i):
        return self.bufs[i]


WEIGHT_SPECS = {
    "L0_attn_Wc_down": (1024, 512), "L0_attn_Wq_up": (512, 1024), "L0_attn_Wkv_up": (512, 128),
    "L0_attn_Wc_proj": (1024, 1024),
    "L0_ffn_gate_w": (1024, 8), "L0_ffn_gate_b": (8,),
    "L0_ffn_sh_w1": (1024, 4096), "L0_ffn_sh_b1": (4096,),
    "L0_ffn_sh_w2": (4096, 1024), "L0_ffn_sh_b2": (1024,),
    "L0_ffn_ew1": (8, 1024, 4096), "L0_ffn_eb1": (8, 4096),
    "L0_ffn_ew2": (8, 4096, 1024), "L0_ffn_eb2": (8, 1024),
    "L1_attn_q_w": (16, 64, 1024), "L1_attn_q_b": (16, 1, 64),
    "L1_attn_k_w": (16, 64, 1024), "L1_attn_k_b": (16, 1, 64),
    "L1_attn_v_w": (16, 64, 1024), "L1_attn_v_b": (16, 1, 64),
    "L1_attn_l_w": (1024, 1024), "L1_attn_l_b": (1, 1024),
    "L1_ffn_gate_w": (1024, 8), "L1_ffn_gate_b": (8,),
    "L1_ffn_w1": (8, 1024, 4096), "L1_ffn_b1": (8, 4096),
    "L1_ffn_w2": (8, 4096, 1024), "L1_ffn_b2": (8, 1024),
    "L1_ffn_ws1": (1024, 4096), "L1_ffn_bs1": (4096,),
    "L1_ffn_ws2": (4096, 1024), "L1_ffn_bs2": (1024,),
}


class MKB:
    def __init__(self, nc, ctx, blocks, dumps):
        self.nc = nc
        self.ctx = ctx
        self.p = Plan()
        self.blocks = blocks
        self.dumps = set(dumps)

        self.x_in = nc.dram_tensor("x_in", [T, D], F32, kind="ExternalInput").ap()
        self.out = nc.dram_tensor("x_out", [T, D], F32, kind="ExternalOutput").ap()
        self.wap = {}

        self.XT = nc.dram_tensor("XT", [D, T], F32, kind="Internal").ap()
        self.XT_b = DBufs("XT", 8)
        self.c1 = [nc.dram_tensor(f"c1_{L}", [9 * 1024, 4096], BF16, kind="Internal").ap() for L in range(2)]
        self.c2 = [nc.dram_tensor(f"c2_{L}", [9 * 4096, 1024], BF16, kind="Internal").ap() for L in range(2)]
        self.c1_b = [DBufs(f"c1_{L}", 9) for L in range(2)]
        self.c2_b = [DBufs(f"c2_{L}", 9) for L in range(2)]

        sb = lambda name, shape, dt: ctx.enter_context(nc.sbuf_tensor(name, shape, dt))
        self.identf = sb("identf", [128, 128], F32)
        self.onesb = sb("onesb", [1, 512], BF16)
        self.onesf = sb("onesf", [65, 512], F32)
        self.onescol_b = sb("onescol_b", [128, 1], BF16)
        self.onescol_f = sb("onescol_f", [128, 1], F32)
        self.fidxf = sb("fidxf", [128, 512], F32)
        self.pidxf = sb("pidxf", [128, 1], F32)
        self.eps = sb("eps", [128, 1], F32)
        self.rows = sb("rows", [65, 2048], F32)
        self.srow = sb("srow", [1, 2048], F32)
        self.gx = sb("gx", [65, 2048], F32)
        self.mbc = sb("mbc", [128, 2048], BF16)
        self.rbc = sb("rbc", [128, 2048], BF16)
        self.b1cols = sb("b1cols", [128, 576], F32)
        self.qkvb = sb("qkvb", [128, 24], F32)
        self.lbcols = sb("lbcols", [128, 8], F32)
        self.gl_tok = sb("gl_tok", [128, 128], F32)
        self.gscr = sb("gscr", [128, 128], F32)
        self.gscr2 = sb("gscr2", [128, 128], F32)
        self.gscr3 = sb("gscr3", [128, 128], F32)
        self.gcoef = sb("gcoef", [128, 128], F32)
        self.gst = sb("gst", [128, 64], F32)
        self.gw_sb = sb("gw_sb", [128, 64], F32)
        self.gb_sb = sb("gb_sb", [1, 8], F32)
        self.gbc = sb("gbc", [40, 1], F32)
        self.s_col = sb("s_col", [8, 1], F32)
        self.Mcol = sb("Mcol", [128, 16], F32)
        self.kstat = sb("kstat", [128, 8], F32)
        self.bias_ld = sb("bias_ld", [128, 128], F32)

        names = ["identf", "onesb", "onesf", "onescol_b", "onescol_f", "masks1", "masks2", "vb_bc",
                 "fidxf", "pidxf", "eps", "rows", "srow", "gx", "mbc", "rbc", "b1cols",
                 "qkvb", "lbcols", "gl_tok", "gscr", "gscr2", "gscr3", "gcoef", "gst",
                 "gw_sb", "gb_sb", "gbc", "s_col", "Mcol", "kstat", "bias_ld"]
        self.B = {k: Buf(k) for k in names}

        self.PA = ctx.enter_context(nc.psum_tensor("PA", [128, 2048], F32))
        self.PB = ctx.enter_context(nc.psum_tensor("PB", [128, 2048], F32))
        self.PAb = [Buf(f"PA{b}") for b in range(4)]
        self.PBb = [Buf(f"PB{b}") for b in range(4)]

        self.AB = Arena(ctx.enter_context(nc.sbuf_tensor("AB", [128, 73000], BF16)), 73000, "AB")
        self.AF = Arena(ctx.enter_context(nc.sbuf_tensor("AFA", [128, 5500], F32)), 5500, "AF")
        self.xnT = None

    def t(self, emit, reads=(), writes=(), deps=(), tag=""):
        return self.p.op("tensor", emit, reads, writes, deps, tag=tag)
    def v(self, emit, reads=(), writes=(), deps=(), tag=""):
        return self.p.op("vector", emit, reads, writes, deps, tag=tag)
    def s(self, emit, reads=(), writes=(), deps=(), tag=""):
        return self.p.op("scalar", emit, reads, writes, deps, tag=tag)
    def g(self, emit, reads=(), writes=(), deps=(), tag=""):
        return self.p.op("gpsimd", emit, reads, writes, deps, tag=tag)
    def dma(self, queue, emit, reads=(), writes=(), deps=(), tag=""):
        return self.p.op(queue, emit, reads, writes, deps, dma=True, tag=tag)
    def pa(self, bank, r0=0, nrows=128, c0=0, ncols=512):
        return self.PA[r0:r0 + nrows, bank * 512 + c0:bank * 512 + c0 + ncols]
    def pb(self, bank, r0=0, nrows=128, c0=0, ncols=512):
        return self.PB[r0:r0 + nrows, bank * 512 + c0:bank * 512 + c0 + ncols]

    def prolog_consts(self):
        B = self.B
        self.g(lambda e: e.iota(self.fidxf[:], pattern=[[1, 512]], base=0, channel_multiplier=0,
                                allow_small_or_imprecise_dtypes=True),
               writes=[B["fidxf"]], tag="iota_f")
        self.g(lambda e: e.iota(self.pidxf[:], pattern=[[0, 1]], base=0, channel_multiplier=1,
                                allow_small_or_imprecise_dtypes=True),
               writes=[B["pidxf"]], tag="iota_p")
        self.v(lambda e: e.tensor_single_scalar(out=self.identf[:], in_=self.fidxf[:, 0:128],
                                                scalar=self.pidxf[:, 0:1], op=ALU.is_equal),
               reads=[B["fidxf"], B["pidxf"]], writes=[B["identf"]], tag="ident")
        for nm, val in [("onesb", 1.0), ("onesf", 1.0), ("onescol_b", 1.0), ("onescol_f", 1.0),
                        ("eps", 1e-5)]:
            t_ = getattr(self, nm)
            self.v(lambda e, t_=t_, val=val: e.memset(t_[:], val), writes=[B[nm]], tag=f"ms_{nm}")
        if "l0a" in self.blocks:
            self.masksR = self.AB.alloc(2048, "masks", grain=2048)
            self.masks2 = lambda o: self.masksR.ap(o * 512, 512)
            mscr = self.AF.alloc(512, "mscr")
            bm = mscr.bufs[0]
            for o in range(4):
                self.v(lambda e, o=o: e.tensor_scalar(out=mscr.ap(0, 512), in0=self.fidxf[:],
                                                      scalar1=float(o * 128), scalar2=None, op0=ALU.subtract),
                       reads=[B["fidxf"], bm], writes=[bm], tag=f"mk{o}a")
                self.v(lambda e, o=o: e.tensor_scalar(
                        out=self.masks2(o), in0=mscr.ap(0, 512),
                        scalar1=self.pidxf[:, 0:1], scalar2=-1e30, op0=ALU.is_lt, op1=ALU.mult),
                       reads=[bm, B["pidxf"]], writes=[B["masks2"]], tag=f"mk{o}b")
            mscr.free()

    def prolog_input_transpose(self):
        B = self.B
        xti = self.AF.alloc(2 * 1024, "xti", grain=1024)
        xtt = self.AF.alloc(2 * 1024, "xtt", grain=1024)
        for ti in range(16):
            i = ti % 2
            dst = xti.ap(i * 1024, 1024)
            self.dma("scalar", lambda e, d=dst, ti=ti: e.dma_start(out=d, in_=self.x_in[ti * 128:(ti + 1) * 128, :]),
                     reads=xti.bufs_for(i * 1024, 1024), writes=xti.bufs_for(i * 1024, 1024), tag=f"xin{ti}")
            for dc in range(8):
                ps = self.pb(dc % 2, c0=0, ncols=128)
                self.t(lambda e, ps=ps, s=xti.ap(i * 1024 + dc * 128, 128): e.transpose(ps, s, self.identf[:]),
                       reads=xti.bufs_for(i * 1024, 1024) + [B["identf"]], writes=[self.PBb[dc % 2]],
                       tag=f"xtr{ti}.{dc}")
                self.v(lambda e, ps=ps, d=xtt.ap(i * 1024 + dc * 128, 128): e.tensor_copy(out=d, in_=ps),
                       reads=[self.PBb[dc % 2]], writes=xtt.bufs_for(i * 1024, 1024), tag=f"xev{ti}.{dc}")
            dst2 = self.XT[:, ti * 128:(ti + 1) * 128].rearrange("(c p) f -> p c f", p=128)
            src2 = xtt.ap(i * 1024, 1024).rearrange("p (c f) -> p c f", c=8)
            self.dma("scalar", lambda e, d=dst2, s=src2: e.dma_start(out=d, in_=s),
                     reads=xtt.bufs_for(i * 1024, 1024), writes=[self.XT_b[c] for c in range(8)],
                     tag=f"xst{ti}")
        xti.free(); xtt.free()

    def prolog_casts_ffn(self, L):
        names1 = {0: ("L0_ffn_ew1", "L0_ffn_sh_w1"), 1: ("L1_ffn_w1", "L1_ffn_ws1")}[L]
        names2 = {0: ("L0_ffn_ew2", "L0_ffn_sh_w2"), 1: ("L1_ffn_w2", "L1_ffn_ws2")}[L]
        for e_ in [8] + list(range(8)):
            if e_ == 8:
                s1, s2 = self.wap[names1[1]], self.wap[names2[1]]
            else:
                s1 = self.wap[names1[0]].rearrange("e d h -> (e d) h")[e_ * 1024:(e_ + 1) * 1024, :]
                s2 = self.wap[names2[0]].rearrange("e h d -> (e h) d")[e_ * 4096:(e_ + 1) * 4096, :]
            d1 = self.c1[L][e_ * 1024:(e_ + 1) * 1024, :]
            d2 = self.c2[L][e_ * 4096:(e_ + 1) * 4096, :]
            self.dma("gpsimd", lambda e, d=d1, s=s1: e.dma_start(out=d, in_=s),
                     writes=[self.c1_b[L][e_]], tag=f"cast1_{L}.{e_}")
            self.dma("gpsimd", lambda e, d=d2, s=s2: e.dma_start(out=d, in_=s),
                     writes=[self.c2_b[L][e_]], tag=f"cast2_{L}.{e_}")

    def prolog_biases(self):
        B = self.B
        ld = self.bias_ld
        jobs = []
        for L in range(2):
            n1 = {0: "L0_ffn_eb1", 1: "L1_ffn_b1"}[L]
            ns1 = {0: "L0_ffn_sh_b1", 1: "L1_ffn_bs1"}[L]
            jobs.append((self.wap[n1].rearrange("e h -> (e h)"), 32768, self.b1cols, L * 288, "b1cols"))
            jobs.append((self.wap[ns1], 4096, self.b1cols, L * 288 + 256, "b1cols"))
        jobs.append((self.wap["L1_attn_l_b"].rearrange("o p -> (o p)"), 1024, self.lbcols, 0, "lbcols"))
        for hn, i in [("L1_attn_q_b", 0), ("L1_attn_k_b", 1), ("L1_attn_v_b", 2)]:
            jobs.append((self.wap[hn].rearrange("h o p -> (h o p)"), 1024, self.qkvb, i * 8, "qkvb"))
        for (src, n, dstt, col0, dname) in jobs:
            rows = n // 128
            for blk in range((rows + 127) // 128):
                r0 = blk * 128
                nr = min(rows, r0 + 128) - r0
                sap = src[r0 * 128:(r0 + nr) * 128].rearrange("(r c) -> r c", c=128)
                dst = ld[0:nr, 0:128]
                self.dma("scalar", lambda e, d=dst, s=sap: e.dma_start(out=d, in_=s),
                         reads=[B["bias_ld"]], writes=[B["bias_ld"]], tag="biasld")
                ps = self.pa(0, nrows=128, ncols=nr)
                self.t(lambda e, d=dst, ps=ps, nr=nr: e.transpose(ps, d, self.identf[0:nr, 0:nr]),
                       reads=[B["bias_ld"], B["identf"]], writes=[self.PAb[0]], tag="biastr")
                dap = dstt[0:128, col0 + blk * 128:col0 + blk * 128 + nr]
                ps2 = self.pa(0, nrows=128, ncols=nr)
                self.v(lambda e, dap=dap, ps2=ps2: e.tensor_copy(out=dap, in_=ps2),
                       reads=[self.PAb[0]], writes=[B[dname]], tag="biasev")

    def load_gates(self, L):
        gw = self.wap[{0: "L0_ffn_gate_w", 1: "L1_ffn_gate_w"}[L]]
        gb = self.wap[{0: "L0_ffn_gate_b", 1: "L1_ffn_gate_b"}[L]]
        B = self.B
        s3 = gw.rearrange("(c p) e -> p c e", p=128)
        d3 = self.gw_sb[:].rearrange("p (c e) -> p c e", e=8)
        self.dma("scalar", lambda e, d=d3, s=s3: e.dma_start(out=d, in_=s),
                 reads=[B["gw_sb"]], writes=[B["gw_sb"]], tag=f"gwld{L}")
        self.dma("scalar", lambda e, s=gb.rearrange("(o e) -> o e", o=1): e.dma_start(out=self.gb_sb[0:1, 0:8], in_=s),
                 reads=[B["gb_sb"]], writes=[B["gb_sb"]], tag=f"gbld{L}")

    def _ln_rows_and_bcast(self, width, tag, ffn_tail=False):
        B = self.B
        nb = width // 512
        R = self.rows
        S = self.srow
        G0 = self.gx[0:1, 0:width]
        if ffn_tail:
            sq_ap, sq_bufs = self.pa(1, nrows=1, ncols=width), [self.PAb[1]]
            bcm_ap, bcm_buf = self.pa(2), self.PAb[2]
            bcr_ap, bcr_buf = self.pa(3), self.PAb[3]
        else:
            sq_ap, sq_bufs = self.PB[0:1, 0:width], self.PBb[:nb]
            bcm_ap, bcm_buf = None, None
        self.v(lambda e: e.tensor_copy(out=R[0:1, 0:width], in_=self.PA[0:1, 0:width]),
               reads=self.PAb[:nb], writes=[B["rows"]], tag=f"{tag}.evm")
        self.v(lambda e, sq_ap=sq_ap: e.tensor_copy(out=S[0:1, 0:width], in_=sq_ap),
               reads=sq_bufs, writes=[B["srow"]], tag=f"{tag}.evs")
        self.v(lambda e: e.tensor_scalar(out=R[0:1, 0:width], in0=R[0:1, 0:width],
                                         scalar1=1.0 / D, scalar2=None, op0=ALU.mult),
               reads=[B["rows"]], writes=[B["rows"]], tag=f"{tag}.m")
        self.v(lambda e: e.tensor_scalar(out=S[0:1, 0:width], in0=S[0:1, 0:width],
                                         scalar1=1.0 / D, scalar2=None, op0=ALU.mult),
               reads=[B["srow"]], writes=[B["srow"]], tag=f"{tag}.s2")
        self.s(lambda e: e.activation(out=G0, in_=R[0:1, 0:width], func=AF.Square),
               reads=[B["rows"], B["gx"]], writes=[B["gx"]], tag=f"{tag}.mm")
        self.v(lambda e: e.tensor_sub(out=S[0:1, 0:width], in0=S[0:1, 0:width], in1=G0),
               reads=[B["srow"], B["gx"]], writes=[B["srow"]], tag=f"{tag}.var")
        self.s(lambda e: e.activation(out=G0, in_=S[0:1, 0:width],
                                      func=AF.Ln, bias=self.eps[0:1, 0:1]),
               reads=[B["srow"], B["eps"], B["gx"]], writes=[B["gx"]], tag=f"{tag}.ln")
        self.s(lambda e: e.activation(out=R[64:65, 0:width], in_=G0,
                                      func=AF.Exp, scale=-0.5),
               reads=[B["gx"], B["rows"]], writes=[B["rows"]], tag=f"{tag}.exp")
        if ffn_tail:
            self.t(lambda e: e.matmul(bcm_ap, self.onesf[0:1, 0:128],
                                      R[0:1, 0:512], start=True, stop=True),
                   reads=[B["onesf"], B["rows"]], writes=[bcm_buf], tag=f"{tag}.bcm")
            self.t(lambda e: e.matmul(bcr_ap, self.onesf[64:65, 0:128],
                                      R[64:65, 0:512], start=True, stop=True),
                   reads=[B["onesf"], B["rows"]], writes=[bcr_buf], tag=f"{tag}.bcr")
            self.v(lambda e: e.tensor_copy(out=self.mbc[:, 0:width], in_=bcm_ap),
                   reads=[bcm_buf], writes=[B["mbc"]], tag=f"{tag}.evbm")
            self.v(lambda e: e.tensor_copy(out=self.rbc[:, 0:width], in_=bcr_ap),
                   reads=[bcr_buf], writes=[B["rbc"]], tag=f"{tag}.evbr")
            return
        for b in range(nb):
            self.t(lambda e, b=b: e.matmul(self.pa(b), self.onesf[0:1, 0:128],
                                           R[0:1, b * 512:(b + 1) * 512], start=True, stop=True),
                   reads=[B["onesf"], B["rows"]], writes=[self.PAb[b]], tag=f"{tag}.bcm{b}")
            self.t(lambda e, b=b: e.matmul(self.pb(b), self.onesf[64:65, 0:128],
                                           R[64:65, b * 512:(b + 1) * 512], start=True, stop=True),
                   reads=[B["onesf"], B["rows"]], writes=[self.PBb[b]], tag=f"{tag}.bcr{b}")
        self.v(lambda e: e.tensor_copy(out=self.mbc[:, 0:width], in_=self.PA[0:128, 0:width]),
               reads=self.PAb[:nb], writes=[B["mbc"]], tag=f"{tag}.evbm")
        self.v(lambda e: e.tensor_copy(out=self.rbc[:, 0:width], in_=self.PB[0:128, 0:width]),
               reads=self.PBb[:nb], writes=[B["rbc"]], tag=f"{tag}.evbr")

    def pre_ln(self, L, gate, tag):
        B = self.B
        xnT = self.AB.alloc(16384, f"xnT_{tag}", grain=2048)
        xf = self.AF.alloc(2 * 2048, f"xf_{tag}", grain=2048)
        xc = self.AB.alloc(2 * 2048, f"xc_{tag}", grain=2048)
        xq = self.AB.alloc(2 * 2048, f"xq_{tag}", grain=2048)
        if gate:
            self.load_gates(L)
        for dc in range(8):
            i = dc % 2
            xfa = xf.ap(i * 2048, 2048)
            self.dma("scalar", lambda e, d=xfa, dc=dc: e.dma_start(out=d, in_=self.XT[dc * 128:(dc + 1) * 128, :]),
                     reads=[self.XT_b[dc]] + xf.bufs_for(i * 2048, 2048),
                     writes=xf.bufs_for(i * 2048, 2048), tag=f"{tag}.ldA{dc}")
            xca = xc.ap(i * 2048, 2048)
            self.v(lambda e, s=xfa, d=xca: e.tensor_copy(out=d, in_=s),
                   reads=xf.bufs_for(i * 2048, 2048), writes=xc.bufs_for(i * 2048, 2048), tag=f"{tag}.cst{dc}")
            xqa = xq.ap(i * 2048, 2048)
            self.s(lambda e, s=xca, d=xqa: e.activation(out=d, in_=s, func=AF.Square),
                   reads=xc.bufs_for(i * 2048, 2048), writes=xq.bufs_for(i * 2048, 2048), tag=f"{tag}.sq{dc}")
            for b in range(4):
                self.t(lambda e, b=b, a=xc.ap(i * 2048 + b * 512, 512), dc=dc:
                       e.matmul(self.pa(b, nrows=1), self.onescol_b[:], a, start=(dc == 0), stop=(dc == 7)),
                       reads=xc.bufs_for(i * 2048, 2048) + [B["onescol_b"]], writes=[self.PAb[b]],
                       tag=f"{tag}.mv{dc}.{b}")
                self.t(lambda e, b=b, a=xq.ap(i * 2048 + b * 512, 512), dc=dc:
                       e.matmul(self.pb(b, nrows=1), self.onescol_b[:], a, start=(dc == 0), stop=(dc == 7)),
                       reads=xq.bufs_for(i * 2048, 2048) + [B["onescol_b"]], writes=[self.PBb[b]],
                       tag=f"{tag}.sv{dc}.{b}")
        self._ln_rows_and_bcast(2048, tag)
        for dc in range(8):
            i = dc % 2
            xfa = xf.ap(i * 2048, 2048)
            self.dma("scalar", lambda e, d=xfa, dc=dc: e.dma_start(out=d, in_=self.XT[dc * 128:(dc + 1) * 128, :]),
                     reads=[self.XT_b[dc]] + xf.bufs_for(i * 2048, 2048),
                     writes=xf.bufs_for(i * 2048, 2048), tag=f"{tag}.ldB{dc}")
            if gate:
                for b in range(4):
                    self.t(lambda e, b=b, a=xf.ap(i * 2048 + b * 512, 512), dc=dc:
                           e.matmul(self.pa(b, nrows=8), self.gw_sb[:, dc * 8:(dc + 1) * 8], a,
                                    start=(dc == 0), stop=(dc == 7)),
                           reads=xf.bufs_for(i * 2048, 2048) + [B["gw_sb"]], writes=[self.PAb[b]],
                           tag=f"{tag}.gr{dc}.{b}")
            self.v(lambda e, d=xfa: e.tensor_sub(out=d, in0=d, in1=self.mbc[:]),
                   reads=xf.bufs_for(i * 2048, 2048) + [B["mbc"]], writes=xf.bufs_for(i * 2048, 2048),
                   tag=f"{tag}.a1{dc}")
            xna = xnT.ap(dc * 2048, 2048)
            self.v(lambda e, s=xfa, d=xna: e.tensor_mul(out=d, in0=s, in1=self.rbc[:]),
                   reads=xf.bufs_for(i * 2048, 2048) + [B["rbc"]], writes=xnT.bufs_for(dc * 2048, 2048),
                   tag=f"{tag}.a2{dc}")
        if gate:
            self.t(lambda e: e.transpose(self.PB[0:8, 0:1], self.gb_sb[0:1, 0:8], self.identf[0:1, 0:1]),
                   reads=[B["gb_sb"], B["identf"]], writes=[self.PBb[0]], tag=f"{tag}.gbt")
            self.v(lambda e: e.tensor_copy(out=self.gbc[32:40, 0:1], in_=self.PB[0:8, 0:1]),
                   reads=[self.PBb[0]], writes=[B["gbc"]], tag=f"{tag}.gbc")
            for dc in range(8):
                self.t(lambda e, dc=dc: e.matmul(self.pb(0, nrows=8, ncols=1),
                                                 self.gw_sb[:, dc * 8:(dc + 1) * 8], self.onescol_f[:],
                                                 start=(dc == 0), stop=(dc == 7)),
                       reads=[B["gw_sb"], B["onescol_f"]], writes=[self.PBb[0]], tag=f"{tag}.sc{dc}")
            self._gating_finalize(L, tag)
        xf.free(); xc.free(); xq.free()
        self.xnT = xnT
        return xnT

    def _gating_finalize(self, L, tag):
        B = self.B
        GX = self.gx
        self.v(lambda e: e.tensor_copy(out=GX[0:8, :], in_=self.PA[0:8, 0:2048]),
               reads=self.PAb, writes=[B["gx"]], tag=f"{tag}.gev")
        self.v(lambda e: e.tensor_copy(out=self.s_col[:], in_=self.PB[0:8, 0:1]),
               reads=[self.PBb[0]], writes=[B["s_col"]], tag=f"{tag}.scv")
        for b in range(4):
            self.t(lambda e, b=b: e.matmul(self.pb(b, nrows=8), self.onesf[0:1, 0:8],
                                           self.rows[0:1, b * 512:(b + 1) * 512], start=True, stop=True),
                   reads=[B["onesf"], B["rows"]], writes=[self.PBb[b]], tag=f"{tag}.m8{b}")
        self.v(lambda e: e.scalar_tensor_tensor(out=GX[32:40, :], in0=self.PB[0:8, 0:2048],
                                                scalar=self.s_col[0:8, 0:1], in1=GX[0:8, :],
                                                op0=ALU.mult, op1=ALU.subtract),
               reads=self.PBb + [B["s_col"], B["gx"]], writes=[B["gx"]], tag=f"{tag}.gms")
        for b in range(4):
            self.t(lambda e, b=b: e.matmul(self.pb(b, nrows=8), self.onesf[64:65, 0:8],
                                           self.rows[64:65, b * 512:(b + 1) * 512], start=True, stop=True),
                   reads=[B["onesf"], B["rows"]], writes=[self.PBb[b]], tag=f"{tag}.r8{b}")
        self.v(lambda e: e.tensor_mul(out=GX[32:40, :], in0=GX[32:40, :], in1=self.PB[0:8, 0:2048]),
               reads=[B["gx"]] + self.PBb, writes=[B["gx"]], tag=f"{tag}.gfin")
        self.v(lambda e: e.tensor_scalar(out=GX[32:40, :], in0=GX[32:40, :],
                                         scalar1=self.gbc[32:40, 0:1], scalar2=None, op0=ALU.subtract),
               reads=[B["gx"], B["gbc"]], writes=[B["gx"]], tag=f"{tag}.gbias")
        for ti in range(16):
            self.t(lambda e, ti=ti: e.transpose(self.PA[0:128, ti * 8:ti * 8 + 8],
                                                GX[32:40, ti * 128:(ti + 1) * 128], self.identf[32:40, 32:40]),
                   reads=[B["gx"], B["identf"]], writes=[self.PAb[0]], tag=f"{tag}.gt{ti}")
        self.v(lambda e: e.tensor_copy(out=self.gl_tok[:], in_=self.PA[0:128, 0:128]),
               reads=[self.PAb[0]], writes=[B["gl_tok"]], tag=f"{tag}.gtv")
        self.v(lambda e: e.tensor_scalar(out=self.gl_tok[:], in0=self.gl_tok[:],
                                         scalar1=-1.0, scalar2=None, op0=ALU.mult),
               reads=[B["gl_tok"]], writes=[B["gl_tok"]], tag=f"{tag}.gneg")
        G3 = self.gl_tok[:].rearrange("p (t e) -> p t e", e=8)
        gst = self.gst
        if L == 0:
            self.v(lambda e: e.reduce_max(out=gst[:, 0:16], in_=G3, axis=AX.X),
                   reads=[B["gl_tok"]], writes=[B["gst"]], tag=f"{tag}.mx")
            for ti in range(16):
                self.v(lambda e, ti=ti: e.tensor_scalar(out=self.gscr[:, ti * 8:ti * 8 + 8],
                                                        in0=self.gl_tok[:, ti * 8:ti * 8 + 8],
                                                        scalar1=gst[:, ti:ti + 1], scalar2=None, op0=ALU.subtract),
                       reads=[B["gl_tok"], B["gst"]], writes=[B["gscr"]], tag=f"{tag}.sb{ti}")
            self.s(lambda e: e.activation(out=self.gscr[:], in_=self.gscr[:], func=AF.Exp),
                   reads=[B["gscr"]], writes=[B["gscr"]], tag=f"{tag}.ex")
            GS = self.gscr[:].rearrange("p (t e) -> p t e", e=8)
            self.v(lambda e: e.reduce_sum(out=gst[:, 16:32], in_=GS, axis=AX.X),
                   reads=[B["gscr"]], writes=[B["gst"]], tag=f"{tag}.sm")
            self.v(lambda e: e.reciprocal(out=gst[:, 16:32], in_=gst[:, 16:32]),
                   reads=[B["gst"]], writes=[B["gst"]], tag=f"{tag}.rc")
            for ti in range(16):
                self.v(lambda e, ti=ti: e.tensor_scalar(out=self.gscr[:, ti * 8:ti * 8 + 8],
                                                        in0=self.gscr[:, ti * 8:ti * 8 + 8],
                                                        scalar1=gst[:, 16 + ti:17 + ti], scalar2=None, op0=ALU.mult),
                       reads=[B["gscr"], B["gst"]], writes=[B["gscr"]], tag=f"{tag}.pr{ti}")
            self.v(lambda e: e.reduce_max(out=gst[:, 0:16], in_=GS, axis=AX.X),
                   reads=[B["gscr"]], writes=[B["gst"]], tag=f"{tag}.v1")
            for ti in range(16):
                self.v(lambda e, ti=ti: e.tensor_scalar(out=self.gscr2[:, ti * 8:ti * 8 + 8],
                                                        in0=self.gscr[:, ti * 8:ti * 8 + 8],
                                                        scalar1=gst[:, ti:ti + 1], scalar2=None, op0=ALU.is_equal),
                       reads=[B["gscr"], B["gst"]], writes=[B["gscr2"]], tag=f"{tag}.e1{ti}")
            self.v(lambda e: e.scalar_tensor_tensor(out=self.gscr3[:], in0=self.gscr2[:], scalar=-1e30,
                                                    in1=self.gscr[:], op0=ALU.mult, op1=ALU.add),
                   reads=[B["gscr2"], B["gscr"]], writes=[B["gscr3"]], tag=f"{tag}.p2")
            GS3 = self.gscr3[:].rearrange("p (t e) -> p t e", e=8)
            self.v(lambda e: e.reduce_max(out=gst[:, 16:32], in_=GS3, axis=AX.X),
                   reads=[B["gscr3"]], writes=[B["gst"]], tag=f"{tag}.v2")
            for ti in range(16):
                self.v(lambda e, ti=ti: e.tensor_scalar(out=self.gscr3[:, ti * 8:ti * 8 + 8],
                                                        in0=self.gscr3[:, ti * 8:ti * 8 + 8],
                                                        scalar1=gst[:, 16 + ti:17 + ti], scalar2=None, op0=ALU.is_equal),
                       reads=[B["gscr3"], B["gst"]], writes=[B["gscr3"]], tag=f"{tag}.e2{ti}")
            self.v(lambda e: e.tensor_add(out=gst[:, 32:48], in0=gst[:, 0:16], in1=gst[:, 16:32]),
                   reads=[B["gst"]], writes=[B["gst"]], tag=f"{tag}.dn")
            self.v(lambda e: e.tensor_scalar(out=gst[:, 32:48], in0=gst[:, 32:48], scalar1=1e-9,
                                             scalar2=None, op0=ALU.add),
                   reads=[B["gst"]], writes=[B["gst"]], tag=f"{tag}.dne")
            self.v(lambda e: e.reciprocal(out=gst[:, 32:48], in_=gst[:, 32:48]),
                   reads=[B["gst"]], writes=[B["gst"]], tag=f"{tag}.dnr")
            self.v(lambda e: e.tensor_add(out=self.gscr2[:], in0=self.gscr2[:], in1=self.gscr3[:]),
                   reads=[B["gscr2"], B["gscr3"]], writes=[B["gscr2"]], tag=f"{tag}.mu")
            self.v(lambda e: e.tensor_mul(out=self.gscr[:], in0=self.gscr[:], in1=self.gscr2[:]),
                   reads=[B["gscr"], B["gscr2"]], writes=[B["gscr"]], tag=f"{tag}.pw")
            for ti in range(16):
                self.v(lambda e, ti=ti: e.tensor_scalar(out=self.gcoef[:, ti * 8:ti * 8 + 8],
                                                        in0=self.gscr[:, ti * 8:ti * 8 + 8],
                                                        scalar1=gst[:, 32 + ti:33 + ti], scalar2=None, op0=ALU.mult),
                       reads=[B["gscr"], B["gst"]], writes=[B["gcoef"]], tag=f"{tag}.wE{ti}")
        else:
            self.v(lambda e: e.reduce_max(out=gst[:, 0:16], in_=G3, axis=AX.X),
                   reads=[B["gl_tok"]], writes=[B["gst"]], tag=f"{tag}.v1")
            for ti in range(16):
                self.v(lambda e, ti=ti: e.tensor_scalar(out=self.gscr[:, ti * 8:ti * 8 + 8],
                                                        in0=self.gl_tok[:, ti * 8:ti * 8 + 8],
                                                        scalar1=gst[:, ti:ti + 1], scalar2=None, op0=ALU.is_equal),
                       reads=[B["gl_tok"], B["gst"]], writes=[B["gscr"]], tag=f"{tag}.e1{ti}")
            self.v(lambda e: e.scalar_tensor_tensor(out=self.gscr2[:], in0=self.gscr[:], scalar=-1e30,
                                                    in1=self.gl_tok[:], op0=ALU.mult, op1=ALU.add),
                   reads=[B["gscr"], B["gl_tok"]], writes=[B["gscr2"]], tag=f"{tag}.p2")
            GS2 = self.gscr2[:].rearrange("p (t e) -> p t e", e=8)
            self.v(lambda e: e.reduce_max(out=gst[:, 16:32], in_=GS2, axis=AX.X),
                   reads=[B["gscr2"]], writes=[B["gst"]], tag=f"{tag}.v2")
            for ti in range(16):
                self.v(lambda e, ti=ti: e.tensor_scalar(out=self.gscr3[:, ti * 8:ti * 8 + 8],
                                                        in0=self.gscr2[:, ti * 8:ti * 8 + 8],
                                                        scalar1=gst[:, 16 + ti:17 + ti], scalar2=None, op0=ALU.is_equal),
                       reads=[B["gscr2"], B["gst"]], writes=[B["gscr3"]], tag=f"{tag}.e2{ti}")
            self.v(lambda e: e.tensor_sub(out=gst[:, 32:48], in0=gst[:, 0:16], in1=gst[:, 16:32]),
                   reads=[B["gst"]], writes=[B["gst"]], tag=f"{tag}.d12")
            self.s(lambda e: e.activation(out=gst[:, 48:64], in_=gst[:, 32:48], func=AF.Sigmoid),
                   reads=[B["gst"]], writes=[B["gst"]], tag=f"{tag}.sg")
            self.v(lambda e: e.tensor_scalar(out=gst[:, 32:48], in0=gst[:, 48:64], scalar1=-1.0,
                                             scalar2=1.0, op0=ALU.mult, op1=ALU.add),
                   reads=[B["gst"]], writes=[B["gst"]], tag=f"{tag}.g2")
            for ti in range(16):
                self.v(lambda e, ti=ti: e.tensor_scalar(out=self.gcoef[:, ti * 8:ti * 8 + 8],
                                                        in0=self.gscr[:, ti * 8:ti * 8 + 8],
                                                        scalar1=gst[:, 48 + ti:49 + ti], scalar2=None, op0=ALU.mult),
                       reads=[B["gscr"], B["gst"]], writes=[B["gcoef"]], tag=f"{tag}.c1{ti}")
                self.v(lambda e, ti=ti: e.scalar_tensor_tensor(out=self.gcoef[:, ti * 8:ti * 8 + 8],
                                                               in0=self.gscr3[:, ti * 8:ti * 8 + 8],
                                                               scalar=gst[:, 32 + ti:33 + ti],
                                                               in1=self.gcoef[:, ti * 8:ti * 8 + 8],
                                                               op0=ALU.mult, op1=ALU.add),
                       reads=[B["gscr3"], B["gst"], B["gcoef"]], writes=[B["gcoef"]], tag=f"{tag}.c2{ti}")

    def ffn(self, L, tag):
        B = self.B
        xnT = self.xnT
        h = self.AB.alloc(16384, f"h_{tag}", grain=512)
        w1q = self.AB.alloc(2 * 4096, f"w1q_{tag}", grain=4096)
        w2q = self.AB.alloc(2 * 4096, f"w2q_{tag}", grain=4096)
        fT = self.AB.alloc(4096, f"fT_{tag}", grain=512)
        fq = self.AB.alloc(2 * 512, f"fq_{tag}", grain=512)
        b2reg = self.AB.alloc(9216, f"b2_{tag}", grain=9216)
        acc = self.AF.alloc(4096, f"acc_{tag}", grain=512)
        tl1 = self.AF.alloc(2 * 512, f"tl1_{tag}", grain=512)
        xo2 = self.AB.alloc(2 * 1024, f"xo2_{tag}", grain=1024)
        b2e = self.wap[{0: "L0_ffn_eb2", 1: "L1_ffn_b2"}[L]].rearrange("e d -> (e d)").rearrange("(o n) -> o n", o=1)
        b2s = self.wap[{0: "L0_ffn_sh_b2", 1: "L1_ffn_bs2"}[L]].rearrange("(o n) -> o n", o=1)
        self.dma("gpsimd", lambda e: e.dma_start(out=b2reg.ap(0, 8192, nrows=1), in_=b2e),
                 reads=b2reg.bufs, writes=b2reg.bufs, tag=f"{tag}.b2e")
        self.dma("gpsimd", lambda e: e.dma_start(out=b2reg.ap(8192, 1024, nrows=1), in_=b2s),
                 reads=b2reg.bufs, writes=b2reg.bufs, tag=f"{tag}.b2s")

        experts = [8] + list(range(8))
        loadseq = []
        for sc in range(4):
            for e_ in experts:
                for q8 in range(8):
                    loadseq.append(("w1", e_, q8))
                for q8 in range(8):
                    loadseq.append(("w2", e_, q8))
        li = [0]

        def plan_load():
            if li[0] >= len(loadseq):
                return
            kind, e_, q8 = loadseq[li[0]]
            li[0] += 1
            par = q8 % 2
            if kind == "w1":
                dst = w1q.ap(par * 4096, 4096).rearrange("p (c n) -> p c n", c=8)
                src = self.c1[L][e_ * 1024:(e_ + 1) * 1024, q8 * 512:(q8 + 1) * 512] \
                    .rearrange("(c p) n -> p c n", p=128)
                bufs, cb = w1q.bufs_for(par * 4096, 4096), self.c1_b[L][e_]
            else:
                dst = w2q.ap(par * 4096, 4096).rearrange("p (c n) -> p c n", c=4)
                src = self.c2[L][e_ * 4096 + q8 * 512:e_ * 4096 + (q8 + 1) * 512, :] \
                    .rearrange("(c p) n -> p c n", p=128)
                bufs, cb = w2q.bufs_for(par * 4096, 4096), self.c2_b[L][e_]
            self.dma("sync", lambda e, d=dst, s=src: e.dma_start(out=d, in_=s),
                     reads=[cb] + bufs, writes=bufs, tag=f"{tag}.{kind}ld{e_}.{q8}")

        plan_load(); plan_load()
        for sc in range(4):
            for e_ in experts:
                for q8 in range(8):
                    par = q8 % 2
                    for j in range(4):
                        hc = q8 * 4 + j
                        ps, pbuf = self.pb(hc % 2), self.PBb[hc % 2]
                        for dc in range(8):
                            lh = w1q.ap(par * 4096 + dc * 512 + j * 128, 128)
                            rh = xnT.ap(dc * 2048 + sc * 512, 512)
                            self.t(lambda e, ps=ps, lh=lh, rh=rh, dc=dc:
                                   e.matmul(ps, lh, rh, start=(dc == 0), stop=(dc == 7)),
                                   reads=w1q.bufs_for(par * 4096, 4096) + xnT.bufs_for(dc * 2048, 2048),
                                   writes=[pbuf], tag=f"{tag}.up{sc}.{e_}.{hc}.{dc}")
                        bcol = L * 288 + (256 if e_ == 8 else e_ * 32) + hc
                        ha = h.ap(hc * 512, 512)
                        self.s(lambda e, ps=ps, ha=ha, bcol=bcol:
                               e.activation(out=ha, in_=ps, func=GELU_L[L],
                                            bias=self.b1cols[:, bcol:bcol + 1]),
                               reads=[pbuf, B["b1cols"]], writes=h.bufs_for(hc * 512, 512),
                               tag=f"{tag}.ge{sc}.{e_}.{hc}")
                    plan_load()
                banks = [(self.pa(k), self.PAb[k]) for k in range(4)] + \
                        [(self.pb(k), self.PBb[k]) for k in range(4)]
                for k in range(8):
                    tsub, dh = k // 2, k % 2
                    ps, pbuf = banks[k]
                    c0 = (8192 if e_ == 8 else e_ * 1024) + dh * 512
                    self.t(lambda e, ps=ps, brow=b2reg.ap(c0, 512, nrows=1):
                           e.matmul(ps, self.onesb[0:1, 0:128], brow, start=True, stop=False),
                           reads=[B["onesb"]] + b2reg.bufs, writes=[pbuf], tag=f"{tag}.au{sc}.{e_}.{k}")
                for q8 in range(8):
                    par = q8 % 2
                    for j in range(4):
                        hc = q8 * 4 + j
                        for k in range(8):
                            tsub, dh = k // 2, k % 2
                            ps, pbuf = banks[k]
                            lh = h.ap(hc * 512 + tsub * 128, 128)
                            rh = w2q.ap(par * 4096 + j * 1024 + dh * 512, 512)
                            self.t(lambda e, ps=ps, lh=lh, rh=rh, last=(q8 == 7 and j == 3):
                                   e.matmul(ps, lh, rh, start=False, stop=last),
                                   reads=h.bufs_for(hc * 512, 512) + w2q.bufs_for(par * 4096, 4096),
                                   writes=[pbuf], tag=f"{tag}.dn{sc}.{e_}.{hc}.{k}")
                    plan_load()
                for k in range(8):
                    tsub, dh = k // 2, k % 2
                    ps, pbuf = banks[k]
                    aap = acc.ap(tsub * 1024 + dh * 512, 512)
                    abufs = acc.bufs_for(tsub * 1024 + dh * 512, 512)
                    if e_ == 8:
                        self.v(lambda e, ps=ps, aap=aap: e.tensor_copy(out=aap, in_=ps),
                               reads=[pbuf], writes=abufs, tag=f"{tag}.e8{sc}.{k}")
                    else:
                        col = (sc * 4 + tsub) * 8 + e_
                        self.v(lambda e, ps=ps, aap=aap, col=col:
                               e.scalar_tensor_tensor(out=aap, in0=ps, scalar=self.gcoef[:, col:col + 1],
                                                      in1=aap, op0=ALU.mult, op1=ALU.add),
                               reads=[pbuf, B["gcoef"]] + abufs, writes=abufs,
                               tag=f"{tag}.ev{sc}.{e_}.{k}")
            ctr = 0
            for tsub in range(4):
                for dc in range(8):
                    slot = ctr % 8; ctr += 1
                    bank = 2 + slot // 4
                    ps = self.pa(bank, c0=(slot % 4) * 128, ncols=128)
                    pbuf = self.PAb[bank]
                    src = acc.ap(tsub * 1024 + dc * 128, 128)
                    self.t(lambda e, ps=ps, src=src: e.transpose(ps, src, self.identf[:]),
                           reads=acc.bufs_for(tsub * 1024 + dc * 128, 128) + [B["identf"]],
                           writes=[pbuf], tag=f"{tag}.tr{sc}.{tsub}.{dc}")
                    fap = fT.ap(dc * 512 + tsub * 128, 128)
                    fbufs = fT.bufs_for(dc * 512, 512)
                    if L == 0:
                        xs = xnT.ap(dc * 2048 + sc * 512 + tsub * 128, 128)
                        self.v(lambda e, ps=ps, fap=fap, xs=xs: e.tensor_add(out=fap, in0=ps, in1=xs),
                               reads=[pbuf] + xnT.bufs_for(dc * 2048, 2048), writes=fbufs,
                               tag=f"{tag}.fx{sc}.{tsub}.{dc}")
                    else:
                        self.v(lambda e, ps=ps, fap=fap: e.tensor_copy(out=fap, in_=ps),
                               reads=[pbuf], writes=fbufs, tag=f"{tag}.fx{sc}.{tsub}.{dc}")
            for dc in range(8):
                i = dc % 2
                fqa = fq.ap(i * 512, 512)
                self.s(lambda e, s=fT.ap(dc * 512, 512), d=fqa: e.activation(out=d, in_=s, func=AF.Square),
                       reads=fT.bufs_for(dc * 512, 512), writes=fq.bufs_for(i * 512, 512),
                       tag=f"{tag}.fsq{sc}.{dc}")
                self.t(lambda e, a=fT.ap(dc * 512, 512), dc=dc:
                       e.matmul(self.pa(0, nrows=1), self.onescol_b[:], a, start=(dc == 0), stop=(dc == 7)),
                       reads=fT.bufs_for(dc * 512, 512) + [B["onescol_b"]], writes=[self.PAb[0]],
                       tag=f"{tag}.tm{sc}.{dc}")
                self.t(lambda e, a=fqa, dc=dc:
                       e.matmul(self.pa(1, nrows=1), self.onescol_b[:], a, start=(dc == 0), stop=(dc == 7)),
                       reads=fq.bufs_for(i * 512, 512) + [B["onescol_b"]], writes=[self.PAb[1]],
                       tag=f"{tag}.ts{sc}.{dc}")
            self._ln_rows_and_bcast(512, f"{tag}.t{sc}", ffn_tail=True)
            for dc in range(8):
                i = dc % 2
                t1 = tl1.ap(i * 512, 512)
                xoa = xo2.ap(i * 1024, 1024).bitcast(F32)
                self.dma("scalar", lambda e, d=xoa, dc=dc, sc=sc:
                         e.dma_start(out=d, in_=self.XT[dc * 128:(dc + 1) * 128, sc * 512:(sc + 1) * 512]),
                         reads=[self.XT_b[dc]] + xo2.bufs_for(i * 1024, 1024),
                         writes=xo2.bufs_for(i * 1024, 1024), tag=f"{tag}.xl{sc}.{dc}")
                self.v(lambda e, s=fT.ap(dc * 512, 512), d=t1: e.tensor_sub(out=d, in0=s, in1=self.mbc[:, 0:512]),
                       reads=fT.bufs_for(dc * 512, 512) + [B["mbc"]] + tl1.bufs_for(i * 512, 512),
                       writes=tl1.bufs_for(i * 512, 512), tag=f"{tag}.l1{sc}.{dc}")
                self.v(lambda e, d=t1: e.tensor_mul(out=d, in0=d, in1=self.rbc[:, 0:512]),
                       reads=tl1.bufs_for(i * 512, 512) + [B["rbc"]], writes=tl1.bufs_for(i * 512, 512),
                       tag=f"{tag}.l2{sc}.{dc}")
                self.v(lambda e, d=t1, xoa=xoa: e.tensor_add(out=d, in0=d, in1=xoa),
                       reads=tl1.bufs_for(i * 512, 512) + xo2.bufs_for(i * 1024, 1024),
                       writes=tl1.bufs_for(i * 512, 512), tag=f"{tag}.l3{sc}.{dc}")
                dst = self.XT[dc * 128:(dc + 1) * 128, sc * 512:(sc + 1) * 512]
                self.dma("scalar", lambda e, d=dst, s=t1: e.dma_start(out=d, in_=s),
                         reads=tl1.bufs_for(i * 512, 512) + [self.XT_b[dc]], writes=[self.XT_b[dc]],
                         tag=f"{tag}.st{sc}.{dc}")
        for r in (h, w1q, w2q, fT, fq, b2reg, acc, tl1, xo2):
            r.free()
        xnT.free()
        self.xnT = None

    def load_l0_attn_weights(self):
        self.wcd = self.AB.alloc(4096, "wcd", grain=4096)
        self.wqu = self.AB.alloc(4096, "wqu", grain=4096)
        self.wkv = self.AB.alloc(512, "wkv", grain=512)
        self.wcp = self.AB.alloc(8192, "wcp", grain=8192)
        for (reg, nm, cdim) in [(self.wcd, "L0_attn_Wc_down", 512), (self.wqu, "L0_attn_Wq_up", 1024),
                                (self.wkv, "L0_attn_Wkv_up", 128), (self.wcp, "L0_attn_Wc_proj", 1024)]:
            nchunk = self.wap[nm].tensor.shape[0] // 128
            src = self.wap[nm].rearrange("(c p) n -> p c n", p=128)
            dst = reg.ap(0, nchunk * cdim).rearrange("p (c n) -> p c n", c=nchunk)
            self.dma("gpsimd", lambda e, d=dst, s=src: e.dma_start(out=d, in_=s),
                     writes=reg.bufs, tag=f"ld_{nm}")

    def l0_attn(self, tag="l0a"):
        B = self.B
        xnT = self.xnT
        lat = self.AB.alloc(8192, f"lat_{tag}", grain=2048)
        for lc in range(4):
            for tb in range(4):
                ps, pbuf = self.pa(tb), self.PAb[tb]
                for dc in range(8):
                    lh = self.wcd.ap(dc * 512 + lc * 128, 128)
                    rh = xnT.ap(dc * 2048 + tb * 512, 512)
                    self.t(lambda e, ps=ps, lh=lh, rh=rh, dc=dc:
                           e.matmul(ps, lh, rh, start=(dc == 0), stop=(dc == 7)),
                           reads=self.wcd.bufs + xnT.bufs_for(dc * 2048, 2048), writes=[pbuf],
                           tag=f"{tag}.lat{lc}.{tb}.{dc}")
                dst = lat.ap(lc * 2048 + tb * 512, 512)
                self.s(lambda e, ps=ps, d=dst: e.copy(out=d, in_=ps),
                       reads=[pbuf], writes=lat.bufs_for(lc * 2048, 2048), tag=f"{tag}.le{lc}.{tb}")
        xnT.free()
        self.xnT = None
        kTaug = self.AB.alloc(2048, f"kta_{tag}", grain=2048)
        vaug = self.AB.alloc(1040, f"vau_{tag}", grain=1040)
        qTf = self.AB.alloc(16384, f"qTf_{tag}", grain=2048)
        yscr = self.AF.alloc(2048, f"ys_{tag}", grain=2048)
        for tb in range(4):
            ps, pbuf = self.pa(tb, nrows=64), self.PAb[tb]
            for lc in range(4):
                lh = self.wkv.ap(lc * 128, 64)
                rh = lat.ap(lc * 2048 + tb * 512, 512)
                self.t(lambda e, ps=ps, lh=lh, rh=rh, lc=lc:
                       e.matmul(ps, lh, rh, start=(lc == 0), stop=(lc == 3)),
                       reads=self.wkv.bufs + lat.bufs_for(lc * 2048, 2048), writes=[pbuf],
                       tag=f"{tag}.kt{tb}.{lc}")
            self.s(lambda e, ps=ps, d=kTaug.ap(tb * 512, 512, r0=0, nrows=64): e.copy(out=d, in_=ps),
                   reads=[pbuf], writes=kTaug.bufs, tag=f"{tag}.kte{tb}")
        self.v(lambda e: e.memset(kTaug.ap(0, 2048, r0=64, nrows=1), -1.0),
               reads=kTaug.bufs, writes=kTaug.bufs, tag=f"{tag}.km1")
        for si in range(16):
            ps, pbuf = self.pa(si % 4, ncols=64), self.PAb[si % 4]
            for lc in range(4):
                lh = lat.ap(lc * 2048 + si * 128, 128)
                rh = self.wkv.ap(lc * 128 + 64, 64)
                self.t(lambda e, ps=ps, lh=lh, rh=rh, lc=lc:
                       e.matmul(ps, lh, rh, start=(lc == 0), stop=(lc == 3)),
                       reads=self.wkv.bufs + lat.bufs_for(lc * 2048, 2048), writes=[pbuf],
                       tag=f"{tag}.v{si}.{lc}")
            self.s(lambda e, ps=ps, d=vaug.ap(si * 65, 64): e.copy(out=d, in_=ps),
                   reads=[pbuf], writes=vaug.bufs, tag=f"{tag}.ve{si}")
            self.v(lambda e, si=si: e.memset(vaug.ap(si * 65 + 64, 1), 1.0),
                   reads=vaug.bufs, writes=vaug.bufs, tag=f"{tag}.vm{si}")
        for hp in range(8):
            for tb in range(4):
                k8 = (hp * 4 + tb) % 8
                ps = self.pa(k8) if k8 < 4 else self.pb(k8 - 4)
                pbuf = self.PAb[k8] if k8 < 4 else self.PBb[k8 - 4]
                for lc in range(4):
                    lh = self.wqu.ap(lc * 1024 + hp * 128, 128)
                    rh = lat.ap(lc * 2048 + tb * 512, 512)
                    self.t(lambda e, ps=ps, lh=lh, rh=rh, lc=lc:
                           e.matmul(ps, lh, rh, start=(lc == 0), stop=(lc == 3)),
                           reads=self.wqu.bufs + lat.bufs_for(lc * 2048, 2048), writes=[pbuf],
                           tag=f"{tag}.q{hp}.{tb}")
                dst = qTf.ap(hp * 2048 + tb * 512, 512)
                self.s(lambda e, ps=ps, d=dst: e.activation(out=d, in_=ps, func=AF.Copy, scale=0.125),
                       reads=[pbuf], writes=qTf.bufs_for(hp * 2048, 2048), tag=f"{tag}.qe{hp}.{tb}")
        for r in (lat, self.wcd, self.wqu, self.wkv):
            r.free()
        qTaug = self.AB.alloc(2048, f"qta_{tag}", grain=2048)
        Ap = [self.AB.alloc(2048, f"A{i}_{tag}", grain=2048) for i in range(2)]
        yallT = self.AB.alloc(16384, f"ya_{tag}", grain=2048)
        sqs = self.AB.alloc(2048, f"sqs_{tag}", grain=2048)
        self.s(lambda e: e.activation(out=sqs.ap(0, 2048, r0=0, nrows=64),
                                      in_=kTaug.ap(0, 2048, r0=0, nrows=64), func=AF.Square),
               reads=kTaug.bufs + sqs.bufs, writes=sqs.bufs, tag=f"{tag}.ksq")
        for tb in range(4):
            self.t(lambda e, tb=tb: e.matmul(self.pb(tb, nrows=1), self.onescol_b[0:64, 0:1],
                                             sqs.ap(tb * 512, 512, r0=0, nrows=64), start=True, stop=True),
                   reads=sqs.bufs + [B["onescol_b"]], writes=[self.PBb[tb]], tag=f"{tag}.knm{tb}")
        self.v(lambda e: e.tensor_copy(out=self.gx[0:1, 0:2048], in_=self.PB[0:1, 0:2048]),
               reads=self.PBb + [B["gx"]], writes=[B["gx"]], tag=f"{tag}.knr")
        self.v(lambda e: e.reduce_max(out=self.gx[0:1, 2048 - 1:2048] if False else self.kstat[0:1, 0:1],
                                      in_=self.gx[0:1, 0:2048], axis=AX.X),
               reads=[B["gx"], B["kstat"]], writes=[B["kstat"]], tag=f"{tag}.kmx")
        self.s(lambda e: e.activation(out=self.kstat[0:1, 1:2], in_=self.kstat[0:1, 0:1], func=AF.Ln),
               reads=[B["kstat"]], writes=[B["kstat"]], tag=f"{tag}.kln")
        self.s(lambda e: e.activation(out=self.kstat[0:1, 2:3], in_=self.kstat[0:1, 1:2],
                                      func=AF.Exp, scale=0.5),
               reads=[B["kstat"]], writes=[B["kstat"]], tag=f"{tag}.kex")
        for hh in range(16):
            r0h = (hh % 2) * 64
            ch = (hh // 2) * 2048
            self.v(lambda e, ch=ch, r0h=r0h: e.tensor_copy(out=qTaug.ap(0, 2048, r0=0, nrows=64),
                                                           in_=qTf.ap(ch, 2048, r0=r0h, nrows=64)),
                   reads=qTf.bufs_for(ch, 2048) + qTaug.bufs, writes=qTaug.bufs,
                   tag=f"{tag}.qc{hh}")
            self.s(lambda e: e.activation(out=sqs.ap(0, 2048, r0=0, nrows=64),
                                          in_=qTaug.ap(0, 2048, r0=0, nrows=64), func=AF.Square),
                   reads=qTaug.bufs + sqs.bufs, writes=sqs.bufs, tag=f"{tag}.qsq{hh}")
            for tb in range(4):
                self.t(lambda e, tb=tb: e.matmul(self.pb(tb, nrows=1), self.onescol_b[0:64, 0:1],
                                                 sqs.ap(tb * 512, 512, r0=0, nrows=64), start=True, stop=True),
                       reads=sqs.bufs + [B["onescol_b"]], writes=[self.PBb[tb]], tag=f"{tag}.qnm{hh}.{tb}")
            self.s(lambda e: e.activation(out=self.gx[0:1, 0:2048], in_=self.PB[0:1, 0:2048], func=AF.Ln),
                   reads=self.PBb + [B["gx"]], writes=[B["gx"]], tag=f"{tag}.qln{hh}")
            self.s(lambda e: e.activation(out=self.gx[0:1, 0:2048], in_=self.gx[0:1, 0:2048],
                                          func=AF.Exp, scale=0.5),
                   reads=[B["gx"]], writes=[B["gx"]], tag=f"{tag}.qex{hh}")
            self.v(lambda e: e.tensor_scalar(out=qTaug.ap(0, 2048, r0=64, nrows=1),
                                             in0=self.gx[0:1, 0:2048], scalar1=self.kstat[0:1, 2:3],
                                             scalar2=None, op0=ALU.mult),
                   reads=[B["gx"], B["kstat"]] + qTaug.bufs, writes=qTaug.bufs, tag=f"{tag}.qbr{hh}")
            for si in range(16):
                par = si % 2
                tb0 = si // 4
                for tb in range(tb0, 4):
                    ps, pbuf = self.pa(tb), self.PAb[tb]
                    lh = kTaug.ap(si * 128, 128, r0=0, nrows=65)
                    rh = qTaug.ap(tb * 512, 512, r0=0, nrows=65)
                    self.t(lambda e, ps=ps, lh=lh, rh=rh: e.matmul(ps, lh, rh, start=True, stop=True),
                           reads=kTaug.bufs + qTaug.bufs, writes=[pbuf], tag=f"{tag}.p2m{hh}.{si}.{tb}")
                dps = self.pa(tb0)
                self.v(lambda e, dps=dps, o=(si % 4): e.tensor_add(
                        out=dps, in0=dps, in1=self.masks2(o)),
                       reads=[self.PAb[tb0], B["masks2"]], writes=[self.PAb[tb0]],
                       tag=f"{tag}.p2d{hh}.{si}")
                w = 2048 - tb0 * 512
                dst = Ap[par].ap(tb0 * 512, w)
                self.s(lambda e, tb0=tb0, dst=dst: e.activation(
                        out=dst, in_=self.PA[0:128, tb0 * 512:2048], func=AF.Exp),
                       reads=[self.PAb[b] for b in range(tb0, 4)] + Ap[par].bufs,
                       writes=Ap[par].bufs, tag=f"{tag}.p2e{hh}.{si}")
                for tb in range(tb0, 4):
                    ps, pbuf = self.pb(tb, nrows=65), self.PBb[tb]
                    lh = vaug.ap(si * 65, 65)
                    rh = Ap[par].ap(tb * 512, 512)
                    self.t(lambda e, ps=ps, lh=lh, rh=rh, si=si, tb=tb:
                           e.matmul(ps, lh, rh, start=(si == 0), stop=(si == 4 * tb + 3)),
                           reads=vaug.bufs + Ap[par].bufs, writes=[pbuf],
                           tag=f"{tag}.ym{hh}.{si}.{tb}")
                if (getattr(self, "dbg_l0a", None) is not None and hh == self.dbg_l0a[0]
                        and si == 0):
                    dA = self.dbg_l0a[1]
                    mk = self.dma("gpsimd", lambda e, dA=dA: e.dma_start(out=dA, in_=Ap[0].ap(0, 2048)),
                                  reads=Ap[0].bufs, tag=f"{tag}.dbgA{hh}")
                    self.g(None, deps=[mk], tag=f"{tag}.dbgwa{hh}")
            if getattr(self, "dbg_l0a", None) is not None and hh == self.dbg_l0a[0]:
                dPB = self.dbg_l0a[2]
                dbgR = self.AF.alloc(2048, "dbgpb", grain=2048)
                self.v(lambda e: e.tensor_copy(out=dbgR.ap(0, 2048, r0=0, nrows=65),
                                               in_=self.PB[0:65, 0:2048]),
                       reads=self.PBb + dbgR.bufs, writes=dbgR.bufs, tag=f"{tag}.dbgc{hh}")
                mk2 = self.dma("gpsimd", lambda e, dPB=dPB, dbgR=dbgR: e.dma_start(
                        out=dPB, in_=dbgR.ap(0, 2048, r0=0, nrows=65)),
                               reads=dbgR.bufs, tag=f"{tag}.dbgP{hh}")
                self.g(None, deps=[mk2], tag=f"{tag}.dbgw{hh}")
                dbgR.free()
            self.v(lambda e: e.tensor_copy(out=self.gx[64:65, :], in_=self.PB[64:65, 0:2048]),
                   reads=self.PBb, writes=[B["gx"]], tag=f"{tag}.lv{hh}")
            self.v(lambda e: e.reciprocal(out=self.gx[64:65, :], in_=self.gx[64:65, :]),
                   reads=[B["gx"]], writes=[B["gx"]], tag=f"{tag}.lr{hh}")
            for tb in range(4):
                self.t(lambda e, tb=tb: e.matmul(self.pa(tb, nrows=64), self.onesf[64:65, 0:64],
                                                 self.gx[64:65, tb * 512:(tb + 1) * 512],
                                                 start=True, stop=True),
                       reads=[B["onesf"], B["gx"]], writes=[self.PAb[tb]], tag=f"{tag}.rb{hh}.{tb}")
            self.v(lambda e: e.tensor_copy(out=yscr.ap(0, 2048, r0=0, nrows=64),
                                           in_=self.PB[0:64, 0:2048]),
                   reads=self.PBb + yscr.bufs, writes=yscr.bufs, tag=f"{tag}.ye{hh}")
            ydst = yallT.ap((hh // 2) * 2048, 2048, r0=(hh % 2) * 64, nrows=64)
            self.v(lambda e, ydst=ydst: e.tensor_mul(out=ydst, in0=yscr.ap(0, 2048, r0=0, nrows=64),
                                                     in1=self.PA[0:64, 0:2048]),
                   reads=yscr.bufs + self.PAb + yallT.bufs_for((hh // 2) * 2048, 2048),
                   writes=yallT.bufs_for((hh // 2) * 2048, 2048), tag=f"{tag}.ys{hh}")
        for r in (kTaug, qTf, qTaug, vaug, Ap[0], Ap[1], self.masksR, yscr, sqs):
            r.free()
        attn_out = self.AB.alloc(16384, f"ao_{tag}", grain=2048)
        for oc in range(8):
            for tb in range(4):
                ps, pbuf = self.pa(tb), self.PAb[tb]
                for ic in range(8):
                    lh = self.wcp.ap(ic * 1024 + oc * 128, 128)
                    rh = yallT.ap(ic * 2048 + tb * 512, 512)
                    self.t(lambda e, ps=ps, lh=lh, rh=rh, ic=ic:
                           e.matmul(ps, lh, rh, start=(ic == 0), stop=(ic == 7)),
                           reads=self.wcp.bufs + yallT.bufs_for(ic * 2048, 2048), writes=[pbuf],
                           tag=f"{tag}.pj{oc}.{tb}.{ic}")
                dst = attn_out.ap(oc * 2048 + tb * 512, 512)
                self.s(lambda e, ps=ps, d=dst: e.copy(out=d, in_=ps),
                       reads=[pbuf], writes=attn_out.bufs_for(oc * 2048, 2048), tag=f"{tag}.pe{oc}.{tb}")
        yallT.free()
        self.tail_f(attn_out, tag)
        attn_out.free()
        self.wcp.free()

    def tail_f(self, src, tag):
        B = self.B
        sq = self.AB.alloc(2 * 2048, f"sq_{tag}", grain=2048)
        t1r = self.AF.alloc(2 * 2048, f"t1_{tag}", grain=2048)
        xo = self.AB.alloc(2 * 4096, f"xo_{tag}", grain=4096)
        for dc in range(8):
            i = dc % 2
            sqa = sq.ap(i * 2048, 2048)
            self.s(lambda e, s=src.ap(dc * 2048, 2048), d=sqa: e.activation(out=d, in_=s, func=AF.Square),
                   reads=src.bufs_for(dc * 2048, 2048) + sq.bufs_for(i * 2048, 2048),
                   writes=sq.bufs_for(i * 2048, 2048), tag=f"{tag}.q{dc}")
            for b in range(4):
                self.t(lambda e, b=b, a=src.ap(dc * 2048 + b * 512, 512), dc=dc:
                       e.matmul(self.pa(b, nrows=1), self.onescol_b[:], a, start=(dc == 0), stop=(dc == 7)),
                       reads=src.bufs_for(dc * 2048, 2048) + [B["onescol_b"]], writes=[self.PAb[b]],
                       tag=f"{tag}.m{dc}.{b}")
                self.t(lambda e, b=b, a=sq.ap(i * 2048 + b * 512, 512), dc=dc:
                       e.matmul(self.pb(b, nrows=1), self.onescol_b[:], a, start=(dc == 0), stop=(dc == 7)),
                       reads=sq.bufs_for(i * 2048, 2048) + [B["onescol_b"]], writes=[self.PBb[b]],
                       tag=f"{tag}.s{dc}.{b}")
        self._ln_rows_and_bcast(2048, f"{tag}.t")
        for dc in range(8):
            i = dc % 2
            t1 = t1r.ap(i * 2048, 2048)
            xoa = xo.ap(i * 4096, 4096).bitcast(F32)
            self.dma("scalar", lambda e, d=xoa, dc=dc: e.dma_start(out=d, in_=self.XT[dc * 128:(dc + 1) * 128, :]),
                     reads=[self.XT_b[dc]] + xo.bufs_for(i * 4096, 4096),
                     writes=xo.bufs_for(i * 4096, 4096), tag=f"{tag}.xl{dc}")
            self.v(lambda e, s=src.ap(dc * 2048, 2048), d=t1: e.tensor_sub(out=d, in0=s, in1=self.mbc[:]),
                   reads=src.bufs_for(dc * 2048, 2048) + [B["mbc"]] + t1r.bufs_for(i * 2048, 2048),
                   writes=t1r.bufs_for(i * 2048, 2048), tag=f"{tag}.a{dc}")
            self.v(lambda e, d=t1: e.tensor_mul(out=d, in0=d, in1=self.rbc[:]),
                   reads=t1r.bufs_for(i * 2048, 2048) + [B["rbc"]],
                   writes=t1r.bufs_for(i * 2048, 2048), tag=f"{tag}.b{dc}")
            self.v(lambda e, d=t1, xoa=xoa: e.tensor_add(out=d, in0=d, in1=xoa),
                   reads=t1r.bufs_for(i * 2048, 2048) + xo.bufs_for(i * 4096, 4096),
                   writes=t1r.bufs_for(i * 2048, 2048), tag=f"{tag}.c{dc}")
            dst = self.XT[dc * 128:(dc + 1) * 128, :]
            self.dma("scalar", lambda e, d=dst, s=t1: e.dma_start(out=d, in_=s),
                     reads=t1r.bufs_for(i * 2048, 2048) + [self.XT_b[dc]], writes=[self.XT_b[dc]],
                     tag=f"{tag}.st{dc}")
        sq.free(); t1r.free(); xo.free()

    def l1_attn(self, tag="l1a"):
        B = self.B
        xnT = self.xnT
        wbuf = self.AB.alloc(4096, f"w_{tag}", grain=4096)
        v_all = self.AB.alloc(16384, f"vl_{tag}", grain=1024)
        vbbc = self.AB.alloc(1024, f"vbbc_{tag}", grain=1024)

        stg = self.AB.alloc(2048, f"stg_{tag}", grain=2048)
        wnames = ["L1_attn_q_w", "L1_attn_k_w", "L1_attn_v_w", "L1_attn_l_w"]

        def load_w(i, reg, half):
            wsrc = self.wap[wnames[i]]
            if len(wsrc.tensor.shape) == 3:
                wsrc = wsrc.rearrange("h p d -> (h p) d")
            for rc in range(half * 4, half * 4 + 4):
                j = 0
                sa = stg.ap(j * 2048, 2048).bitcast(F32)
                self.dma("sync", lambda e, d=sa, s=wsrc[rc * 128:(rc + 1) * 128, :]:
                         e.dma_start(out=d, in_=s),
                         reads=stg.bufs_for(j * 2048, 2048), writes=stg.bufs_for(j * 2048, 2048),
                         tag=f"{tag}.wl{i}.{rc}")
                for cc in range(8):
                    ps = self.pb(3, c0=(cc % 4) * 128, ncols=128) if cc % 2 else                          self.pa(3, c0=(cc % 4) * 128, ncols=128)
                    pbuf = self.PBb[3] if cc % 2 else self.PAb[3]
                    tsrc = stg.ap(j * 2048 + cc * 256, 256).bitcast(F32)
                    self.t(lambda e, ps=ps, tsrc=tsrc: e.transpose(ps, tsrc, self.identf[:]),
                           reads=stg.bufs_for(j * 2048, 2048) + [B["identf"]], writes=[pbuf],
                           tag=f"{tag}.wt{i}.{rc}.{cc}")
                    dst = reg.ap(cc * 512 + (rc % 4) * 128, 128)
                    self.v(lambda e, ps=ps, d=dst: e.tensor_copy(out=d, in_=ps),
                           reads=[pbuf] + reg.bufs, writes=reg.bufs, tag=f"{tag}.we{i}.{rc}.{cc}")
        load_w(2, wbuf, 0)
        vbs = self.wap["L1_attn_v_b"].rearrange("h o p -> (h o p)").rearrange("(o n) -> o n", o=1)
        self.dma("scalar", lambda e: e.dma_start(out=self.rows[64:65, 0:1024], in_=vbs),
                 reads=[B["rows"]], writes=[B["rows"]], tag=f"{tag}.vb")
        for hb in range(2):
            self.t(lambda e, hb=hb: e.matmul(self.pb(2 + hb), self.onesf[64:65, 0:128],
                                             self.rows[64:65, hb * 512:(hb + 1) * 512], start=True, stop=True),
                   reads=[B["onesf"], B["rows"]], writes=[self.PBb[2 + hb]], tag=f"{tag}.vc{hb}")
        self.v(lambda e: e.tensor_copy(out=vbbc.ap(0, 1024), in_=self.PB[0:128, 1024:2048]),
               reads=[self.PBb[2], self.PBb[3]], writes=vbbc.bufs, tag=f"{tag}.vbe")
        for hb in range(2):
            if hb == 1:
                load_w(2, wbuf, 1)
            for si in range(16):
                bk = 2 * (si % 2) + hb
                ps, pbuf = self.pa(bk), self.PAb[bk]
                for dc in range(8):
                    lh = xnT.ap(dc * 2048 + si * 128, 128)
                    rh = wbuf.ap(dc * 512, 512)
                    self.t(lambda e, ps=ps, lh=lh, rh=rh, dc=dc:
                           e.matmul(ps, lh, rh, start=(dc == 0), stop=(dc == 7)),
                           reads=xnT.bufs_for(dc * 2048, 2048) + wbuf.bufs, writes=[pbuf],
                           tag=f"{tag}.v{si}.{hb}.{dc}")
                dst = v_all.ap(si * 1024 + hb * 512, 512)
                self.v(lambda e, ps=ps, d=dst, hb=hb:
                       e.tensor_add(out=d, in0=ps, in1=vbbc.ap(hb * 512, 512)),
                       reads=[pbuf] + vbbc.bufs, writes=v_all.bufs_for(si * 1024 + hb * 512, 512),
                       tag=f"{tag}.ve{si}.{hb}")
        vbbc.free()
        qkregs = []
        for (wi, bcol0, scale) in ((0, 0, None), (1, 8, 0.125)):
            reg = self.AB.alloc(16384, f"{'q' if wi == 0 else 'k'}T_{tag}", grain=2048)
            qkregs.append(reg)
            for hp in range(8):
                if hp % 4 == 0:
                    load_w(wi, wbuf, hp // 4)
                for tb in range(4):
                    k8 = (hp * 4 + tb) % 8
                    ps = self.pa(k8) if k8 < 4 else self.pb(k8 - 4)
                    pbuf = self.PAb[k8] if k8 < 4 else self.PBb[k8 - 4]
                    for dc in range(8):
                        lh = wbuf.ap(dc * 512 + (hp % 4) * 128, 128)
                        rh = xnT.ap(dc * 2048 + tb * 512, 512)
                        self.t(lambda e, ps=ps, lh=lh, rh=rh, dc=dc:
                               e.matmul(ps, lh, rh, start=(dc == 0), stop=(dc == 7)),
                               reads=wbuf.bufs + xnT.bufs_for(dc * 2048, 2048), writes=[pbuf],
                               tag=f"{tag}.qk{wi}.{hp}.{tb}")
                    dst = reg.ap(hp * 2048 + tb * 512, 512)
                    bias = self.qkvb[:, bcol0 + hp:bcol0 + hp + 1]
                    if scale is None:
                        self.v(lambda e, ps=ps, d=dst, bias=bias:
                               e.tensor_scalar(out=d, in0=ps, scalar1=bias, scalar2=None, op0=ALU.add),
                               reads=[pbuf, B["qkvb"]] + reg.bufs_for(hp * 2048, 2048),
                               writes=reg.bufs_for(hp * 2048, 2048), tag=f"{tag}.qe{wi}.{hp}.{tb}")
                    else:
                        self.v(lambda e, ps=ps, d=dst, bias=bias, sc_=scale:
                               e.tensor_scalar(out=d, in0=ps, scalar1=bias, scalar2=sc_,
                                               op0=ALU.add, op1=ALU.mult),
                               reads=[pbuf, B["qkvb"]] + reg.bufs_for(hp * 2048, 2048),
                               writes=reg.bufs_for(hp * 2048, 2048), tag=f"{tag}.qe{wi}.{hp}.{tb}")
        qT, kT = qkregs
        xnT.free()
        self.xnT = None
        wbuf.free()
        stg.free()
        concatT = self.AB.alloc(16384, f"cc_{tag}", grain=2048)
        Ap_ = [self.AB.alloc(2048, f"A{i}_{tag}", grain=512) for i in range(2)]
        vsr = self.AB.alloc(2 * 64, f"vs_{tag}", grain=64)
        sqs1 = self.AB.alloc(2048, f"sq1_{tag}", grain=2048)
        kstatB = [Buf("kstat0"), Buf("kstat1")]
        gst = self.gst
        for hh in range(16):
            r0h = (hh % 2) * 64
            ch = (hh // 2) * 2048
            self.s(lambda e, ch=ch, r0h=r0h: e.activation(out=sqs1.ap(0, 2048, r0=64, nrows=64),
                    in_=qT.ap(ch, 2048, r0=r0h, nrows=64), func=AF.Square),
                   reads=qT.bufs_for(ch, 2048) + sqs1.bufs, writes=sqs1.bufs, tag=f"{tag}.qsq{hh}")
            for tb in range(4):
                self.t(lambda e, tb=tb: e.matmul(self.pa(tb, nrows=1), self.onescol_b[64:128, 0:1],
                        sqs1.ap(tb * 512, 512, r0=64, nrows=64), start=True, stop=True),
                       reads=sqs1.bufs + [B["onescol_b"]], writes=[self.PAb[tb]], tag=f"{tag}.qnm{hh}.{tb}")
            self.v(lambda e: e.reduce_max(out=self.kstat[0:1, 4:5], in_=self.PA[0:1, 0:2048], axis=AX.X),
                   reads=self.PAb + [B["kstat"]], writes=[B["kstat"]], tag=f"{tag}.qmx{hh}")
            self.s(lambda e: e.activation(out=self.kstat[0:1, 5:6], in_=self.kstat[0:1, 4:5], func=AF.Sqrt),
                   reads=[B["kstat"]], writes=[B["kstat"]], tag=f"{tag}.qsr{hh}")
            self.t(lambda e: e.matmul(self.pa(1, c0=16, ncols=1), self.onesf[0:1, 0:128],
                    self.kstat[0:1, 5:6], start=True, stop=True),
                   reads=[B["onesf"], B["kstat"]], writes=[self.PAb[1]], tag=f"{tag}.mbc{hh}")
            self.v(lambda e: e.tensor_copy(out=gst[:, 16:17], in_=self.pa(1, c0=16, ncols=1)),
                   reads=[self.PAb[1], B["gst"]], writes=[B["gst"]], tag=f"{tag}.mev{hh}")
            self.s(lambda e, ch=ch, r0h=r0h: e.activation(out=sqs1.ap(0, 2048, r0=0, nrows=64),
                    in_=kT.ap(ch, 2048, r0=r0h, nrows=64), func=AF.Square),
                   reads=kT.bufs_for(ch, 2048) + sqs1.bufs, writes=sqs1.bufs, tag=f"{tag}.ksq{hh}")
            for ki in range(16):
                self.t(lambda e, ki=ki: e.matmul(self.pa(0, c0=ki, ncols=1),
                        sqs1.ap(ki * 128, 128, r0=0, nrows=64), self.onescol_b[0:64, 0:1],
                        start=True, stop=True),
                       reads=sqs1.bufs + [B["onescol_b"]], writes=[self.PAb[0]], tag=f"{tag}.knc{hh}.{ki}")
            self.s(lambda e: e.activation(out=gst[:, 0:16], in_=self.pa(0, c0=0, ncols=16), func=AF.Sqrt),
                   reads=[self.PAb[0], B["gst"]], writes=[B["gst"]], tag=f"{tag}.ksr{hh}")
            self.v(lambda e: e.tensor_scalar(out=gst[:, 0:16], in0=gst[:, 0:16],
                    scalar1=gst[:, 16:17], scalar2=-1.0, op0=ALU.mult, op1=ALU.mult),
                   reads=[B["gst"]], writes=[B["gst"]], tag=f"{tag}.btl{hh}")
            if getattr(self, "dbg_l1a", None) is not None and hh == self.dbg_l1a[0]:
                dG = self.dbg_l1a[1]
                mg = self.dma("gpsimd", lambda e, dG=dG: e.dma_start(out=dG, in_=gst[:, 0:32]),
                              reads=[B["gst"]], tag=f"{tag}.dbgG{hh}")
                self.g(None, deps=[mg], tag=f"{tag}.dbgwg{hh}")
            for ki in range(16):
                par = ki % 2
                kb = kstatB[par]
                Ap = Ap_[par]
                c2 = par * 2
                lh = kT.ap(ch + ki * 128, 128, r0=r0h, nrows=64)
                for qh in range(2):
                    for qb_ in range(2):
                        bank = 2 * qh + qb_
                        ps, pbuf = self.pa(bank), self.PAb[bank]
                        rh = qT.ap(ch + qh * 1024 + qb_ * 512, 512, r0=r0h, nrows=64)
                        self.t(lambda e, ps=ps, lh=lh, rh=rh: e.matmul(ps, lh, rh, start=True, stop=True),
                               reads=kT.bufs_for(ch, 2048) + qT.bufs_for(ch, 2048), writes=[pbuf],
                               tag=f"{tag}.s{hh}.{ki}.{bank}")
                self.s(lambda e, Ap=Ap, ki=ki, c2=c2: e.activation(
                        out=Ap.ap(0, 2048), in_=self.PA[0:128, 0:2048],
                        func=AF.Exp, bias=gst[:, ki:ki + 1],
                        accum_out=self.kstat[:, c2:c2 + 1]),
                       reads=self.PAb + [B["gst"], kb] + Ap.bufs,
                       writes=Ap.bufs + [kb], tag=f"{tag}.ex{hh}.{ki}")
                if (getattr(self, "dbg_l1a", None) is not None and hh == self.dbg_l1a[0]
                        and ki == 0):
                    dA = self.dbg_l1a[2]
                    ma = self.dma("gpsimd", lambda e, dA=dA, Ap=Ap: e.dma_start(out=dA, in_=Ap.ap(0, 2048)),
                                  reads=Ap.bufs, tag=f"{tag}.dbgA{hh}")
                    self.g(None, deps=[ma], tag=f"{tag}.dbgwa{hh}")
                self.v(lambda e, c2=c2: e.reciprocal(out=self.kstat[:, c2 + 1:c2 + 2],
                                                     in_=self.kstat[:, c2:c2 + 1]),
                       reads=[kb], writes=[kb], tag=f"{tag}.rc{hh}.{ki}")
                vs = vsr.ap(par * 64, 64)
                self.v(lambda e, d=vs, s=v_all.ap(ki * 1024 + hh * 64, 64), c2=c2:
                       e.tensor_scalar(out=d, in0=s, scalar1=self.kstat[:, c2 + 1:c2 + 2],
                                       scalar2=None, op0=ALU.mult),
                       reads=v_all.bufs_for(ki * 1024 + hh * 64, 64) + [kb] + vsr.bufs_for(par * 64, 64),
                       writes=vsr.bufs_for(par * 64, 64), tag=f"{tag}.vs{hh}.{ki}")
                for qb_ in range(4):
                    ps, pbuf = self.pb(qb_, nrows=64), self.PBb[qb_]
                    self.t(lambda e, ps=ps, vs=vs, rh=Ap.ap(qb_ * 512, 512), ki=ki:
                           e.matmul(ps, vs, rh, start=(ki == 0), stop=(ki == 15)),
                           reads=vsr.bufs_for(par * 64, 64) + Ap.bufs_for(qb_ * 512, 512), writes=[pbuf],
                           tag=f"{tag}.c{hh}.{ki}.{qb_}")
            cdst = concatT.ap(ch, 2048, r0=r0h, nrows=64)
            self.v(lambda e, d=cdst: e.tensor_copy(out=d, in_=self.PB[0:64, 0:2048]),
                   reads=self.PBb + concatT.bufs_for(ch, 2048),
                   writes=concatT.bufs_for(ch, 2048), tag=f"{tag}.cv{hh}")
        for r in (qT, kT, Ap_[0], Ap_[1], vsr, v_all, sqs1):
            r.free()
        attn_out = self.AB.alloc(16384, f"ao_{tag}", grain=2048)
        wbuf2 = self.AB.alloc(4096, f"w2_{tag}", grain=4096)
        stg = self.AB.alloc(2048, f"stg2_{tag}", grain=2048)
        for oc in range(8):
            if oc % 4 == 0:
                load_w(3, wbuf2, oc // 4)
            for tb in range(4):
                ps, pbuf = self.pa(tb), self.PAb[tb]
                for ic in range(8):
                    lh = wbuf2.ap(ic * 512 + (oc % 4) * 128, 128)
                    rh = concatT.ap(ic * 2048 + tb * 512, 512)
                    self.t(lambda e, ps=ps, lh=lh, rh=rh, ic=ic:
                           e.matmul(ps, lh, rh, start=(ic == 0), stop=(ic == 7)),
                           reads=wbuf2.bufs + concatT.bufs_for(ic * 2048, 2048), writes=[pbuf],
                           tag=f"{tag}.pj{oc}.{tb}.{ic}")
                dst = attn_out.ap(oc * 2048 + tb * 512, 512)
                self.v(lambda e, ps=ps, d=dst, oc=oc:
                       e.tensor_scalar(out=d, in0=ps, scalar1=self.lbcols[:, oc:oc + 1],
                                       scalar2=None, op0=ALU.add),
                       reads=[pbuf, B["lbcols"]], writes=attn_out.bufs_for(oc * 2048, 2048),
                       tag=f"{tag}.pe{oc}.{tb}")
        self.tail_f(attn_out, tag)
        for r in (wbuf2, concatT, attn_out, stg):
            r.free()

    def epilog(self):
        B = self.B
        xtt = self.AF.alloc(2 * 1024, "ep_in", grain=1024)
        ott = self.AF.alloc(2 * 1024, "ep_out", grain=1024)
        stores = []
        for ti in range(16):
            i = ti % 2
            dst = xtt.ap(i * 1024, 1024).rearrange("p (c f) -> p c f", c=8)
            src = self.XT[:, ti * 128:(ti + 1) * 128].rearrange("(c p) f -> p c f", p=128)
            self.dma("scalar", lambda e, d=dst, s=src: e.dma_start(out=d, in_=s),
                     reads=[self.XT_b[c] for c in range(8)] + xtt.bufs_for(i * 1024, 1024),
                     writes=xtt.bufs_for(i * 1024, 1024), tag=f"ep.ld{ti}")
            for dc in range(8):
                ps = self.pb(dc % 2, c0=(dc // 2 % 2) * 256, ncols=128)
                pbuf = self.PBb[dc % 2]
                self.t(lambda e, ps=ps, s=xtt.ap(i * 1024 + dc * 128, 128): e.transpose(ps, s, self.identf[:]),
                       reads=xtt.bufs_for(i * 1024, 1024) + [B["identf"]], writes=[pbuf],
                       tag=f"ep.tr{ti}.{dc}")
                self.v(lambda e, ps=ps, d=ott.ap(i * 1024 + dc * 128, 128): e.tensor_copy(out=d, in_=ps),
                       reads=[pbuf] + ott.bufs_for(i * 1024, 1024), writes=ott.bufs_for(i * 1024, 1024),
                       tag=f"ep.ev{ti}.{dc}")
            st = self.dma("scalar", lambda e, s=ott.ap(i * 1024, 1024), ti=ti:
                          e.dma_start(out=self.out[ti * 128:(ti + 1) * 128, :], in_=s),
                          reads=ott.bufs_for(i * 1024, 1024), tag=f"ep.st{ti}")
            stores.append(st)
        fin = self.p.op("scalar", None, deps=stores, tag="ep.fin")
        xtt.free(); ott.free()

    def dump_xt(self, name):
        d = self.nc.dram_tensor(f"dump_{name}", [D, T], F32, kind="ExternalOutput").ap()
        o = self.dma("gpsimd", lambda e, d=d: e.dma_start(out=d, in_=self.XT[:, :]),
                     reads=[self.XT_b[c] for c in range(8)], tag=f"dump.{name}")
        self.p.op("gpsimd", None, deps=[o], tag=f"dumpfin.{name}")

    def dump_region(self, name, region, ncols, rows=128):
        dt = region.arena.h.dtype if hasattr(region.arena.h, "dtype") else BF16
        d = self.nc.dram_tensor(f"dump_{name}", [rows, ncols], dt, kind="ExternalOutput").ap()
        self.dma("gpsimd", lambda e, d=d: e.dma_start(out=d, in_=region.ap(0, ncols, r0=0, nrows=rows)),
                 reads=region.bufs, tag=f"dump.{name}")


def build_nc(blocks=("l0a", "l0f", "l1a", "l1f"), dumps=()):
    nc = bass.Bass("TRN2", target_bir_lowering=False)
    with ExitStack() as ctx:
        b = MKB(nc, ctx, blocks, dumps)
        for nm, shape in WEIGHT_SPECS.items():
            b.wap[nm] = nc.dram_tensor(nm, list(shape), F32, kind="ExternalInput").ap()
        b.prolog_consts()
        if "l0a" in blocks:
            b.load_l0_attn_weights()
        b.prolog_input_transpose()
        b.prolog_biases()
        if "l0f" in blocks:
            b.prolog_casts_ffn(0)
        if "l0a" in blocks:
            b.pre_ln(0, False, "p0")
            b.l0_attn()
            if "xt_l0a" in dumps:
                b.dump_xt("xt_l0a")
        if "l0f" in blocks:
            b.pre_ln(0, True, "g0")
            b.ffn(0, "f0")
            if "xt_l0f" in dumps:
                b.dump_xt("xt_l0f")
        if "l1f" in blocks:
            b.prolog_casts_ffn(1)
        if "l1a" in blocks:
            b.pre_ln(1, False, "p2")
            b.l1_attn()
            if "xt_l1a" in dumps:
                b.dump_xt("xt_l1a")
        if "l1f" in blocks:
            b.pre_ln(1, True, "g1")
            b.ffn(1, "f1")
        b.epilog()
        b.p.finalize(nc, ctx)
        block = ctx.enter_context(nc.Block())
        block.tensor(lambda e: b.p.run_engine(e, "tensor"))
        block.vector(lambda e: b.p.run_engine(e, "vector"))
        block.scalar(lambda e: b.p.run_engine(e, "scalar"))
        block.gpsimd(lambda e: b.p.run_engine(e, "gpsimd"))
        block.sync(lambda e: b.p.run_engine(e, "sync"))
    return nc, b


def kernel(**inputs):
    x = np.ascontiguousarray(inputs["L0_attn_x"], dtype=np.float32)
    nc, _ = build_nc()
    in_maps = []
    for c in range(8):
        m = {"x_in": x[c]}
        for nm in WEIGHT_SPECS:
            m[nm] = np.ascontiguousarray(inputs[nm], np.float32)
        in_maps.append(m)
    res = run_bass_kernel_spmd(nc, in_maps, core_ids=list(range(8)))
    return np.stack([r["x_out"] for r in res.results]).astype(np.float32)


CAP = 768
NTS = CAP // 128


def _sparse_methods():
    def sparse_dram(self):
        nc = self.nc
        self.xnD = nc.dram_tensor("xnD", [T, D], F32, kind="Internal").ap()
        self.xnD_b = Buf("xnD")
        self.accD = nc.dram_tensor("accD", [T, D], F32, kind="Internal").ap()
        self.accD_b = Buf("accD")
        self.yD = nc.dram_tensor("yD", [16384, D], BF16, kind="Internal").ap()
        self.yD_b = [Buf(f"yD{e}") for e in range(8)]
        self.idxD = nc.dram_tensor("idxD", [16384, 1], I32, kind="Internal").ap()
        self.idxD_b = Buf("idxD")

    def prolog_sparse_consts(self):
        B = self.B
        self.trir = self.AB.alloc(256, "trir", grain=256)
        trif = self.trir.ap(0, 256).bitcast(F32)
        self.v(lambda e, trif=trif: e.tensor_single_scalar(out=trif, in_=self.fidxf[:, 0:128],
                                                           scalar=self.pidxf[:, 0:1], op=ALU.is_gt),
               reads=[B["fidxf"], B["pidxf"]] + self.trir.bufs, writes=self.trir.bufs, tag="tri")
        self.idbr = self.AB.alloc(128, "idbr", grain=128)
        self.v(lambda e: e.tensor_copy(out=self.idbr.ap(0, 128), in_=self.identf[:]),
               reads=[B["identf"]] + self.idbr.bufs, writes=self.idbr.bufs, tag="idb")
        self.tokr = self.AB.alloc(32, "tokr", grain=32)
        toki = self.tokr.ap(0, 32).bitcast(I32)
        self.g(lambda e, toki=toki: e.iota(toki, pattern=[[128, 16]], base=0, channel_multiplier=1),
               reads=self.tokr.bufs, writes=self.tokr.bufs, tag="toki")
        self.gm1r = self.AB.alloc(256, "gm1r", grain=256)
        self.gm2r = self.AB.alloc(256, "gm2r", grain=256)

    def prep_sparse(self, L, tag):
        B = self.B
        xnT = self.xnT
        rt = self.AB.alloc(1700, f"rt_{tag}", grain=1700)
        self.rt = rt
        gmB = rt.ap(0, 256).bitcast(F32)
        slt = rt.ap(256, 256).bitcast(F32)
        smf = rt.ap(512, 256).bitcast(F32)
        g1f = rt.ap(768, 32).bitcast(F32)
        g2f = rt.ap(800, 32).bitcast(F32)
        g1t = rt.ap(832, 32).bitcast(F32)
        g2t = rt.ap(864, 32).bitcast(F32)
        cnt = rt.ap(896, 16).bitcast(F32)
        basec = rt.ap(912, 16).bitcast(F32)
        racc = rt.ap(928, 2).bitcast(F32)
        brow = rt.ap(930, 256).bitcast(F32)
        smI = rt.ap(1000, 256).bitcast(I32)
        gxI = rt.ap(1256, 64).bitcast(I32)
        idxS = rt.ap(1320, 96).bitcast(I32)
        zz = rt.ap(1416, 256).bitcast(I32)
        rb = rt.bufs
        m1 = self.gm1r.ap(0, 256).bitcast(F32)
        m2 = self.gm2r.ap(0, 256).bitcast(F32)
        mb1, mb2 = self.gm1r.bufs, self.gm2r.bufs

        xot = self.AB.alloc(2 * 2048, f"xot_{tag}", grain=2048)
        idb = self.idbr.ap(0, 128)
        for ti in range(16):
            i = ti % 2
            ota = xot.ap(i * 2048, 2048).bitcast(F32)
            for dc in range(8):
                slot = dc % 4
                bk = self.pa(2 + slot // 2, c0=(slot % 2) * 256, ncols=64)
                pbuf = self.PAb[2 + slot // 2]
                pbv = bk.bitcast(BF16)
                src = xnT.ap(dc * 2048 + ti * 128, 128)
                self.t(lambda e, d=pbv, s=src, idb=idb: e.transpose(d, s, idb),
                       reads=xnT.bufs_for(dc * 2048, 2048) + self.idbr.bufs, writes=[pbuf],
                       tag=f"{tag}.xt{ti}.{dc}")
                self.v(lambda e, s=pbv, d=ota[0:128, dc * 128:dc * 128 + 128]: e.tensor_copy(out=d, in_=s),
                       reads=[pbuf] + xot.bufs_for(i * 2048, 2048),
                       writes=xot.bufs_for(i * 2048, 2048), tag=f"{tag}.xe{ti}.{dc}")
            self.dma("sync", lambda e, s=ota, ti=ti: e.dma_start(
                    out=self.xnD[ti * 128:(ti + 1) * 128, :], in_=s),
                    reads=xot.bufs_for(i * 2048, 2048) + [self.xnD_b], writes=[self.xnD_b],
                    tag=f"{tag}.xs{ti}")
        xot.free()

        for ap_ in (g1f, g2f, g1t, g2t):
            self.v(lambda e, a=ap_: e.memset(a, 0.0), reads=rb, writes=rb, tag=f"{tag}.rz")
        self.v(lambda e: e.tensor_add(out=gmB, in0=m1, in1=m2),
               reads=mb1 + mb2 + rb, writes=rb, tag=f"{tag}.gmB")
        trif = self.trir.ap(0, 256).bitcast(F32)
        for e_ in range(8):
            self.v(lambda e: e.memset(racc, 0.0), reads=rb, writes=rb, tag=f"{tag}.rc0.{e_}")
            for ti in range(16):
                c = ti * 8 + e_
                self.v(lambda e, c=c: e.tensor_copy(out=slt[0:128, c:c + 1], in_=racc),
                       reads=rb, writes=rb, tag=f"{tag}.pc{e_}.{ti}")
                self.v(lambda e, c=c: e.tensor_add(out=racc, in0=racc, in1=gmB[0:128, c:c + 1]),
                       reads=rb, writes=rb, tag=f"{tag}.pa{e_}.{ti}")
            self.v(lambda e, e_=e_: e.tensor_copy(out=cnt[0:128, e_:e_ + 1], in_=racc),
                   reads=rb, writes=rb, tag=f"{tag}.cn{e_}")
            self.t(lambda e, e_=e_: e.matmul(self.PA[0:1, 1536:1664], cnt[0:128, e_:e_ + 1],
                                             trif, start=True, stop=True),
                   reads=rb + self.trir.bufs, writes=[self.PAb[3]], tag=f"{tag}.tm{e_}")
            self.v(lambda e: e.tensor_copy(out=brow[0:1, 0:128], in_=self.PA[0:1, 1536:1664]),
                   reads=[self.PAb[3]] + rb, writes=rb, tag=f"{tag}.te{e_}")
            self.t(lambda e: e.transpose(self.PA[0:128, 1536:1537], brow[0:1, 0:128],
                                         self.identf[0:1, 0:1]),
                   reads=rb + [B["identf"]], writes=[self.PAb[3]], tag=f"{tag}.tt{e_}")
            self.v(lambda e, e_=e_: e.tensor_copy(out=basec[0:128, e_:e_ + 1],
                                                  in_=self.PA[0:128, 1536:1537]),
                   reads=[self.PAb[3]] + rb, writes=rb, tag=f"{tag}.be{e_}")
            for ti in range(16):
                c = ti * 8 + e_
                self.v(lambda e, c=c, e_=e_: e.tensor_scalar(out=slt[0:128, c:c + 1],
                        in0=slt[0:128, c:c + 1], scalar1=basec[0:128, e_:e_ + 1],
                        scalar2=None, op0=ALU.add),
                       reads=rb, writes=rb, tag=f"{tag}.sb{e_}.{ti}")
                self.v(lambda e, c=c, ti=ti, e_=e_: e.scalar_tensor_tensor(
                        out=g1f[0:128, ti:ti + 1],
                        in0=slt[0:128, c:c + 1] if e_ == 0 else smf[0:128, c:c + 1],
                        scalar=m1[0:128, c:c + 1], in1=g1f[0:128, ti:ti + 1],
                        op0=ALU.mult, op1=ALU.add) if False else
                        e.scalar_tensor_tensor(
                        out=g1f[0:128, ti:ti + 1], in0=slt[0:128, c:c + 1],
                        scalar=m1[0:128, c:c + 1], in1=g1f[0:128, ti:ti + 1],
                        op0=ALU.mult, op1=ALU.add),
                       reads=rb + mb1, writes=rb, tag=f"{tag}.gi1.{e_}.{ti}")
                self.v(lambda e, c=c, ti=ti: e.scalar_tensor_tensor(
                        out=g2f[0:128, ti:ti + 1], in0=slt[0:128, c:c + 1],
                        scalar=m2[0:128, c:c + 1], in1=g2f[0:128, ti:ti + 1],
                        op0=ALU.mult, op1=ALU.add),
                       reads=rb + mb2, writes=rb, tag=f"{tag}.gi2.{e_}.{ti}")
                self.v(lambda e, c=c, ti=ti: e.scalar_tensor_tensor(
                        out=g1t[0:128, ti:ti + 1], in0=self.gcoef[:, c:c + 1],
                        scalar=m1[0:128, c:c + 1], in1=g1t[0:128, ti:ti + 1],
                        op0=ALU.mult, op1=ALU.add),
                       reads=rb + mb1 + [B["gcoef"]], writes=rb, tag=f"{tag}.gt1.{e_}.{ti}")
                self.v(lambda e, c=c, ti=ti: e.scalar_tensor_tensor(
                        out=g2t[0:128, ti:ti + 1], in0=self.gcoef[:, c:c + 1],
                        scalar=m2[0:128, c:c + 1], in1=g2t[0:128, ti:ti + 1],
                        op0=ALU.mult, op1=ALU.add),
                       reads=rb + mb2 + [B["gcoef"]], writes=rb, tag=f"{tag}.gt2.{e_}.{ti}")
                if e_ > 0:
                    self.v(lambda e, c=c, ti=ti, e_=e_: e.scalar_tensor_tensor(
                            out=g1f[0:128, ti:ti + 1], in0=m1[0:128, c:c + 1],
                            scalar=float(e_ * CAP), in1=g1f[0:128, ti:ti + 1],
                            op0=ALU.mult, op1=ALU.add),
                           reads=rb + mb1, writes=rb, tag=f"{tag}.gc1.{e_}.{ti}")
                    self.v(lambda e, c=c, ti=ti, e_=e_: e.scalar_tensor_tensor(
                            out=g2f[0:128, ti:ti + 1], in0=m2[0:128, c:c + 1],
                            scalar=float(e_ * CAP), in1=g2f[0:128, ti:ti + 1],
                            op0=ALU.mult, op1=ALU.add),
                           reads=rb + mb2, writes=rb, tag=f"{tag}.gc2.{e_}.{ti}")
        self.v(lambda e: e.tensor_copy(out=gxI[0:128, 0:16], in_=g1f), reads=rb, writes=rb, tag=f"{tag}.gI1")
        self.v(lambda e: e.tensor_copy(out=gxI[0:128, 16:32], in_=g2f), reads=rb, writes=rb, tag=f"{tag}.gI2")
        self.v(lambda e: e.memset(zz, 0), reads=rb, writes=rb, tag=f"{tag}.zz")
        self.dma("gpsimd", lambda e: e.dma_start(
                out=self.idxD.rearrange("(p c) o -> p (c o)", p=128),
                in_=zz),
                reads=rb + [self.idxD_b], writes=[self.idxD_b], tag=f"{tag}.iz")
        toki = self.tokr.ap(0, 32).bitcast(I32)
        for ti in range(16):
            for half in range(2):
                icol = half * 16 + ti
                self.dma("gpsimd", lambda e, icol=icol, ti=ti: e.indirect_dma_start(
                        out=self.idxD,
                        out_offset=bass.IndirectOffsetOnAxis(gxI[0:128, icol:icol + 1], 0),
                        in_=toki[0:128, ti:ti + 1], in_offset=None),
                        reads=rb + self.tokr.bufs + [self.idxD_b], writes=[self.idxD_b],
                        tag=f"{tag}.sc{ti}.{half}")
        for e_ in range(8):
            for j in range(NTS):
                col = e_ * NTS + j
                self.dma("sync", lambda e, col=col, e_=e_, j=j: e.dma_start(
                        out=idxS[0:128, col:col + 1],
                        in_=self.idxD[e_ * CAP + j * 128:e_ * CAP + (j + 1) * 128, :]),
                        reads=[self.idxD_b] + rb, writes=rb, tag=f"{tag}.il{e_}.{j}")
        self.rt_aps = dict(g1t=g1t, g2t=g2t, gxI=gxI, idxS=idxS)

    def ffn_sparse(self, L, tag):
        B = self.B
        xnT = self.xnT
        idxS = self.rt_aps["idxS"]
        rb = self.rt.bufs
        h = self.AB.alloc(16384, f"h_{tag}", grain=512)
        w1q = self.AB.alloc(2 * 4096, f"w1q_{tag}", grain=4096)
        w2q = self.AB.alloc(2 * 4096, f"w2q_{tag}", grain=4096)
        b2reg = self.AB.alloc(9216, f"b2_{tag}", grain=9216)
        acc = self.AF.alloc(4096, f"acc_{tag}", grain=512)
        b2e = self.wap[{0: "L0_ffn_eb2", 1: "L1_ffn_b2"}[L]].rearrange("e d -> (e d)").rearrange("(o n) -> o n", o=1)
        b2s = self.wap[{0: "L0_ffn_sh_b2", 1: "L1_ffn_bs2"}[L]].rearrange("(o n) -> o n", o=1)
        self.dma("gpsimd", lambda e: e.dma_start(out=b2reg.ap(0, 8192, nrows=1), in_=b2e),
                 reads=b2reg.bufs, writes=b2reg.bufs, tag=f"{tag}.b2e")
        self.dma("gpsimd", lambda e: e.dma_start(out=b2reg.ap(8192, 1024, nrows=1), in_=b2s),
                 reads=b2reg.bufs, writes=b2reg.bufs, tag=f"{tag}.b2s")

        loadseq = []
        for sc in range(4):
            for q8 in range(8):
                loadseq.append(("w1", 8, q8, 0))
            for q8 in range(8):
                loadseq.append(("w2f", 8, q8, 0))
        for e_ in range(8):
            for q8 in range(8):
                loadseq.append(("w1", e_, q8, 0))
            for dh in range(2):
                for q8 in range(8):
                    loadseq.append(("w2h", e_, q8, dh))
        li = [0]

        def plan_load():
            if li[0] >= len(loadseq):
                return
            kind, e_, q8, dh = loadseq[li[0]]
            li[0] += 1
            par = q8 % 2
            if kind == "w1":
                dst = w1q.ap(par * 4096, 4096).rearrange("p (c n) -> p c n", c=8)
                src = self.c1[L][e_ * 1024:(e_ + 1) * 1024, q8 * 512:(q8 + 1) * 512] \
                    .rearrange("(c p) n -> p c n", p=128)
                bufs, cb = w1q.bufs_for(par * 4096, 4096), self.c1_b[L][e_]
            elif kind == "w2f":
                dst = w2q.ap(par * 4096, 4096).rearrange("p (c n) -> p c n", c=4)
                src = self.c2[L][e_ * 4096 + q8 * 512:e_ * 4096 + (q8 + 1) * 512, :] \
                    .rearrange("(c p) n -> p c n", p=128)
                bufs, cb = w2q.bufs_for(par * 4096, 4096), self.c2_b[L][e_]
            else:
                dst = w2q.ap(par * 2048, 2048).rearrange("p (c n) -> p c n", c=4)
                src = self.c2[L][e_ * 4096 + q8 * 512:e_ * 4096 + (q8 + 1) * 512,
                                 dh * 512:(dh + 1) * 512].rearrange("(c p) n -> p c n", p=128)
                bufs, cb = w2q.bufs_for(par * 2048, 2048), self.c2_b[L][e_]
            self.dma("sync", lambda e, d=dst, s=src: e.dma_start(out=d, in_=s),
                     reads=[cb] + bufs, writes=bufs, tag=f"{tag}.{kind}ld{e_}.{q8}.{dh}")

        plan_load(); plan_load()
        e_ = 8
        for sc in range(4):
            for q8 in range(8):
                par = q8 % 2
                for j in range(4):
                    hc = q8 * 4 + j
                    ps, pbuf = self.pb(hc % 2), self.PBb[hc % 2]
                    for dc in range(8):
                        lh = w1q.ap(par * 4096 + dc * 512 + j * 128, 128)
                        rh = xnT.ap(dc * 2048 + sc * 512, 512)
                        self.t(lambda e, ps=ps, lh=lh, rh=rh, dc=dc:
                               e.matmul(ps, lh, rh, start=(dc == 0), stop=(dc == 7)),
                               reads=w1q.bufs_for(par * 4096, 4096) + xnT.bufs_for(dc * 2048, 2048),
                               writes=[pbuf], tag=f"{tag}.up{sc}.{hc}")
                    bcol = L * 288 + 256 + hc
                    self.s(lambda e, ps=ps, ha=h.ap(hc * 512, 512), bcol=bcol:
                           e.activation(out=ha, in_=ps, func=GELU_L[L],
                                        bias=self.b1cols[:, bcol:bcol + 1]),
                           reads=[pbuf, B["b1cols"]], writes=h.bufs_for(hc * 512, 512),
                           tag=f"{tag}.ge{sc}.{hc}")
                plan_load()
            banks = [(self.pa(k), self.PAb[k]) for k in range(4)] + \
                    [(self.pb(k), self.PBb[k]) for k in range(4)]
            for k in range(8):
                tsub, dh = k // 2, k % 2
                ps, pbuf = banks[k]
                c0 = 8192 + dh * 512
                self.t(lambda e, ps=ps, brow=b2reg.ap(c0, 512, nrows=1):
                       e.matmul(ps, self.onesb[0:1, 0:128], brow, start=True, stop=False),
                       reads=[B["onesb"]] + b2reg.bufs, writes=[pbuf], tag=f"{tag}.au{sc}.{k}")
            for q8 in range(8):
                par = q8 % 2
                for j in range(4):
                    hc = q8 * 4 + j
                    for k in range(8):
                        tsub, dh = k // 2, k % 2
                        ps, pbuf = banks[k]
                        lh = h.ap(hc * 512 + tsub * 128, 128)
                        rh = w2q.ap(par * 4096 + j * 1024 + dh * 512, 512)
                        self.t(lambda e, ps=ps, lh=lh, rh=rh, last=(q8 == 7 and j == 3):
                               e.matmul(ps, lh, rh, start=False, stop=last),
                               reads=h.bufs_for(hc * 512, 512) + w2q.bufs_for(par * 4096, 4096),
                               writes=[pbuf], tag=f"{tag}.dn{sc}.{hc}.{k}")
                plan_load()
            for k in range(8):
                tsub, dh = k // 2, k % 2
                ps, pbuf = banks[k]
                aap = acc.ap(tsub * 1024 + dh * 512, 512)
                self.v(lambda e, ps=ps, aap=aap: e.tensor_copy(out=aap, in_=ps),
                       reads=[pbuf] + acc.bufs_for(tsub * 1024 + dh * 512, 512),
                       writes=acc.bufs_for(tsub * 1024 + dh * 512, 512), tag=f"{tag}.shv{sc}.{k}")
            for tsub in range(4):
                self.dma("scalar", lambda e, s=acc.ap(tsub * 1024, 1024), sc=sc, tsub=tsub:
                         e.dma_start(out=self.accD[sc * 512 + tsub * 128:sc * 512 + (tsub + 1) * 128, :], in_=s),
                         reads=acc.bufs_for(tsub * 1024, 1024) + [self.accD_b], writes=[self.accD_b],
                         tag=f"{tag}.shs{sc}.{tsub}")
        xnT.free()
        self.xnT = None
        h.free()
        hS = self.AB.alloc(32 * CAP, f"hS_{tag}", grain=CAP)
        xgT = [self.AB.alloc(8 * CAP, f"xgT{i}_{tag}", grain=CAP) for i in range(2)]
        xgk = self.AB.alloc(2048, f"xgk_{tag}", grain=2048)
        ysb = self.AB.alloc(2 * 512, f"ysb_{tag}", grain=512)

        def plan_gather(e_):
            reg = xgT[e_ % 2]
            for j in range(NTS):
                i = 0
                ga = xgk.ap(0, 2048).bitcast(F32)
                icol = e_ * NTS + j
                self.dma("gpsimd", lambda e, d=ga, icol=icol: e.indirect_dma_start(
                        out=d, out_offset=None, in_=self.xnD,
                        in_offset=bass.IndirectOffsetOnAxis(idxS[0:128, icol:icol + 1], 0)),
                        reads=[self.xnD_b] + rb + xgk.bufs_for(i * 2048, 2048),
                        writes=xgk.bufs_for(i * 2048, 2048), tag=f"{tag}.g{e_}.{j}")
                for dc in range(8):
                    slot = dc % 4
                    ps = self.pa(2 + slot // 2, c0=(slot % 2) * 256, ncols=128)
                    pbuf = self.PAb[2 + slot // 2]
                    self.t(lambda e, ps=ps, s=ga[0:128, dc * 128:(dc + 1) * 128]:
                           e.transpose(ps, s, self.identf[:]),
                           reads=xgk.bufs_for(i * 2048, 2048) + [B["identf"]], writes=[pbuf],
                           tag=f"{tag}.gt{e_}.{j}.{dc}")
                    dcol = dc * CAP + j * 128
                    self.v(lambda e, ps=ps, d=reg.ap(dcol, 128): e.tensor_copy(out=d, in_=ps),
                           reads=[pbuf] + reg.bufs_for(dcol, 128), writes=reg.bufs_for(dcol, 128),
                           tag=f"{tag}.gv{e_}.{j}.{dc}")

        plan_gather(0)
        for e_ in range(8):
            reg = xgT[e_ % 2]
            for q8 in range(8):
                par = q8 % 2
                for j in range(4):
                    hc = q8 * 4 + j
                    b0 = (hc % 2) * 2
                    psA, pbA = self.pa(b0), self.PAb[b0]
                    psB, pbB = self.pa(b0 + 1, ncols=256), self.PAb[b0 + 1]
                    for dc in range(8):
                        lh = w1q.ap(par * 4096 + dc * 512 + j * 128, 128)
                        self.t(lambda e, ps=psA, lh=lh, rh=reg.ap(dc * CAP, 512), dc=dc:
                               e.matmul(ps, lh, rh, start=(dc == 0), stop=(dc == 7)),
                               reads=w1q.bufs_for(par * 4096, 4096) + reg.bufs_for(dc * CAP, CAP),
                               writes=[pbA], tag=f"{tag}.e{e_}up{hc}a")
                        self.t(lambda e, ps=psB, lh=lh, rh=reg.ap(dc * CAP + 512, 256), dc=dc:
                               e.matmul(ps, lh, rh, start=(dc == 0), stop=(dc == 7)),
                               reads=w1q.bufs_for(par * 4096, 4096) + reg.bufs_for(dc * CAP, CAP),
                               writes=[pbB], tag=f"{tag}.e{e_}up{hc}b")
                    bcol = L * 288 + e_ * 32 + hc
                    self.s(lambda e, ps=psA, d=hS.ap(hc * CAP, 512), bcol=bcol:
                           e.activation(out=d, in_=ps, func=GELU_L[L], bias=self.b1cols[:, bcol:bcol + 1]),
                           reads=[pbA, B["b1cols"]] + hS.bufs_for(hc * CAP, CAP),
                           writes=hS.bufs_for(hc * CAP, CAP), tag=f"{tag}.e{e_}ge{hc}a")
                    self.s(lambda e, ps=psB, d=hS.ap(hc * CAP + 512, 256), bcol=bcol:
                           e.activation(out=d, in_=ps, func=GELU_L[L], bias=self.b1cols[:, bcol:bcol + 1]),
                           reads=[pbB, B["b1cols"]] + hS.bufs_for(hc * CAP, CAP),
                           writes=hS.bufs_for(hc * CAP, CAP), tag=f"{tag}.e{e_}ge{hc}b")
                plan_load()
            if e_ < 7:
                plan_gather(e_ + 1)
            for dh in range(2):
                dbanks = [(self.pb(k), self.PBb[k]) for k in range(4)] + \
                         [(self.pa(k), self.PAb[k]) for k in range(2)]
                for k in range(NTS):
                    ps, pbuf = dbanks[k]
                    c0 = e_ * 1024 + dh * 512
                    self.t(lambda e, ps=ps, brow=b2reg.ap(c0, 512, nrows=1):
                           e.matmul(ps, self.onesb[0:1, 0:128], brow, start=True, stop=False),
                           reads=[B["onesb"]] + b2reg.bufs, writes=[pbuf], tag=f"{tag}.e{e_}au{dh}.{k}")
                for q8 in range(8):
                    par = q8 % 2
                    for j in range(4):
                        hc = q8 * 4 + j
                        for k in range(NTS):
                            ps, pbuf = dbanks[k]
                            lh = hS.ap(hc * CAP + k * 128, 128)
                            rh = w2q.ap(par * 2048 + j * 512, 512)
                            self.t(lambda e, ps=ps, lh=lh, rh=rh, last=(q8 == 7 and j == 3):
                                   e.matmul(ps, lh, rh, start=False, stop=last),
                                   reads=hS.bufs_for(hc * CAP, CAP) + w2q.bufs_for(par * 2048, 2048),
                                   writes=[pbuf], tag=f"{tag}.e{e_}dn{dh}.{hc}.{k}")
                    plan_load()
                for k in range(NTS):
                    ps, pbuf = dbanks[k]
                    i = k % 2
                    ya = ysb.ap(i * 512, 512)
                    self.v(lambda e, ps=ps, d=ya: e.tensor_copy(out=d, in_=ps),
                           reads=[pbuf] + ysb.bufs_for(i * 512, 512),
                           writes=ysb.bufs_for(i * 512, 512), tag=f"{tag}.e{e_}ye{dh}.{k}")
                    r0 = e_ * CAP + k * 128
                    self.dma("scalar", lambda e, s=ya, r0=r0, dh=dh: e.dma_start(
                            out=self.yD[r0:r0 + 128, dh * 512:(dh + 1) * 512], in_=s),
                            reads=ysb.bufs_for(i * 512, 512) + [self.yD_b[e_]],
                            writes=[self.yD_b[e_]], tag=f"{tag}.e{e_}ys{dh}.{k}")
        for r in (hS, xgT[0], xgT[1], xgk, ysb, w1q, w2q, b2reg, acc):
            r.free()
        self._recombine(L, tag)
        self.rt.free()

    def _recombine(self, L, tag):
        B = self.B
        g1t, g2t, gxI = self.rt_aps["g1t"], self.rt_aps["g2t"], self.rt_aps["gxI"]
        rb = self.rt.bufs
        fac = self.AF.alloc(2 * 1024, f"fac_{tag}", grain=1024)
        sqr = self.AB.alloc(2 * 1024, f"sqr_{tag}", grain=1024)
        y1r = self.AB.alloc(2 * 1024, f"y1_{tag}", grain=1024)
        y2r = self.AB.alloc(2 * 1024, f"y2_{tag}", grain=1024)
        xnr = self.AB.alloc(2 * 2048, f"xnr_{tag}", grain=2048)
        xor_ = self.AB.alloc(2 * 2048, f"xor_{tag}", grain=2048)
        stc = self.AF.alloc(16, f"stc_{tag}", grain=16)
        lnr = self.AF.alloc(2 * 1024, f"lnr_{tag}", grain=1024)
        for ti in range(16):
            i = ti % 2
            fa = fac.ap(i * 1024, 1024)
            self.dma("scalar", lambda e, d=fa, ti=ti: e.dma_start(
                    out=d, in_=self.accD[ti * 128:(ti + 1) * 128, :]),
                    reads=[self.accD_b] + fac.bufs_for(i * 1024, 1024),
                    writes=fac.bufs_for(i * 1024, 1024), tag=f"{tag}.ral{ti}")
            y1a = y1r.ap(i * 1024, 1024)
            self.dma("gpsimd", lambda e, d=y1a, ti=ti: e.indirect_dma_start(
                    out=d, out_offset=None, in_=self.yD,
                    in_offset=bass.IndirectOffsetOnAxis(gxI[0:128, ti:ti + 1], 0)),
                    reads=[self.yD_b[x] for x in range(8)] + rb + y1r.bufs_for(i * 1024, 1024),
                    writes=y1r.bufs_for(i * 1024, 1024), tag=f"{tag}.rg1{ti}")
            y2a = y2r.ap(i * 1024, 1024)
            self.dma("gpsimd", lambda e, d=y2a, ti=ti: e.indirect_dma_start(
                    out=d, out_offset=None, in_=self.yD,
                    in_offset=bass.IndirectOffsetOnAxis(gxI[0:128, 16 + ti:17 + ti], 0)),
                    reads=[self.yD_b[x] for x in range(8)] + rb + y2r.bufs_for(i * 1024, 1024),
                    writes=y2r.bufs_for(i * 1024, 1024), tag=f"{tag}.rg2{ti}")
            self.v(lambda e, fa=fa, y1a=y1a, ti=ti: e.scalar_tensor_tensor(
                    out=fa, in0=y1a, scalar=g1t[0:128, ti:ti + 1], in1=fa,
                    op0=ALU.mult, op1=ALU.add),
                   reads=y1r.bufs_for(i * 1024, 1024) + rb + fac.bufs_for(i * 1024, 1024),
                   writes=fac.bufs_for(i * 1024, 1024), tag=f"{tag}.rc1{ti}")
            self.v(lambda e, fa=fa, y2a=y2a, ti=ti: e.scalar_tensor_tensor(
                    out=fa, in0=y2a, scalar=g2t[0:128, ti:ti + 1], in1=fa,
                    op0=ALU.mult, op1=ALU.add),
                   reads=y2r.bufs_for(i * 1024, 1024) + rb + fac.bufs_for(i * 1024, 1024),
                   writes=fac.bufs_for(i * 1024, 1024), tag=f"{tag}.rc2{ti}")
            if L == 0:
                xna = xnr.ap(i * 2048, 2048).bitcast(F32)
                self.dma("scalar", lambda e, d=xna, ti=ti: e.dma_start(
                        out=d, in_=self.xnD[ti * 128:(ti + 1) * 128, :]),
                        reads=[self.xnD_b] + xnr.bufs_for(i * 2048, 2048),
                        writes=xnr.bufs_for(i * 2048, 2048), tag=f"{tag}.rxn{ti}")
                self.v(lambda e, fa=fa, xna=xna: e.tensor_add(out=fa, in0=fa, in1=xna),
                       reads=xnr.bufs_for(i * 2048, 2048) + fac.bufs_for(i * 1024, 1024),
                       writes=fac.bufs_for(i * 1024, 1024), tag=f"{tag}.rc3{ti}")
            sc0 = stc.ap(0, 16)
            self.v(lambda e, fa=fa: e.reduce_sum(out=sc0[0:128, 0:1], in_=fa, axis=AX.X),
                   reads=fac.bufs_for(i * 1024, 1024) + stc.bufs, writes=stc.bufs, tag=f"{tag}.rm{ti}")
            sq = sqr.ap(i * 1024, 1024)
            self.s(lambda e, fa=fa, d=sq: e.activation(out=d, in_=fa, func=AF.Square),
                   reads=fac.bufs_for(i * 1024, 1024) + sqr.bufs_for(i * 1024, 1024),
                   writes=sqr.bufs_for(i * 1024, 1024), tag=f"{tag}.rsq{ti}")
            self.v(lambda e, d=sq: e.reduce_sum(out=sc0[0:128, 1:2], in_=d, axis=AX.X),
                   reads=sqr.bufs_for(i * 1024, 1024) + stc.bufs, writes=stc.bufs, tag=f"{tag}.rs{ti}")
            self.v(lambda e: e.tensor_scalar(out=sc0[0:128, 0:1], in0=sc0[0:128, 0:1],
                                             scalar1=1.0 / D, scalar2=None, op0=ALU.mult),
                   reads=stc.bufs, writes=stc.bufs, tag=f"{tag}.r1{ti}")
            self.v(lambda e: e.tensor_scalar(out=sc0[0:128, 1:2], in0=sc0[0:128, 1:2],
                                             scalar1=1.0 / D, scalar2=None, op0=ALU.mult),
                   reads=stc.bufs, writes=stc.bufs, tag=f"{tag}.r2{ti}")
            self.v(lambda e: e.tensor_mul(out=sc0[0:128, 2:3], in0=sc0[0:128, 0:1], in1=sc0[0:128, 0:1]),
                   reads=stc.bufs, writes=stc.bufs, tag=f"{tag}.r3{ti}")
            self.v(lambda e: e.tensor_sub(out=sc0[0:128, 1:2], in0=sc0[0:128, 1:2], in1=sc0[0:128, 2:3]),
                   reads=stc.bufs, writes=stc.bufs, tag=f"{tag}.r4{ti}")
            self.s(lambda e: e.activation(out=sc0[0:128, 2:3], in_=sc0[0:128, 1:2],
                                          func=AF.Ln, bias=self.eps[0:128, 0:1]),
                   reads=stc.bufs + [B["eps"]], writes=stc.bufs, tag=f"{tag}.r5{ti}")
            self.s(lambda e: e.activation(out=sc0[0:128, 3:4], in_=sc0[0:128, 2:3],
                                          func=AF.Exp, scale=-0.5),
                   reads=stc.bufs, writes=stc.bufs, tag=f"{tag}.r6{ti}")
            ln = lnr.ap(i * 1024, 1024)
            self.v(lambda e, fa=fa, d=ln: e.tensor_scalar(out=d, in0=fa,
                    scalar1=sc0[0:128, 0:1], scalar2=sc0[0:128, 3:4],
                    op0=ALU.subtract, op1=ALU.mult),
                   reads=fac.bufs_for(i * 1024, 1024) + stc.bufs + lnr.bufs_for(i * 1024, 1024),
                   writes=lnr.bufs_for(i * 1024, 1024), tag=f"{tag}.rap{ti}")
            xoa = xor_.ap(i * 2048, 2048).bitcast(F32)
            src3 = self.XT[:, ti * 128:(ti + 1) * 128].rearrange("(c p) f -> p c f", p=128)
            self.dma("scalar", lambda e, d=xoa.rearrange("p (c f) -> p c f", c=8), s=src3:
                     e.dma_start(out=d, in_=s),
                     reads=[self.XT_b[c] for c in range(8)] + xor_.bufs_for(i * 2048, 2048),
                     writes=xor_.bufs_for(i * 2048, 2048), tag=f"{tag}.rxo{ti}")
            for dc in range(8):
                slot = dc % 4
                ps = self.pb(2 + slot // 2, c0=(slot % 2) * 256, ncols=128)
                pbuf = self.PBb[2 + slot // 2]
                self.t(lambda e, ps=ps, s=ln[0:128, dc * 128:(dc + 1) * 128]:
                       e.transpose(ps, s, self.identf[:]),
                       reads=lnr.bufs_for(i * 1024, 1024) + [B["identf"]], writes=[pbuf],
                       tag=f"{tag}.rtr{ti}.{dc}")
                self.v(lambda e, ps=ps, d=xoa[0:128, dc * 128:(dc + 1) * 128]:
                       e.tensor_add(out=d, in0=d, in1=ps),
                       reads=[pbuf] + xor_.bufs_for(i * 2048, 2048),
                       writes=xor_.bufs_for(i * 2048, 2048), tag=f"{tag}.rad{ti}.{dc}")
            self.dma("scalar", lambda e, s=xoa.rearrange("p (c f) -> p c f", c=8), d3=src3:
                     e.dma_start(out=d3, in_=s),
                     reads=xor_.bufs_for(i * 2048, 2048) + [self.XT_b[c] for c in range(8)],
                     writes=[self.XT_b[c] for c in range(8)], tag=f"{tag}.rst{ti}")
        for r in (fac, sqr, y1r, y2r, xnr, xor_, stc, lnr):
            r.free()

    for f in (sparse_dram, prolog_sparse_consts, prep_sparse, ffn_sparse, _recombine):
        setattr(MKB, f.__name__, f)


_sparse_methods()
